# version 10
# baseline (speedup 1.0000x reference)
import numpy as np
import ml_dtypes
import concourse.bass as bass
import concourse.mybir as mybir
from concourse.bass_utils import run_bass_kernel_spmd

F32 = mybir.dt.float32
BF16 = mybir.dt.bfloat16
FP8 = mybir.dt.float8e5
AF = mybir.ActivationFunctionType
ALU = mybir.AluOpType
AX = mybir.AxisListType.X
BF = ml_dtypes.bfloat16

T = 2048
NT = 16
NJ = 8
D = 1024
TSA = 128
NCA = T // TSA
TS1 = 128
NC1 = T // TS1
TSF = 512
NCF = T // TSF
TSD = 512
NSD = T // TSD

BUILD_UPTO = 9
DEBUG_KEYS = ()

RG = [[0, 1], [2, 3], [4, 5], [6, 7]]

CONSTS = [2.0, -1e30, 1e-6, -80.0, 1.5, -0.5, 1e-8, 1.0 / 1024.0, 1.0 / 64.0,
          1e-5, 8.0, 0.1, -1.0, 1.0, 0.0]


def _pp(vec):
    v = np.ascontiguousarray(np.asarray(vec, np.float32)).reshape(-1)
    k = v.shape[0] // 128
    return np.ascontiguousarray(v.reshape(k, 128).T)


def _wT(w):
    wt = np.ascontiguousarray(np.asarray(w, np.float32).T)
    kin, mout = wt.shape
    return np.ascontiguousarray(wt.reshape(kin // 128, 128, mout).transpose(1, 0, 2))


class Ctx:
    ENG = {"v": "vector", "c": "scalar", "p": "tensor", "g": "gpsimd", "s": "sync"}

    def __init__(self, nc, block, sems):
        self.nc, self.block, self.sem = nc, block, sems
        self.cnt = {k: 0 for k in sems}
        self.wcache = {e: {} for e in self.ENG}

    def sec(self, eng, fn):
        ctx = self

        def body(e):
            class E:
                @staticmethod
                def w(prod, val=None):
                    v = ctx.cnt[prod] if val is None else val
                    if v <= 0 or ctx.wcache[eng].get(prod, 0) >= v:
                        return
                    e.wait_ge(ctx.sem[prod], v)
                    ctx.wcache[eng][prod] = v

                @staticmethod
                def inc(instr, key, n=1):
                    instr.then_inc(ctx.sem[key], n)
                    ctx.cnt[key] += n
                    return ctx.cnt[key]

                @staticmethod
                def g(instr, key):
                    instr.then_inc(ctx.sem[key], 1)
                    ctx.cnt[key] += 1
                    e.wait_ge(ctx.sem[key], ctx.cnt[key])
                    ctx.wcache[eng][key] = ctx.cnt[key]
            fn(e, E)
        getattr(self.block, self.ENG[eng])(body)


def build_nc(build_upto=BUILD_UPTO, debug_keys=DEBUG_KEYS):
    nc = bass.Bass("TRN2", target_bir_lowering=False, debug=False)

    def din(name, shape, dtype=F32):
        return nc.dram_tensor(name, list(shape), dtype, kind="ExternalInput").ap()

    def scratch(name, shape, dtype=F32):
        return nc.dram_tensor(name, list(shape), dtype, kind="Internal").ap()

    x_own = din("x_own", [T, D])
    xprev3 = din("xprev3", [3, D])
    mask_own = din("mask_own", [T, D], BF16)
    flag_send = din("flag_send", [128, 1])
    flag_recv = din("flag_recv", [128, 1])
    consts_in = din("consts_in", [128, len(CONSTS)])
    ident_b_in = din("ident_b_i", [128, 128], BF16)
    ones_f_in = din("ones_f_i", [1, 128])
    ones_colb_in = din("ones_colb_i", [128, 1], BF16)
    iota_t_in = din("iota_t_i", [128, TS1])
    G_in = din("G_in", [128, 8 * 16])
    GT_in = din("GT_in", [16, 8 * 128])

    wrT = [din(f"wrT{l}", [128, 8, D]) for l in range(3)]
    wiT = [din(f"wiT{l}", [128, 8, D]) for l in range(3)]
    woT = [din(f"woT{l}", [128, 8, D]) for l in range(3)]
    cw_in = [din(f"cw{l}", [128, 32]) for l in range(3)]
    cb_in = [din(f"cb{l}", [128, 8]) for l in range(3)]
    wrb_in = [din(f"wrb{l}", [128, 8]) for l in range(3)]
    wib_in = [din(f"wib{l}", [128, 8]) for l in range(3)]
    wrbr_in = [din(f"wrbr{l}", [1, D]) for l in range(3)]
    wibr_in = [din(f"wibr{l}", [1, D]) for l in range(3)]
    la_in = [din(f"la{l}", [128, 8]) for l in range(3)]
    nwpp_in = [din(f"nwpp{l}", [128, 8]) for l in range(2)]
    nw2row_in = din("nw2row", [1, D])
    nowrow_in = din("nowrow", [1, D])

    w1aT_in = din("w1aT", [128, 8, 512])
    w1bT_in = din("w1bT", [128, 8, 512])
    b1r_in = din("b1r", [128, 4])
    w2T_in = din("w2T", [128, 4, 64])
    b2r_in = din("b2r", [1, 64])
    slot_init_in = din("slot_init", [64, D])
    slot_scale_in = din("slot_scale", [64, 1])
    wihT_in = din("wihT", [128, 8, 3072])
    whhT_in = din("whhT", [128, 8, 3072])
    bih_in = din("bih", [1, 3072])
    bhh_in = din("bhh", [1, 3072])
    wvTf_in = din("wvTf", [128, 8, D])
    bv_in = din("bv", [1, D])
    wfoT_in = din("wfoT", [128, 8, D])
    bo_in = din("bo", [1, D])
    tau_in = din("tau", [1, 1])

    wqT_in = din("wqT", [128, 8, D])
    wkT_in = din("wkT", [128, 8, D])
    wvT1_in = din("wvT1", [128, 8, D])
    wgT_in = din("wgT", [128, 8, D])
    bg_in = din("bg", [128, 8])
    woutT_in = din("woutT", [128, 8, D])
    dp_in = din("dp", [16, 1])

    W1fT_in = din("W1fT", [128, 8, 4096])
    b1f_in = din("b1f", [128, 32])
    W2fT_in = din("W2fT", [128, 32, D])
    b2f_in = din("b2f", [1, D])

    y_out = nc.dram_tensor("y", [T, D], F32, kind="ExternalOutput").ap()
    dbg = {}

    def dump_req(name, shape, dtype=F32):
        if name in debug_keys:
            if name not in dbg:
                dbg[name] = nc.dram_tensor("dbg_" + name, list(shape), dtype,
                                           kind="ExternalOutput").ap()
            return dbg[name]
        return None

    sp_cloc = scratch("sp_cloc", [128, NJ, T])
    sp_b = scratch("sp_b", [128, NJ, T], BF16)
    sp_P = scratch("sp_P", [128, NJ, T])
    sp_x1 = scratch("sp_x1", [128, NT, D])
    sp_at = scratch("sp_at", [128, NT, D])
    sp_v = scratch("sp_v", [128, NJ, T], BF16)
    sp_kv = scratch("sp_kv", [128, NJ, T], BF16)
    sp_ks = scratch("sp_ks", [128, NJ, T], BF16)
    cc_slot_i = scratch("cc_slot_i", [64, 1025])
    cc_slot_o = scratch("cc_slot_o", [64, 1025])
    cc1i = [scratch(f"cc1i{l}", [128, 8]) for l in range(3)]
    cc1o = [scratch(f"cc1o{l}", [128, 8]) for l in range(3)]
    cc2i = [scratch(f"cc2i{l}", [128, 8]) for l in range(3)]
    cc2o = [scratch(f"cc2o{l}", [128, 8]) for l in range(3)]
    cchi = [scratch(f"cchi{l}", [128, 24]) for l in range(2)]
    ccho = [scratch(f"ccho{l}", [128, 24]) for l in range(2)]
    ccl1i = scratch("ccl1i", [128, 16])
    ccl1o = scratch("ccl1o", [128, 16])

    def mk(name, shape, dtype=F32, side="left"):
        g = nc.sbuf_tensor(name, list(shape), dtype, side=side)
        return g.__enter__(), g

    def sbp(name, shape, dtype=F32):
        t, _ = mk(name, shape, dtype)
        return t

    consts = sbp("sb_consts", [128, len(CONSTS)])
    for i, v in enumerate(CONSTS):
        nc.const_aps.register(F32, float(v), consts[:, i:i + 1])
    ident_b = sbp("sb_ident", [128, 128], BF16)
    ones_f = sbp("sb_onesf", [1, 128])
    ones_colb = sbp("sb_onescolb", [128, 1], BF16)
    Gt = sbp("sb_G", [128, 8 * 16])
    GTt = sbp("sb_GT", [16, 8 * 128])
    flagS = sbp("sb_flagS", [128, 1])
    flagR = sbp("sb_flagR", [128, 1])
    cw = [sbp(f"sb_cw{l}", [128, 32]) for l in range(3)]
    cb = [sbp(f"sb_cb{l}", [128, 8]) for l in range(3)]
    wrb = [sbp(f"sb_wrb{l}", [128, 8]) for l in range(3)]
    wib = [sbp(f"sb_wib{l}", [128, 8]) for l in range(3)]
    la = [sbp(f"sb_la{l}", [128, 8]) for l in range(3)]
    scv = [sbp(f"sb_scv{l}", [128, 8]) for l in range(3)]
    nwpp = [sbp(f"sb_nwpp{l}", [128, 8]) for l in range(2)]
    nw2row = sbp("sb_nw2row", [1, D])
    nowrow = sbp("sb_nowrow", [1, D])
    bg_pp = sbp("sb_bg", [128, 8])
    b1f_pp = sbp("sb_b1f", [128, 32])
    b1fq_pp = sbp("sb_b1fq", [128, 32])
    b1q8 = sbp("sb_b1q8", [128, 32], FP8)
    dp_s = sbp("sb_dp", [16, 4])
    logd = sbp("sb_logd", [128, 8])

    st1 = sbp("sb_st1", [128, NT])
    st2 = sbp("sb_st2", [128, NT])
    stm = sbp("sb_stm", [128, NT])
    strv = sbp("sb_strv", [128, NT])
    stmr = sbp("sb_stmr", [128, NT])
    scr16 = sbp("sb_scr16", [128, NT])
    sq_scr = sbp("sb_sqscr", [128, D])
    sq_scr2 = sbp("sb_sqscr2", [128, D])
    p3st = sbp("sb_p3st", [3, 4])

    ccar = sbp("sb_ccar", [128, 8])
    nccst = sbp("sb_nccst", [128, 8])
    ccprev = sbp("sb_ccprev", [128, 8])
    hcar = sbp("sb_hcar", [128, 8])
    ccl = sbp("sb_ccl", [128, 8])
    hll = sbp("sb_hll", [128, 8])
    kvcar = sbp("sb_kvcar", [128, 8])
    kscar = sbp("sb_kscar", [128, 8])
    stage32 = sbp("sb_stage32", [128, 32])
    recv32 = sbp("sb_recv32", [128, 32])
    stage16 = sbp("sb_stage16", [128, 16])
    recv16 = sbp("sb_recv16", [128, 16])

    ones_ch = sbp("sb_ones_ch", [128, TSD])
    ones_r5 = sbp("sb_ones_r5", [1, TSD], BF16)

    def psum(name, shape, dtype=F32):
        g = nc.psum_tensor(name, list(shape), dtype)
        return g.__enter__()
    pA = psum("pA", [128, 512])
    pB = psum("pB", [128, 512])
    pC = psum("pC", [128, 512])
    pD = psum("pD", [128, 512])
    pE = psum("pE", [128, 1024])
    pT = psum("pT", [128, 1024], BF16)
    pT2 = psum("pT2", [128, 1024], BF16)

    sems = {}
    for k in ["v", "c", "p", "g", "dS", "dG"]:
        g = nc.semaphore("sem_" + k)
        sems[k] = g.__enter__()

    blockg = nc.Block()
    block = blockg.__enter__()
    ctx = Ctx(nc, block, sems)

    def newton_rsqrt(e, E, veps_ap, rstd_ap, scr_ap):
        E.g(e.reciprocal(out=rstd_ap, in_=scr_ap), "v")
        E.g(e.tensor_tensor(out=scr_ap, in0=rstd_ap, in1=rstd_ap, op=ALU.mult), "v")
        E.g(e.tensor_tensor(out=scr_ap, in0=scr_ap, in1=veps_ap, op=ALU.mult), "v")
        E.g(e.tensor_scalar(out=scr_ap, in0=scr_ap, scalar1=-0.5, scalar2=1.5,
                              op0=ALU.mult, op1=ALU.add), "v")
        E.g(e.tensor_tensor(out=rstd_ap, in0=rstd_ap, in1=scr_ap, op=ALU.mult), "v")

    def collective(stage_ap, recv_ap, dram_i, dram_o):
        def fg(e, E):
            E.w("v")
            E.inc(e.dma_start(out=dram_i, in_=stage_ap), "dG", 16)
            E.w("dG")
            E.inc(e.collective_compute("AllReduce", ALU.add, replica_groups=RG,
                                       ins=[dram_i], outs=[dram_o]), "g")
            E.w("g")
            E.inc(e.dma_start(out=recv_ap, in_=dram_o), "dG", 16)
        ctx.sec("g", fg)

    def repl_row(row_ap, ncols, dst):
        off = 0
        while off < ncols:
            nn = min(512, ncols - off)
            def fp1(e, E, off=off, nn=nn):
                E.w("dS"); E.w("v"); E.w("c")
                E.inc(e.matmul(out=pD[:, 0:nn], lhsT=ones_f[:], rhs=row_ap[:, off:off + nn],
                               start=True, stop=True), "p")
            ctx.sec("p", fp1)
            def fv1(e, E, off=off, nn=nn):
                E.w("p")
                E.inc(e.tensor_copy(out=dst[:, off:off + nn], in_=pD[:, 0:nn]), "v")
            ctx.sec("v", fv1)
            off += nn

    w1p, g_w1p = mk("sb_w1p", [128, 8, D], BF16)
    w2p, g_w2p = mk("sb_w2p", [128, 8, D], BF16)
    Sv, g_Sv = mk("Sv", [64, D])
    SvT_bf, g_SvT = mk("SvT_bf", [128, 8, 64], BF16)
    sscale, g_ssc = mk("sscale", [64, 1])
    alpha_tm, g_atm = mk("alpha_tm", [128, NT, 64], BF16)
    alpha_sm, g_asm = mk("alpha_sm", [64, T], BF16)
    si_full, g_si = mk("si_full", [64, 1025])
    hX, g_hX = mk("hX", [128, NJ, 3 + T], BF16)

    xn_tm_bf, g_xnbf = mk("xn_tm_bf", [128, NT, D], BF16, side="right")
    xp3, g_xp3 = mk("sb_xp3", [3, D], side="right")
    xp3n, g_xp3n = mk("sb_xp3n", [3, D], BF16, side="right")
    xst0, g_xst0 = mk("xst0", [128, D], side="right")
    xst1, g_xst1 = mk("xst1", [128, D], side="right")

    def s0(e, E):
        pairs = [
            (consts[:], consts_in), (ident_b[:], ident_b_in), (ones_f[:], ones_f_in),
            (ones_colb[:], ones_colb_in),
            (Gt[:], G_in), (GTt[:], GT_in), (flagS[:], flag_send), (flagR[:], flag_recv),
            (xp3[:], xprev3),
            (nw2row[:], nw2row_in), (nowrow[:], nowrow_in),
            (bg_pp[:], bg_in), (b1f_pp[:], b1f_in), (dp_s[:, 0:1], dp_in),
        ]
        for l in range(3):
            pairs += [(cw[l][:], cw_in[l]), (cb[l][:], cb_in[l]), (wrb[l][:], wrb_in[l]),
                      (wib[l][:], wib_in[l]), (la[l][:], la_in[l])]
        for l in range(2):
            pairs += [(nwpp[l][:], nwpp_in[l])]
        for dst, src in pairs:
            E.inc(e.dma_start(out=dst, in_=src), "dS", 16)
    ctx.sec("s", s0)

    def s0v(e, E):
        E.inc(e.memset(ones_ch[:], 1.0), "v")
        E.inc(e.memset(ones_r5[:], 1.0), "v")
    ctx.sec("v", s0v)

    x_t = x_own.rearrange("(n p) d -> p n d", p=128)
    for n in range(NT):
        xt = xst0 if n % 2 == 0 else xst1
        def f_lx(e, E, n=n, xt=xt):
            if n >= 2:
                E.w("v"); E.w("c")
            E.inc(e.dma_start(out=xt[:], in_=x_t[:, n, :]), "dS", 16)
        ctx.sec("s" if n % 2 == 0 else "c", f_lx)
        def f_lr(e, E, n=n, xt=xt):
            E.w("dS")
            E.inc(e.reduce_sum(out=st1[:, n:n + 1], in_=xt[:], axis=AX), "v")
        ctx.sec("v", f_lr)
        def f_ls(e, E, n=n, xt=xt):
            E.w("dS")
            E.inc(e.activation(out=sq_scr[:], in_=xt[:], func=AF.Square,
                               accum_out=st2[:, n:n + 1]), "c")
        ctx.sec("c", f_ls)

    def f_ln_mv(e, E):
        E.w("c")
        E.g(e.tensor_scalar(out=stm[:], in0=st1[:], scalar1=1.0 / 1024.0,
                              scalar2=None, op0=ALU.mult), "v")
        E.g(e.tensor_scalar(out=strv[:], in0=st2[:], scalar1=1.0 / 1024.0,
                              scalar2=None, op0=ALU.mult), "v")
        E.g(e.tensor_tensor(out=scr16[:], in0=stm[:], in1=stm[:], op=ALU.mult), "v")
        E.g(e.tensor_tensor(out=strv[:], in0=strv[:], in1=scr16[:], op=ALU.subtract), "v")
        E.g(e.tensor_scalar(out=strv[:], in0=strv[:], scalar1=1e-5,
                              scalar2=None, op0=ALU.add), "v")
    ctx.sec("v", f_ln_mv)

    def f_ln_sqrt(e, E):
        E.w("v")
        E.inc(e.activation(out=scr16[:], in_=strv[:], func=AF.Sqrt), "c")
    ctx.sec("c", f_ln_sqrt)

    def f_ln_fin(e, E):
        E.w("c")
        newton_rsqrt(e, E, strv[:], st2[:], scr16[:])
        E.g(e.tensor_copy(out=strv[:], in_=st2[:]), "v")
        E.g(e.tensor_tensor(out=stmr[:], in0=stm[:], in1=strv[:], op=ALU.mult), "v")
        E.inc(e.reduce_sum(out=p3st[:, 0:1], in_=xp3[:], axis=AX), "v")
    ctx.sec("v", f_ln_fin)
    for n in range(NT):
        xt = xst0 if n % 2 == 0 else xst1
        def f_ax(e, E, n=n, xt=xt):
            E.w("v")
            E.inc(e.dma_start(out=xt[:], in_=x_t[:, n, :]), "dS", 16)
        ctx.sec("s", f_ax)
        def f_av(e, E, n=n, xt=xt):
            E.w("dS")
            E.inc(e.tensor_scalar(out=xn_tm_bf[:, n, :], in0=xt[:],
                                  scalar1=strv[:, n:n + 1], scalar2=stmr[:, n:n + 1],
                                  op0=ALU.mult, op1=ALU.subtract), "v")
        ctx.sec("v", f_av)

    def f_p3c(e, E):
        E.w("dS")
        E.inc(e.activation(out=sq_scr[0:3, :], in_=xp3[:], func=AF.Square,
                           accum_out=p3st[:, 1:2]), "c")
    ctx.sec("c", f_p3c)

    def f_p3v(e, E):
        E.w("c")
        E.g(e.tensor_scalar(out=p3st[:, 0:2], in0=p3st[:, 0:2], scalar1=1.0 / 1024.0,
                              scalar2=None, op0=ALU.mult), "v")
        E.g(e.tensor_scalar(out=p3st[:, 2:3], in0=p3st[:, 0:1], scalar1=p3st[:, 0:1],
                              scalar2=None, op0=ALU.mult), "v")
        E.g(e.tensor_tensor(out=p3st[:, 1:2], in0=p3st[:, 1:2], in1=p3st[:, 2:3], op=ALU.subtract), "v")
        E.g(e.tensor_scalar(out=p3st[:, 1:2], in0=p3st[:, 1:2], scalar1=1e-5,
                              scalar2=None, op0=ALU.add), "v")
    ctx.sec("v", f_p3v)

    def f_p3c2(e, E):
        E.w("v")
        E.inc(e.activation(out=p3st[:, 2:3], in_=p3st[:, 1:2], func=AF.Sqrt), "c")
    ctx.sec("c", f_p3c2)

    def f_p3v2(e, E):
        E.w("c")
        E.g(e.reciprocal(out=p3st[:, 3:4], in_=p3st[:, 2:3]), "v")
        E.g(e.tensor_scalar(out=p3st[:, 0:1], in0=p3st[:, 0:1], scalar1=p3st[:, 3:4],
                              scalar2=None, op0=ALU.mult), "v")
        E.g(e.tensor_scalar(out=xp3n[:], in0=xp3[:], scalar1=p3st[:, 3:4],
                              scalar2=p3st[:, 0:1], op0=ALU.mult, op1=ALU.subtract), "v")
    ctx.sec("v", f_p3v2)

    if (d := dump_req("xn_bf", [128, NT, D], BF16)) is not None:
        def fd(e, E, d=d):
            E.w("v")
            E.inc(e.dma_start(out=d, in_=xn_tm_bf[:]), "dS", 16)
        ctx.sec("s", fd)

    def finish(extra=None):
        def ffin(e, E):
            for k in ["v", "c", "p", "g", "dS", "dG"]:
                E.w(k)
            if extra is not None:
                E.inc(e.dma_start(out=y_out.rearrange("(n p) d -> p n d", p=128),
                                  in_=extra), "dS", 16)
                E.w("dS")
        ctx.sec("s", ffin)
        blockg.__exit__(None, None, None)
        return nc, dbg

    if build_upto < 2:
        return finish(None)

    g_xst1.__exit__(None, None, None)
    g_xst0.__exit__(None, None, None)


    bt_end = {}
    bv_end = {}
    for n in range(NT):
        pTx = pT if n % 2 == 0 else pT2
        def fp(e, E, n=n, pTx=pTx):
            if n < 2:
                E.w("v")
            else:
                E.w("v", bv_end[n - 2])
            for j in range(NJ):
                ins = e.transpose(out=pTx[:, j * 128:(j + 1) * 128],
                                  in_=xn_tm_bf[:, n, j * 128:(j + 1) * 128],
                                  identity=ident_b[:])
            E.inc(ins, "p")
        ctx.sec("p", fp)
        bt_end[n] = ctx.cnt["p"]

        def fv(e, E, n=n, pTx=pTx):
            E.w("p", bt_end[n])
            E.inc(e.tensor_copy(out=hX[:, :, 3 + n * 128: 3 + (n + 1) * 128],
                                in_=pTx[:].rearrange("p (j t) -> p j t", j=NJ)), "v")
        ctx.sec("v", fv)
        bv_end[n] = ctx.cnt["v"]

    def fp2(e, E):
        E.w("v")
        for j in range(NJ):
            ins = e.matmul(out=pA[:, j * 8:j * 8 + 3], lhsT=xp3n[:, j * 128:(j + 1) * 128],
                           rhs=ident_b[0:3, 0:3], start=True, stop=True)
        E.inc(ins, "p")
    ctx.sec("p", fp2)

    def fv2(e, E):
        E.w("p")
        E.inc(e.tensor_copy(out=hX[:, :, 0:3],
                            in_=pA[:, 0:NJ * 8].rearrange("p (j t) -> p j t", j=NJ)[:, :, 0:3]), "v")
    ctx.sec("v", fv2)
    g_xp3n.__exit__(None, None, None)
    g_xp3.__exit__(None, None, None)

    if (d2 := dump_req("xn_fm", [128, NJ, 3 + T], BF16)) is not None:
        def fd2(e, E, d2=d2):
            E.w("v")
            E.inc(e.dma_start(out=d2, in_=hX[:]), "dS", 16)
        ctx.sec("s", fd2)

    routeg = []
    def mkr(name, shape, dtype=F32):
        t, g = mk(name, shape, dtype, side="right")
        routeg.append(g)
        return t
    meanT_bf = mkr("meanT_bf", [128, 8], BF16)
    combo_b = mkr("combo_b", [128, 4])
    b1pp = mkr("b1pp", [128, 4])
    h_route = mkr("h_route", [128, 4, T], BF16)
    logits_tm = mkr("logits_tm", [128, NT, 64])
    vals8 = mkr("vals8", [128, NT, 8])
    mrep_t = mkr("mrep_t", [128, 64])
    e_scr = mkr("e_scr", [128, 64])
    negm = mkr("negm", [128, NT])
    Z8 = mkr("Z8", [128, NT])
    invZ = mkr("invZ", [128, NT])
    invtau = mkr("invtau", [128, 1])
    tau_s = mkr("tau_s", [1, 4])
    b2_rep = mkr("b2_rep", [128, 64])
    b2_row = mkr("b2_row", [1, 64])
    si_stage = mkr("si_stage", [64, 1025])
    sv_bf_scr = mkr("sv_bf_scr", [64, D], BF16)

    if build_upto >= 3:
        def fs_c(e, E):
            E.inc(e.dma_start(out=Sv[:], in_=slot_init_in), "dS", 16)
            E.inc(e.dma_start(out=sscale[:], in_=slot_scale_in), "dS", 16)
            E.inc(e.dma_start(out=b1pp[:], in_=b1r_in), "dS", 16)
            E.inc(e.dma_start(out=b2_row[:], in_=b2r_in), "dS", 16)
            E.inc(e.dma_start(out=tau_s[:, 0:1], in_=tau_in), "dS", 16)
        ctx.sec("s", fs_c)

        def fg0(e, E):
            E.inc(e.dma_start(out=w1p[:, :, 0:512], in_=w1aT_in), "dG", 16)
            E.inc(e.dma_start(out=w1p[:, :, 512:1024], in_=w1bT_in), "dG", 16)
            E.inc(e.dma_start(out=w2p[:, 0:4, 0:64], in_=w2T_in), "dG", 16)
        ctx.sec("g", fg0)

        def fv_c1(e, E):
            E.w("dS")
            E.g(e.tensor_scalar(out=Sv[:], in0=Sv[:], scalar1=sscale[:, 0:1],
                                  scalar2=None, op0=ALU.mult), "v")
            E.g(e.tensor_copy(out=sv_bf_scr[:], in_=Sv[:]), "v")
        ctx.sec("v", fv_c1)

        def fp_c1(e, E):
            E.w("v")
            for j in range(NJ):
                ins = e.transpose(out=pT[:, j * 64:(j + 1) * 64],
                                  in_=sv_bf_scr[:, j * 128:(j + 1) * 128],
                                  identity=ident_b[0:64, 0:64])
            E.inc(ins, "p")
        ctx.sec("p", fp_c1)

        def fv_c2(e, E):
            E.w("p")
            E.g(e.tensor_copy(out=SvT_bf[:], in_=pT[:, 0:512].rearrange("p (j s) -> p j s", j=NJ)), "v")
            E.g(e.tensor_copy(out=sq_scr[:, 0:512], in_=SvT_bf[:].rearrange("p j s -> p (j s)")), "v")
            for j in range(NJ):
                E.inc(e.reduce_sum(out=stage32[:, j:j + 1], in_=sq_scr[:, j * 64:(j + 1) * 64], axis=AX), "v")
            E.w("v")
            E.g(e.tensor_scalar(out=stage32[:, 0:8], in0=stage32[:, 0:8], scalar1=1.0 / 64.0,
                                  scalar2=None, op0=ALU.mult), "v")
            E.g(e.tensor_copy(out=meanT_bf[:], in_=stage32[:, 0:8]), "v")
        ctx.sec("v", fv_c2)

        def fp_c2(e, E):
            E.w("v"); E.w("dG")
            for m in range(4):
                for k in range(8):
                    ins = e.matmul(out=pA[:, m:m + 1],
                                   lhsT=w1p[:, k, 512 + m * 128:512 + (m + 1) * 128],
                                   rhs=meanT_bf[:, k:k + 1], start=(k == 0), stop=(k == 7))
            E.inc(ins, "p")
        ctx.sec("p", fp_c2)

        def fv_c3(e, E):
            E.w("p")
            E.g(e.tensor_tensor(out=combo_b[:], in0=b1pp[:], in1=pA[:, 0:4], op=ALU.add), "v")
        ctx.sec("v", fv_c3)

        def fc_c0(e, E):
            E.w("dS")
            E.inc(e.activation(out=tau_s[:, 1:2], in_=tau_s[:, 0:1], func=AF.Abs), "c")
        ctx.sec("c", fc_c0)

        def fv_c4(e, E):
            E.w("c")
            E.g(e.tensor_scalar(out=tau_s[:, 1:2], in0=tau_s[:, 1:2], scalar1=0.1,
                                  scalar2=None, op0=ALU.add), "v")
            E.g(e.reciprocal(out=tau_s[:, 2:3], in_=tau_s[:, 1:2]), "v")
        ctx.sec("v", fv_c4)

        def fp_c3(e, E):
            E.w("v"); E.w("c")
            E.inc(e.matmul(out=pB[:, 0:1], lhsT=ones_f[:], rhs=tau_s[:, 2:3],
                           start=True, stop=True), "p")
            E.inc(e.matmul(out=pB[:, 64:128], lhsT=ones_f[:], rhs=b2_row[:],
                           start=True, stop=True), "p")
        ctx.sec("p", fp_c3)

        def fv_c5(e, E):
            E.w("p")
            E.inc(e.tensor_copy(out=invtau[:], in_=pB[:, 0:1]), "v")
            E.inc(e.tensor_copy(out=b2_rep[:], in_=pB[:, 64:128]), "v")
        ctx.sec("v", fv_c5)

        for m in range(4):
            for c4 in range(4):
                def fp_h(e, E, m=m, c4=c4):
                    E.w("v"); E.w("dG"); E.w("c")
                    ps = pA if (m * 4 + c4) % 2 == 0 else pB
                    for k in range(8):
                        ins = e.matmul(out=ps[:], lhsT=w1p[:, k, m * 128:(m + 1) * 128],
                                       rhs=hX[:, k, 3 + c4 * 512:3 + (c4 + 1) * 512],
                                       start=(k == 0), stop=(k == 7))
                    E.inc(ins, "p")
                ctx.sec("p", fp_h)

                def fc_h(e, E, m=m, c4=c4):
                    ps = pA if (m * 4 + c4) % 2 == 0 else pB
                    E.w("p")
                    E.inc(e.activation(out=h_route[:, m, c4 * 512:(c4 + 1) * 512], in_=ps[:],
                                       func=AF.Gelu, bias=combo_b[:, m:m + 1]), "c")
                ctx.sec("c", fc_h)

        for n in range(NT):
            def fp_lg(e, E, n=n):
                E.w("c")
                if n > 0:
                    E.w("v")
                for k in range(4):
                    ins = e.matmul(out=pC[:, 0:64], lhsT=h_route[:, k, n * 128:(n + 1) * 128],
                                   rhs=w2p[:, k, 0:64], start=(k == 0), stop=(k == 3))
                E.inc(ins, "p")
            ctx.sec("p", fp_lg)

            def fv_lg(e, E, n=n):
                E.w("p")
                E.g(e.tensor_tensor(out=logits_tm[:, n, :], in0=pC[:, 0:64],
                                    in1=b2_rep[:], op=ALU.add), "v")
                E.inc(e.tensor_scalar(out=logits_tm[:, n, :], in0=logits_tm[:, n, :],
                                      scalar1=invtau[:, 0:1], scalar2=None, op0=ALU.mult), "v")
            ctx.sec("v", fv_lg)

        if (dsv := dump_req("svbf", [64, D], BF16)) is not None:
            def fdsv(e, E, dsv=dsv):
                E.w("v")
                E.inc(e.dma_start(out=dsv, in_=sv_bf_scr[:]), "dS", 16)
            ctx.sec("s", fdsv)
        if (dst := dump_req("svt", [128, 8, 64], BF16)) is not None:
            def fdst(e, E, dst=dst):
                E.w("v")
                E.inc(e.dma_start(out=dst, in_=SvT_bf[:]), "dS", 16)
            ctx.sec("s", fdst)
        if (dm := dump_req("meanT", [128, 8], BF16)) is not None:
            def fdm(e, E, dm=dm):
                E.w("v")
                E.inc(e.dma_start(out=dm, in_=meanT_bf[:]), "dS", 16)
            ctx.sec("s", fdm)
        if (dcb := dump_req("combo", [128, 4])) is not None:
            def fdcb(e, E, dcb=dcb):
                E.w("v")
                E.inc(e.dma_start(out=dcb, in_=combo_b[:]), "dS", 16)
            ctx.sec("s", fdcb)
        if (dhr := dump_req("hroute", [128, 4, T], BF16)) is not None:
            def fdhr(e, E, dhr=dhr):
                E.w("c")
                E.inc(e.dma_start(out=dhr, in_=h_route[:]), "dS", 16)
            ctx.sec("s", fdhr)
        if (d3 := dump_req("logits", [128, NT, 64])) is not None:
            def fd3(e, E, d3=d3):
                E.w("v")
                E.inc(e.dma_start(out=d3, in_=logits_tm[:]), "dS", 16)
            ctx.sec("s", fd3)

        for n in range(NT):
            def fv_t8(e, E, n=n):
                E.g(e.max(out=vals8[:, n, :], in_=logits_tm[:, n, :]), "v")
                E.g(e.match_replace(out=mrep_t[:], in_to_replace=vals8[:, n, :],
                                      in_values=logits_tm[:, n, :], imm_value=-1e30), "v")
                E.g(e.tensor_scalar(out=negm[:, n:n + 1], in0=vals8[:, n, 0:1],
                                      scalar1=-1.0, scalar2=None, op0=ALU.mult), "v")
            ctx.sec("v", fv_t8)

            def fc_t8(e, E, n=n):
                E.w("v")
                E.inc(e.activation(out=e_scr[:], in_=logits_tm[:, n, :], func=AF.Exp,
                                   bias=negm[:, n:n + 1]), "c")
                E.inc(e.activation(out=sq_scr[:, 0:8], in_=vals8[:, n, :], func=AF.Exp,
                                   bias=negm[:, n:n + 1], accum_out=Z8[:, n:n + 1]), "c")
            ctx.sec("c", fc_t8)

            def fv_t9(e, E, n=n):
                E.w("c")
                E.g(e.reciprocal(out=invZ[:, n:n + 1], in_=Z8[:, n:n + 1]), "v")
                E.g(e.tensor_scalar(out=mrep_t[:], in0=mrep_t[:], scalar1=-1e30,
                                      scalar2=None, op0=ALU.is_equal), "v")
                E.g(e.tensor_scalar(out=e_scr[:], in0=e_scr[:], scalar1=invZ[:, n:n + 1],
                                      scalar2=None, op0=ALU.mult), "v")
                E.g(e.tensor_tensor(out=alpha_tm[:, n, :], in0=e_scr[:], in1=mrep_t[:],
                                      op=ALU.mult), "v")
            ctx.sec("v", fv_t9)

        for n in range(NT):
            def fp_at(e, E, n=n):
                E.w("v")
                E.inc(e.transpose(out=pT[0:64, 0:128], in_=alpha_tm[:, n, :],
                                  identity=ident_b[:]), "p")
            ctx.sec("p", fp_at)

            def fv_at(e, E, n=n):
                E.w("p")
                E.inc(e.tensor_copy(out=alpha_sm[:, n * 128:(n + 1) * 128], in_=pT[0:64, 0:128]), "v")
            ctx.sec("v", fv_at)

        def fp_si(e, E):
            E.w("v")
            for n in range(NT):
                for half in range(2):
                    ins = e.matmul(out=pE[0:64, half * 512:(half + 1) * 512],
                                   lhsT=alpha_tm[:, n, :],
                                   rhs=xn_tm_bf[:, n, half * 512:(half + 1) * 512],
                                   start=(n == 0), stop=(n == NT - 1))
            E.inc(ins, "p")
        ctx.sec("p", fp_si)

        def fv_si(e, E):
            E.w("p")
            E.inc(e.tensor_copy(out=si_stage[:, 0:1024], in_=pE[0:64, :]), "v")
            E.inc(e.reduce_sum(out=si_stage[:, 1024:1025], in_=alpha_sm[:], axis=AX), "v")
        ctx.sec("v", fv_si)

        if (dat := dump_req("alpha", [128, NT, 64], BF16)) is not None:
            def fdat(e, E, dat=dat):
                E.w("v")
                E.inc(e.dma_start(out=dat, in_=alpha_tm[:]), "dS", 16)
            ctx.sec("s", fdat)
        if (dss := dump_req("sistage", [64, 1025])) is not None:
            def fdss(e, E, dss=dss):
                E.w("v")
                E.inc(e.dma_start(out=dss, in_=si_stage[:]), "dS", 16)
            ctx.sec("s", fdss)
        collective(si_stage[:], si_full[:], cc_slot_i, cc_slot_o)
        if (d4 := dump_req("si", [64, 1025])) is not None:
            def fd4(e, E, d4=d4):
                E.w("dG")
                E.inc(e.dma_start(out=d4, in_=si_full[:]), "dS", 16)
            ctx.sec("s", fd4)

    for g in reversed(routeg):
        g.__exit__(None, None, None)
    g_xnbf.__exit__(None, None, None)

    wrbrow, g_wrbr = mk("sb_wrbrow", [1, D], BF16, side="right")
    wibrow, g_wibr = mk("sb_wibrow", [1, D], BF16, side="right")
    xcv, g_xcv = mk("sb_xcv", [128, TSD], F32, side="right")
    xbfA, g_xbfA = mk("sb_xbfA", [128, NJ, TSD], BF16, side="right")
    xbfB, g_xbfB = mk("sb_xbfB", [128, NJ, TSD], BF16, side="right")
    rb, g_rb = mk("sb_rb", [128, NJ, TSD], F32, side="right")
    ib, g_ib = mk("sb_ib", [128, NJ, TSD], F32, side="right")
    clb, g_clb = mk("sb_clb", [128, NJ, TSD], F32, side="right")
    hbf_full, g_hbf = mk("hbf_full", [128, NJ, T], BF16, side="right")
    FLT = lambda b: b[:].rearrange("p j t -> p (j t)")

    def rglru_layer(l, h):
        def fc_p(e, E):
            E.w("dS")
            E.g(e.activation(out=scv[l][:], in_=la[l][:], func=AF.Sigmoid), "c")
            E.g(e.activation(out=scv[l][:], in_=scv[l][:], func=AF.Ln), "c")
        ctx.sec("c", fc_p)

        def fv_p(e, E):
            E.w("c")
            E.g(e.tensor_scalar(out=scv[l][:], in0=scv[l][:], scalar1=8.0,
                                  scalar2=None, op0=ALU.mult), "v")
            E.g(e.memset(ccl[:], 0.0), "v")
        ctx.sec("v", fv_p)

        psnap = ctx.cnt["p"]
        def fg_w(e, E, psnap=psnap):
            E.w("p", psnap)
            E.inc(e.dma_start(out=w1p[:], in_=wrT[l]), "dG", 16)
            E.inc(e.dma_start(out=w2p[:], in_=wiT[l]), "dG", 16)
            E.inc(e.dma_start(out=wrbrow[:], in_=wrbr_in[l]), "dG", 16)
            E.inc(e.dma_start(out=wibrow[:], in_=wibr_in[l]), "dG", 16)
        ctx.sec("g", fg_w)
        dg_w = ctx.cnt["dG"]

        cast_end = {}
        mm_end = {}
        ev_end = {}
        spl_end = {}

        def conv_stripe(s):
            c0 = s * TSD
            xbf = xbfA if s % 2 == 0 else xbfB
            def fv_cv(e, E, s=s, c0=c0, xbf=xbf):
                if s >= 2:
                    E.w("p", mm_end[s - 2])
                for j in range(NJ):
                    e.tensor_scalar(out=xbf[:, j, :], in0=h[:, j, c0:c0 + TSD],
                                    scalar1=cw[l][:, j * 4:j * 4 + 1], scalar2=None, op0=ALU.mult)
                    for kk in range(1, 3):
                        e.scalar_tensor_tensor(out=xbf[:, j, :], in0=h[:, j, c0 + kk:c0 + kk + TSD],
                                               scalar=cw[l][:, j * 4 + kk:j * 4 + kk + 1],
                                               in1=xbf[:, j, :], op0=ALU.mult, op1=ALU.add)
                    e.scalar_tensor_tensor(out=xcv[:], in0=h[:, j, c0 + 3:c0 + 3 + TSD],
                                           scalar=cw[l][:, j * 4 + 3:j * 4 + 4],
                                           in1=xbf[:, j, :], op0=ALU.mult, op1=ALU.add)
                    ins = e.tensor_scalar(out=xbf[:, j, :], in0=xcv[:],
                                          scalar1=cb[l][:, j:j + 1], scalar2=None, op0=ALU.add)
                E.inc(ins, "v")
            ctx.sec("v", fv_cv)
            cast_end[s] = ctx.cnt["v"]

        def mms_stripe(s):
            xbf = xbfA if s % 2 == 0 else xbfB
            evc = {}
            for gate, (wp, dst) in enumerate(((w1p, rb), (w2p, ib))):
                for m in range(NJ):
                    ps = (pA, pB, pC, pD)[gate * 2 + (m % 2)]
                    cwait = evc.get((gate, m - 2), ev_end.get(s - 1, 0))
                    brow = (wrbrow, wibrow)[gate]
                    def fp_m(e, E, s=s, m=m, ps=ps, wp=wp, xbf=xbf, cwait=cwait,
                             brow=brow, gate=gate):
                        E.w("v", cast_end[s])
                        E.w("dG", dg_w)
                        E.w("v", cwait)
                        if gate == 1 and m == 0:
                            E.w("dG", spl_end.get(s - 1, 0))
                        e.matmul(out=ps[:, 0:TSD], lhsT=brow[0:1, m * 128:(m + 1) * 128],
                                 rhs=ones_r5[:], start=True, stop=False)
                        for k in range(NJ):
                            ins = e.matmul(out=ps[:, 0:TSD], lhsT=wp[:, k, m * 128:(m + 1) * 128],
                                           rhs=xbf[:, k, :], start=False, stop=(k == NJ - 1))
                        E.inc(ins, "p")
                    ctx.sec("p", fp_m)
                    pm = ctx.cnt["p"]
                    def fv_ev(e, E, s=s, m=m, ps=ps, dst=dst, pm=pm):
                        E.w("p", pm)
                        E.inc(e.tensor_copy(out=dst[:, m, :], in_=ps[:, 0:TSD]), "v")
                    ctx.sec("v", fv_ev)
                    evc[(gate, m)] = ctx.cnt["v"]
            mm_end[s] = ctx.cnt["p"]
            ev_end[s] = ctx.cnt["v"]
            def fc_sig(e, E, s=s):
                E.w("v", ev_end[s])
                E.inc(e.activation(out=FLT(rb), in_=FLT(rb), func=AF.Sigmoid), "c")
                E.inc(e.activation(out=FLT(ib), in_=FLT(ib), func=AF.Sigmoid), "c")
            ctx.sec("c", fc_sig)

        def vchainA_stripe(s):
            xbf = xbfA if s % 2 == 0 else xbfB
            def fv_a(e, E, s=s, xbf=xbf):
                E.w("c")
                if s >= 1:
                    E.w("dG", spl_end[s - 1])
                E.inc(e.tensor_tensor(out=FLT(ib), in0=FLT(ib), in1=FLT(xbf), op=ALU.mult), "v")
                for j in range(NJ):
                    e.tensor_scalar(out=rb[:, j, :], in0=rb[:, j, :],
                                    scalar1=scv[l][:, j:j + 1], scalar2=None, op0=ALU.mult)
                E.w("v")
                for j in range(NJ):
                    ins = e.tensor_tensor_scan(out=clb[:, j, :], data0=ones_ch[:, 0:TSD],
                                               data1=rb[:, j, :], initial=ccl[:, j:j + 1],
                                               op0=ALU.mult, op1=ALU.add)
                E.g(ins, "v")
                E.g(e.tensor_copy(out=ccl[:], in_=clb[:, :, TSD - 1]), "v")
            ctx.sec("v", fv_a)

        def exp_stripe(s):
            vA = ctx.cnt["v"]
            def fc_x(e, E, vA=vA):
                E.w("v", vA)
                E.inc(e.activation(out=FLT(rb), in_=FLT(rb), func=AF.Exp, scale=2.0), "c")
            ctx.sec("c", fc_x)

        def vchainB_stripe(s):
            def fv_b1(e, E):
                E.w("c")
                E.inc(e.tensor_scalar(out=FLT(rb), in0=FLT(rb), scalar1=-1.0,
                                      scalar2=1.0, op0=ALU.mult, op1=ALU.add), "v")
                E.inc(e.tensor_scalar(out=FLT(rb), in0=FLT(rb), scalar1=1e-6,
                                      scalar2=None, op0=ALU.max), "v")
            ctx.sec("v", fv_b1)

        def sqrt_stripe(s, vsnap):
            def fc_s(e, E, vsnap=vsnap):
                E.w("v", vsnap)
                E.inc(e.activation(out=FLT(rb), in_=FLT(rb), func=AF.Sqrt), "c")
            ctx.sec("c", fc_s)

        def bspill_stripe(s):
            c0 = s * TSD
            csnap = ctx.cnt["c"]
            def fv_bb(e, E, csnap=csnap):
                E.w("c", csnap)
                E.inc(e.tensor_tensor(out=FLT(ib), in0=FLT(ib), in1=FLT(rb), op=ALU.mult), "v")
            ctx.sec("v", fv_bb)
            vsn = ctx.cnt["v"]
            def fg_sp(e, E, c0=c0, vsn=vsn):
                E.w("v", vsn)
                E.inc(e.dma_start(out=sp_cloc[:, :, c0:c0 + TSD], in_=clb[:]), "dG", 16)
                E.inc(e.dma_start(out=sp_b[:, :, c0:c0 + TSD], in_=ib[:]), "dG", 16)
            ctx.sec("g", fg_sp)
            spl_end[s] = ctx.cnt["dG"]

        conv_stripe(0)
        conv_stripe(1)
        for s in range(NSD):
            mms_stripe(s)
            vchainA_stripe(s)
            exp_stripe(s)
            if s + 2 < NSD:
                conv_stripe(s + 2)
            vchainB_stripe(s)
            vB = ctx.cnt["v"]
            sqrt_stripe(s, vB)
            bspill_stripe(s)

        def fv_s1(e, E):
            E.w("v")
            E.g(e.tensor_scalar(out=stage32[:, 0:8], in0=ccl[:], scalar1=-80.0,
                                  scalar2=None, op0=ALU.max), "v")
            E.g(e.tensor_scalar(out=stage32[:, 0:8], in0=stage32[:, 0:8],
                                  scalar1=flagS[:, 0:1], scalar2=None, op0=ALU.mult), "v")
        ctx.sec("v", fv_s1)
        collective(stage32[:, 0:8], recv32[:, 0:8], cc1i[l], cc1o[l])

        def fv_r1(e, E):
            E.w("dG")
            E.g(e.tensor_scalar(out=ccar[:], in0=recv32[:, 0:8], scalar1=flagR[:, 0:1],
                                  scalar2=None, op0=ALU.mult), "v")
            E.g(e.tensor_scalar(out=nccst[:], in0=ccar[:], scalar1=-80.0,
                                  scalar2=None, op0=ALU.max), "v")
            E.g(e.tensor_copy(out=ccprev[:], in_=nccst[:]), "v")
            E.g(e.tensor_scalar(out=nccst[:], in0=nccst[:], scalar1=-1.0,
                                  scalar2=None, op0=ALU.mult), "v")
            E.g(e.memset(hll[:], 0.0), "v")
        ctx.sec("v", fv_r1)

        ld2_end = {}
        hs_end = {}
        sp2_end = {}
        for s in range(NSD):
            c0 = s * TSD
            xbf2 = xbfA if s % 2 == 0 else xbfB
            def fs_l2(e, E, s=s, c0=c0, xbf2=xbf2):
                if s == 0:
                    E.w("v")
                    E.w("dS")
                    E.w("dG")
                else:
                    E.w("dS", sp2_end[s - 1])
                    if s >= 2:
                        E.w("v", hs_end[s - 2])
                E.inc(e.dma_start(out=clb[:], in_=sp_cloc[:, :, c0:c0 + TSD]), "dS", 16)
                E.inc(e.dma_start(out=xbf2[:], in_=sp_b[:, :, c0:c0 + TSD]), "dS", 16)
            ctx.sec("s", fs_l2)
            ld2_end[s] = ctx.cnt["dS"]

            def fv_cc(e, E, s=s):
                E.w("dS", ld2_end[s])
                for j in range(NJ):
                    e.tensor_scalar(out=clb[:, j, :], in0=clb[:, j, :],
                                    scalar1=ccar[:, j:j + 1], scalar2=-80.0,
                                    op0=ALU.add, op1=ALU.max)
                for j in range(NJ):
                    e.tensor_tensor(out=rb[:, j, 1:TSD], in0=clb[:, j, 1:TSD],
                                    in1=clb[:, j, 0:TSD - 1], op=ALU.subtract)
                    e.tensor_tensor(out=rb[:, j, 0:1], in0=clb[:, j, 0:1],
                                    in1=ccprev[:, j:j + 1], op=ALU.subtract)
                E.w("v")
                E.g(e.tensor_copy(out=ccprev[:], in_=clb[:, :, TSD - 1]), "v")
            ctx.sec("v", fv_cc)
            vcc = ctx.cnt["v"]

            def fc_x2(e, E, vcc=vcc):
                E.w("v", vcc)
                E.inc(e.activation(out=FLT(rb), in_=FLT(rb), func=AF.Exp), "c")
                E.inc(e.activation(out=FLT(clb), in_=FLT(clb), func=AF.Exp), "c")
            ctx.sec("c", fc_x2)
            cx2 = ctx.cnt["c"]

            xbf2 = xbfA if s % 2 == 0 else xbfB
            def fv_hs(e, E, s=s, c0=c0, cx2=cx2, xbf2=xbf2):
                E.w("c", cx2)
                for j in range(NJ):
                    ins = e.tensor_tensor_scan(out=ib[:, j, :], data0=rb[:, j, :],
                                               data1=xbf2[:, j, :], initial=hll[:, j:j + 1],
                                               op0=ALU.mult, op1=ALU.add)
                E.g(ins, "v")
                E.g(e.tensor_copy(out=hll[:], in_=ib[:, :, TSD - 1]), "v")
                for j in range(NJ):
                    E.inc(e.tensor_copy(out=hbf_full[:, j, c0:c0 + TSD], in_=ib[:, j, :]), "v")
            ctx.sec("v", fv_hs)
            hs_end[s] = ctx.cnt["v"]

            def fs_sP(e, E, s=s, c0=c0, cx2=cx2):
                E.w("c", cx2)
                E.inc(e.dma_start(out=sp_P[:, :, c0:c0 + TSD], in_=clb[:]), "dS", 16)
            ctx.sec("s", fs_sP)
            sp2_end[s] = ctx.cnt["dS"]

        def fv_s2(e, E):
            E.w("v")
            E.g(e.tensor_scalar(out=stage32[:, 0:8], in0=hll[:], scalar1=flagS[:, 0:1],
                                  scalar2=None, op0=ALU.mult), "v")
        ctx.sec("v", fv_s2)
        collective(stage32[:, 0:8], recv32[:, 0:8], cc2i[l], cc2o[l])

        def fv_r2(e, E):
            E.w("dG")
            E.g(e.tensor_scalar(out=hcar[:], in0=recv32[:, 0:8], scalar1=flagR[:, 0:1],
                                  scalar2=None, op0=ALU.mult), "v")
        ctx.sec("v", fv_r2)
        def fc_r2(e, E):
            E.w("v")
            E.inc(e.activation(out=recv32[:, 0:8], in_=nccst[:], func=AF.Exp), "c")
        ctx.sec("c", fc_r2)
        def fv_r2b(e, E):
            E.w("c")
            E.g(e.tensor_tensor(out=hcar[:], in0=hcar[:], in1=recv32[:, 0:8], op=ALU.mult), "v")
        ctx.sec("v", fv_r2b)

        co_end = {}

        def emit_corr(s):
            c0 = s * TSD
            def fs_lP(e, E, s=s, c0=c0):
                if s >= 1:
                    E.w("v", co_end[s - 1])
                else:
                    E.w("dS")
                E.inc(e.dma_start(out=clb[:], in_=sp_P[:, :, c0:c0 + TSD]), "dS", 16)
            ctx.sec("s", fs_lP)
            lp = ctx.cnt["dS"]

            def fv_co(e, E, s=s, c0=c0, lp=lp):
                E.w("dS", lp)
                for j in range(NJ):
                    ins = e.scalar_tensor_tensor(out=hbf_full[:, j, c0:c0 + TSD],
                                                 in0=clb[:, j, :], scalar=hcar[:, j:j + 1],
                                                 in1=hbf_full[:, j, c0:c0 + TSD],
                                                 op0=ALU.mult, op1=ALU.add)
                E.inc(ins, "v")
            ctx.sec("v", fv_co)
            co_end[s] = ctx.cnt["v"]

        psnap2 = ctx.cnt["p"]
        def fg_wo(e, E, psnap2=psnap2):
            E.w("p", psnap2)
            E.inc(e.dma_start(out=w1p[:], in_=woT[l]), "dG", 16)
        ctx.sec("g", fg_wo)
        dg_wo = ctx.cnt["dG"]

        if l < 2:
            for s in range(NSD):
                c0 = s * TSD
                emit_corr(s)
                for m in range(NJ):
                    ps = pA if m % 2 == 0 else pB
                    def fp_w(e, E, s=s, m=m, ps=ps, c0=c0):
                        E.w("v")
                        E.w("dG", dg_wo)
                        for k in range(NJ):
                            ins = e.matmul(out=ps[:, 0:TSD], lhsT=w1p[:, k, m * 128:(m + 1) * 128],
                                           rhs=hbf_full[:, k, c0:c0 + TSD],
                                           start=(k == 0), stop=(k == NJ - 1))
                        E.inc(ins, "p")
                    ctx.sec("p", fp_w)
                    pm = ctx.cnt["p"]
                    def fv_wev(e, E, m=m, ps=ps, pm=pm):
                        E.w("p", pm)
                        E.inc(e.tensor_copy(out=rb[:, m, :], in_=ps[:, 0:TSD]), "v")
                    ctx.sec("v", fv_wev)

                def fv_sq(e, E):
                    E.inc(e.tensor_tensor(out=FLT(xbfB), in0=FLT(rb), in1=FLT(rb),
                                          op=ALU.mult), "v")
                ctx.sec("v", fv_sq)

                def fp_ss(e, E):
                    E.w("v")
                    for k in range(NJ):
                        ins = e.matmul(out=pE[0:1, 0:TSD], lhsT=ones_colb[:],
                                       rhs=xbfB[:, k, :], start=(k == 0), stop=(k == NJ - 1))
                    E.inc(ins, "p")
                ctx.sec("p", fp_ss)

                def fv_rs(e, E):
                    E.w("p")
                    E.g(e.tensor_scalar(out=sq_scr[0:1, 0:TSD], in0=pE[0:1, 0:TSD],
                                          scalar1=1.0 / 1024.0, scalar2=1e-6,
                                          op0=ALU.mult, op1=ALU.add), "v")
                ctx.sec("v", fv_rs)

                def fc_rs(e, E):
                    E.w("v")
                    E.inc(e.activation(out=sq_scr[0:1, TSD:2 * TSD], in_=sq_scr[0:1, 0:TSD],
                                       func=AF.Sqrt), "c")
                ctx.sec("c", fc_rs)

                def fv_rn(e, E):
                    E.w("c")
                    newton_rsqrt(e, E, sq_scr[0:1, 0:TSD], sq_scr2[0:1, 0:TSD],
                                 sq_scr[0:1, TSD:2 * TSD])
                ctx.sec("v", fv_rn)

                def fp_bc(e, E):
                    E.w("v")
                    E.inc(e.matmul(out=pD[:, 0:TSD], lhsT=ones_f[:], rhs=sq_scr2[0:1, 0:TSD],
                                   start=True, stop=True), "p")
                ctx.sec("p", fp_bc)

                def fv_ap(e, E, c0=c0):
                    E.w("p")
                    for j in range(NJ):
                        ins = e.scalar_tensor_tensor(out=h[:, j, 3 + c0:3 + c0 + TSD],
                                                     in0=rb[:, j, :],
                                                     scalar=nwpp[l][:, j:j + 1],
                                                     in1=pD[:, 0:TSD],
                                                     op0=ALU.mult, op1=ALU.mult)
                    E.inc(ins, "v")
                ctx.sec("v", fv_ap)

            def fv_hl(e, E):
                E.w("v")
                for j in range(NJ):
                    ins = e.tensor_copy(out=stage32[:, 8 + j * 3:8 + (j + 1) * 3],
                                        in_=h[:, j, 3 + T - 3:3 + T])
                E.g(ins, "v")
                E.g(e.tensor_scalar(out=stage32[:, 8:32], in0=stage32[:, 8:32],
                                      scalar1=flagS[:, 0:1], scalar2=None, op0=ALU.mult), "v")
            ctx.sec("v", fv_hl)
        else:
            def fp_rep(e, E):
                E.w("dS"); E.w("v"); E.w("c")
                E.inc(e.matmul(out=pA[:], lhsT=ones_f[:], rhs=nw2row[:, 0:512],
                               start=True, stop=True), "p")
                E.inc(e.matmul(out=pB[:], lhsT=ones_f[:], rhs=nw2row[:, 512:1024],
                               start=True, stop=True), "p")
                E.inc(e.matmul(out=pC[:], lhsT=ones_f[:], rhs=nowrow[:, 0:512],
                               start=True, stop=True), "p")
                E.inc(e.matmul(out=pD[:], lhsT=ones_f[:], rhs=nowrow[:, 512:1024],
                               start=True, stop=True), "p")
            ctx.sec("p", fp_rep)
            for nt in range(NT):
                if nt % 4 == 0:
                    emit_corr(nt // 4)
                def fp_w2(e, E, nt=nt):
                    E.w("v"); E.w("dG", dg_wo); E.w("c")
                    for half in range(2):
                        for k in range(NJ):
                            ins = e.matmul(out=pE[:, half * 512:(half + 1) * 512],
                                           lhsT=hbf_full[:, k, nt * 128:(nt + 1) * 128],
                                           rhs=w1p[:, k, half * 512:(half + 1) * 512],
                                           start=(k == 0), stop=(k == NJ - 1))
                    E.inc(ins, "p")
                ctx.sec("p", fp_w2)

                def fc_sq1(e, E, nt=nt):
                    E.w("p")
                    E.inc(e.activation(out=sq_scr[:], in_=pE[:, 0:1024], func=AF.Square,
                                       accum_out=st1[:, 0:1]), "c")
                ctx.sec("c", fc_sq1)

                def fv_e1(e, E, nt=nt):
                    E.w("c")
                    E.g(e.tensor_scalar(out=st1[:, 0:1], in0=st1[:, 0:1], scalar1=1.0 / 1024.0,
                                          scalar2=1e-6, op0=ALU.mult, op1=ALU.add), "v")
                ctx.sec("v", fv_e1)

                def fc_e1(e, E):
                    E.w("v")
                    E.inc(e.activation(out=st2[:, 0:1], in_=st1[:, 0:1], func=AF.Sqrt), "c")
                ctx.sec("c", fc_e1)

                def fv_e2(e, E, nt=nt):
                    E.w("c")
                    newton_rsqrt(e, E, st1[:, 0:1], stm[:, 0:1], st2[:, 0:1])
                    E.g(e.tensor_scalar(out=sq_scr[:], in0=pE[:, 0:1024], scalar1=stm[:, 0:1],
                                          scalar2=None, op0=ALU.mult), "v")
                    E.inc(e.tensor_tensor(out=sq_scr[:, 0:512], in0=sq_scr[:, 0:512], in1=pA[:], op=ALU.mult), "v")
                    E.g(e.tensor_tensor(out=sq_scr[:, 512:1024], in0=sq_scr[:, 512:1024], in1=pB[:], op=ALU.mult), "v")
                ctx.sec("v", fv_e2)

                def fc_sq2(e, E, nt=nt):
                    E.w("v")
                    E.inc(e.activation(out=sq_scr2[:], in_=sq_scr[:], func=AF.Square,
                                       accum_out=st1[:, 1:2]), "c")
                ctx.sec("c", fc_sq2)

                def fv_e2b(e, E, nt=nt):
                    E.w("c")
                    E.g(e.tensor_scalar(out=st1[:, 1:2], in0=st1[:, 1:2], scalar1=1.0 / 1024.0,
                                          scalar2=1e-6, op0=ALU.mult, op1=ALU.add), "v")
                ctx.sec("v", fv_e2b)

                def fc_e2(e, E):
                    E.w("v")
                    E.inc(e.activation(out=st2[:, 1:2], in_=st1[:, 1:2], func=AF.Sqrt), "c")
                ctx.sec("c", fc_e2)

                def fv_e3(e, E, nt=nt):
                    E.w("c")
                    newton_rsqrt(e, E, st1[:, 1:2], stm[:, 1:2], st2[:, 1:2])
                    E.g(e.tensor_scalar(out=sq_scr[:], in0=sq_scr[:], scalar1=stm[:, 1:2],
                                          scalar2=None, op0=ALU.mult), "v")
                    E.inc(e.tensor_tensor(out=sq_scr2[:, 0:512], in0=sq_scr[:, 0:512], in1=pC[:], op=ALU.mult), "v")
                    E.inc(e.tensor_tensor(out=sq_scr2[:, 512:1024], in0=sq_scr[:, 512:1024], in1=pD[:], op=ALU.mult), "v")
                ctx.sec("v", fv_e3)

                def fs_at(e, E, nt=nt):
                    E.w("v")
                    E.inc(e.dma_start(out=sp_at[:, nt, :], in_=sq_scr2[:]), "dS", 16)
                    if (dat2 := dump_req("at", [128, NT, D])) is not None:
                        E.inc(e.dma_start(out=dat2[:, nt, :], in_=sq_scr2[:]), "dS", 16)
                ctx.sec("s", fs_at)

                def fv_g(e, E):
                    E.w("dS")
                ctx.sec("v", fv_g)

    if build_upto >= 4:
        rglru_layer(0, hX)
        collective(stage32[:, 8:32], recv32[:, 8:32], cchi[0], ccho[0])
        def fv_h1(e, E):
            E.w("dG")
            E.g(e.tensor_scalar(out=recv32[:, 8:32], in0=recv32[:, 8:32],
                                  scalar1=flagR[:, 0:1], scalar2=None, op0=ALU.mult), "v")
            E.g(e.tensor_copy(out=hX[:, :, 0:3],
                                in_=recv32[:, 8:32].rearrange("p (j t) -> p j t", j=NJ)), "v")
        ctx.sec("v", fv_h1)
        if (dh0 := dump_req("h0", [128, NJ, 3 + T], BF16)) is not None:
            def fdh0(e, E, dh0=dh0):
                E.w("v")
                E.inc(e.dma_start(out=dh0, in_=hX[:]), "dS", 16)
            ctx.sec("s", fdh0)
        rglru_layer(1, hX)
        collective(stage32[:, 8:32], recv32[:, 8:32], cchi[1], ccho[1])
        def fv_h2(e, E):
            E.w("dG")
            E.g(e.tensor_scalar(out=recv32[:, 8:32], in0=recv32[:, 8:32],
                                  scalar1=flagR[:, 0:1], scalar2=None, op0=ALU.mult), "v")
            E.g(e.tensor_copy(out=hX[:, :, 0:3],
                                in_=recv32[:, 8:32].rearrange("p (j t) -> p j t", j=NJ)), "v")
        ctx.sec("v", fv_h2)
        if (dh1 := dump_req("h1", [128, NJ, 3 + T], BF16)) is not None:
            def fdh1(e, E, dh1=dh1):
                E.w("v")
                E.inc(e.dma_start(out=dh1, in_=hX[:]), "dS", 16)
            ctx.sec("s", fdh1)
        rglru_layer(2, hX)

    if build_upto < 5:
        return finish(None)

    for g in (g_hbf, g_clb, g_ib, g_rb, g_xbfB, g_xbfA, g_xcv, g_wibr, g_wrbr):
        g.__exit__(None, None, None)
    g_hX.__exit__(None, None, None)
    ACC, g_ACC = mk("ACC", [128, NT, D])
    gg = []
    def mkg(name, shape, dtype=F32):
        t, g = mk(name, shape, dtype, side="right")
        gg.append(g)
        return t
    gx_s = mkg("gx_s", [64, 3072])
    gh_s = mkg("gh_s", [64, 3072])
    giT_bf = mkg("giT_bf", [128, 8, 64], BF16)
    hnew = mkg("hnew", [64, D])
    hnT_bf = mkg("hnT_bf", [128, 8, 64], BF16)
    svals = mkg("svals", [64, D])
    svals_bf = mkg("svals_bf", [64, D], BF16)
    scr64 = mkg("scr64", [64, D])
    scr64b = mkg("scr64b", [64, D])
    brow = mkg("brow", [1, 3072])
    bvrow_s = brow[0:1, 0:D]
    borow_s = brow[0:1, D:2 * D]
    out_bf = mkg("out_bf", [128, NJ, TSF], BF16)

    def fs_e(e, E):
        E.inc(e.dma_start(out=brow[:], in_=bih_in), "dS", 16)
    ctx.sec("s", fs_e)

    def fv_gi(e, E):
        E.w("dG"); E.w("dS")
        E.g(e.tensor_scalar(out=si_full[:, 1024:1025], in0=si_full[:, 1024:1025],
                              scalar1=1e-8, scalar2=None, op0=ALU.add), "v")
        E.g(e.reciprocal(out=scr64[:, 0:1], in_=si_full[:, 1024:1025]), "v")
        E.g(e.tensor_scalar(out=scr64b[:], in0=si_full[:, 0:1024],
                              scalar1=scr64[:, 0:1], scalar2=None, op0=ALU.mult), "v")
        E.g(e.tensor_copy(out=svals_bf[:], in_=scr64b[:]), "v")
    ctx.sec("v", fv_gi)

    def fp_gi(e, E):
        E.w("v")
        for j in range(NJ):
            ins = e.transpose(out=pT[:, j * 64:(j + 1) * 64],
                              in_=svals_bf[:, j * 128:(j + 1) * 128],
                              identity=ident_b[0:64, 0:64])
        E.inc(ins, "p")
    ctx.sec("p", fp_gi)

    def fv_gi2(e, E):
        E.w("p")
        E.inc(e.tensor_copy(out=giT_bf[:], in_=pT[:, 0:512].rearrange("p (j s) -> p j s", j=NJ)), "v")
    ctx.sec("v", fv_gi2)

    guse = {}
    gld = {}
    pieces = []
    for gri, (lhsT_src, dst, win) in enumerate(((giT_bf, gx_s, wihT_in),
                                                (SvT_bf, gh_s, whhT_in))):
        for part in range(3):
            pieces.append((gri, part, lhsT_src, dst, win))
    for idx, (gri, part, lhsT_src, dst, win) in enumerate(pieces):
        wbuf = w1p if idx % 2 == 0 else w2p
        if gri == 1 and part == 0:
            def fs_bhh(e, E):
                psn = ctx.cnt["p"]
                E.w("p", psn)
                E.inc(e.dma_start(out=brow[:], in_=bhh_in), "dS", 16)
            ctx.sec("s", fs_bhh)
        def fg_l(e, E, idx=idx, part=part, win=win, wbuf=wbuf):
            E.w("p", guse.get(idx - 2, ctx.cnt["p"] if idx < 2 else 0))
            E.inc(e.dma_start(out=wbuf[:], in_=win[:, :, part * 1024:(part + 1) * 1024]), "dG", 16)
        ctx.sec("g", fg_l)
        gld[idx] = ctx.cnt["dG"]
        for half in range(2):
            def fp_g(e, E, idx=idx, part=part, half=half, lhsT_src=lhsT_src,
                     wbuf=wbuf, gri=gri):
                E.w("dG", gld[idx])
                E.w("v"); E.w("c")
                off = part * 1024 + half * 512
                for k in range(8):
                    e.matmul(out=pD[0:64, 0:512],
                             lhsT=lhsT_src[:, k, :],
                             rhs=wbuf[:, k, half * 512:(half + 1) * 512],
                             start=(k == 0), stop=False)
                ins = e.matmul(out=pD[0:64, 0:512], lhsT=ones_f[:, 0:64],
                               rhs=brow[:, off:off + 512], start=False, stop=True)
                E.inc(ins, "p")
            ctx.sec("p", fp_g)
            pg = ctx.cnt["p"]
            def fv_g2(e, E, part=part, half=half, dst=dst, pg=pg):
                E.w("p", pg)
                off = part * 1024 + half * 512
                E.inc(e.tensor_copy(out=dst[:, off:off + 512], in_=pD[0:64, 0:512]), "v")
            ctx.sec("v", fv_g2)
        guse[idx] = ctx.cnt["p"]

    def fv_gt1(e, E):
        E.w("v")
        E.inc(e.tensor_tensor(out=scr64[:], in0=gx_s[:, 0:1024], in1=gh_s[:, 0:1024], op=ALU.add), "v")
    ctx.sec("v", fv_gt1)
    def fc_g1(e, E):
        E.w("v")
        E.inc(e.activation(out=scr64[:], in_=scr64[:], func=AF.Sigmoid), "c")
    ctx.sec("c", fc_g1)
    def fv_gt2(e, E):
        E.w("c")
        E.inc(e.tensor_tensor(out=scr64b[:], in0=gh_s[:, 2048:3072], in1=scr64[:], op=ALU.mult), "v")
        E.inc(e.tensor_tensor(out=scr64b[:], in0=scr64b[:], in1=gx_s[:, 2048:3072], op=ALU.add), "v")
    ctx.sec("v", fv_gt2)
    def fc_g2(e, E):
        E.w("v")
        E.inc(e.activation(out=scr64b[:], in_=scr64b[:], func=AF.Tanh), "c")
    ctx.sec("c", fc_g2)
    def fv_gt3(e, E):
        E.w("c")
        E.inc(e.tensor_tensor(out=scr64[:], in0=gx_s[:, 1024:2048], in1=gh_s[:, 1024:2048], op=ALU.add), "v")
    ctx.sec("v", fv_gt3)
    def fc_g3(e, E):
        E.w("v")
        E.inc(e.activation(out=scr64[:], in_=scr64[:], func=AF.Sigmoid), "c")
    ctx.sec("c", fc_g3)
    def fv_gt4(e, E):
        E.w("c")
        E.inc(e.tensor_tensor(out=hnew[:], in0=Sv[:], in1=scr64b[:], op=ALU.subtract), "v")
        E.inc(e.tensor_tensor(out=hnew[:], in0=hnew[:], in1=scr64[:], op=ALU.mult), "v")
        E.inc(e.tensor_tensor(out=hnew[:], in0=hnew[:], in1=scr64b[:], op=ALU.add), "v")
        E.inc(e.tensor_copy(out=svals_bf[:], in_=hnew[:]), "v")
    ctx.sec("v", fv_gt4)

    def fp_hn(e, E):
        E.w("v")
        for j in range(NJ):
            ins = e.transpose(out=pT[:, j * 64:(j + 1) * 64],
                              in_=svals_bf[:, j * 128:(j + 1) * 128],
                              identity=ident_b[0:64, 0:64])
        E.inc(ins, "p")
    ctx.sec("p", fp_hn)
    def fv_hn(e, E):
        E.w("p")
        E.inc(e.tensor_copy(out=hnT_bf[:], in_=pT[:, 0:512].rearrange("p (j s) -> p j s", j=NJ)), "v")
    ctx.sec("v", fv_hn)

    def fg_wv(e, E):
        psn = ctx.cnt["p"]
        E.w("p", psn)
        E.inc(e.dma_start(out=w1p[:], in_=wvTf_in), "dG", 16)
        E.inc(e.dma_start(out=w2p[:], in_=wfoT_in), "dG", 16)
        E.inc(e.dma_start(out=bvrow_s, in_=bv_in), "dG", 16)
        E.inc(e.dma_start(out=borow_s, in_=bo_in), "dG", 16)
    ctx.sec("g", fg_wv)

    for half in range(2):
        def fp_sv(e, E, half=half):
            E.w("dG"); E.w("v"); E.w("c")
            for k in range(8):
                e.matmul(out=pD[0:64, 0:512], lhsT=hnT_bf[:, k, :],
                         rhs=w1p[:, k, half * 512:(half + 1) * 512],
                         start=(k == 0), stop=False)
            ins = e.matmul(out=pD[0:64, 0:512], lhsT=ones_f[:, 0:64],
                           rhs=bvrow_s[:, half * 512:(half + 1) * 512],
                           start=False, stop=True)
            E.inc(ins, "p")
        ctx.sec("p", fp_sv)
        def fv_sv2(e, E, half=half):
            E.w("p")
            E.inc(e.tensor_copy(out=svals[:, half * 512:(half + 1) * 512], in_=pD[0:64, 0:512]), "v")
        ctx.sec("v", fv_sv2)
    def fv_svb(e, E):
        E.w("v")
        E.inc(e.tensor_copy(out=svals_bf[:], in_=svals[:]), "v")
    ctx.sec("v", fv_svb)

    for c4 in range(4):
        for m in range(NJ):
            ps = pA if m % 2 == 0 else pB
            def fp_oe(e, E, c4=c4, m=m, ps=ps):
                E.w("v"); E.w("c")
                E.inc(e.matmul(out=ps[:], lhsT=svals_bf[:, m * 128:(m + 1) * 128],
                               rhs=alpha_sm[:, c4 * 512:(c4 + 1) * 512],
                               start=True, stop=True), "p")
            ctx.sec("p", fp_oe)
            poe = ctx.cnt["p"]
            def fv_oe(e, E, c4=c4, m=m, ps=ps, poe=poe):
                E.w("p", poe)
                E.inc(e.tensor_copy(out=out_bf[:, m, :], in_=ps[:]), "v")
            ctx.sec("v", fv_oe)
        for sub in range(4):
            nt = c4 * 4 + sub
            def fp_fo(e, E, sub=sub, nt=nt):
                E.w("v"); E.w("dG")
                for half in range(2):
                    for k in range(NJ):
                        e.matmul(out=pE[:, half * 512:(half + 1) * 512],
                                 lhsT=out_bf[:, k, sub * 128:(sub + 1) * 128],
                                 rhs=w2p[:, k, half * 512:(half + 1) * 512],
                                 start=(k == 0), stop=False)
                    ins = e.matmul(out=pE[:, half * 512:(half + 1) * 512],
                                   lhsT=ones_f[:], rhs=borow_s[:, half * 512:(half + 1) * 512],
                                   start=False, stop=True)
                E.inc(ins, "p")
            ctx.sec("p", fp_fo)

            def fs_x(e, E, nt=nt):
                E.w("v")
                E.inc(e.dma_start(out=sq_scr[:], in_=sp_at[:, nt, :]), "dS", 16)
                E.inc(e.dma_start(out=sq_scr2[:], in_=x_own.rearrange("(n p) d -> p n d", p=128)[:, nt, :]), "dS", 16)
            ctx.sec("s", fs_x)

            def fv_x1(e, E, nt=nt):
                E.w("p"); E.w("dS")
                E.inc(e.tensor_tensor(out=ACC[:, nt, :], in0=pE[:, 0:1024], in1=sq_scr[:], op=ALU.add), "v")
                E.inc(e.tensor_tensor(out=ACC[:, nt, :], in0=ACC[:, nt, :], in1=sq_scr2[:], op=ALU.add), "v")
            ctx.sec("v", fv_x1)

    for g in reversed(gg):
        g.__exit__(None, None, None)

    if (d6 := dump_req("x1", [128, NT, D])) is not None:
        def fd6(e, E, d6=d6):
            E.w("v")
            E.inc(e.dma_start(out=d6, in_=ACC[:]), "dS", 16)
        ctx.sec("s", fd6)

    if build_upto < 6:
        return finish(ACC[:])

    xq_fm, g_xqf = mk("xq_fm", [128, NJ, T], FP8, side="right")
    q_op, g_qop = mk("q_op", [128, NJ, T], BF16, side="right")
    xbt, g_xbt = mk("xbt", [128, D], BF16, side="right")
    xqt, g_xqt = mk("xqt", [128, D], FP8, side="right")
    xct, g_xct = mk("xct", [128, D], BF16, side="right")

    def f2_red(e, E):
        E.w("v")
        for n in range(NT):
            E.inc(e.reduce_sum(out=st1[:, n:n + 1], in_=ACC[:, n, :], axis=AX), "v")
    ctx.sec("v", f2_red)
    def f2_sq(e, E):
        E.w("v")
        for n in range(NT):
            E.inc(e.activation(out=sq_scr[:], in_=ACC[:, n, :], func=AF.Square,
                               accum_out=st2[:, n:n + 1]), "c")
    ctx.sec("c", f2_sq)
    def f2_mv(e, E):
        E.w("c")
        E.g(e.tensor_scalar(out=stm[:], in0=st1[:], scalar1=1.0 / 1024.0,
                              scalar2=None, op0=ALU.mult), "v")
        E.g(e.tensor_scalar(out=strv[:], in0=st2[:], scalar1=1.0 / 1024.0,
                              scalar2=None, op0=ALU.mult), "v")
        E.g(e.tensor_tensor(out=scr16[:], in0=stm[:], in1=stm[:], op=ALU.mult), "v")
        E.g(e.tensor_tensor(out=strv[:], in0=strv[:], in1=scr16[:], op=ALU.subtract), "v")
        E.g(e.tensor_scalar(out=strv[:], in0=strv[:], scalar1=1e-5,
                              scalar2=None, op0=ALU.add), "v")
    ctx.sec("v", f2_mv)
    def f2_sqrt(e, E):
        E.w("v")
        E.inc(e.activation(out=scr16[:], in_=strv[:], func=AF.Sqrt), "c")
    ctx.sec("c", f2_sqrt)
    def f2_fin(e, E):
        E.w("c")
        newton_rsqrt(e, E, strv[:], st2[:], scr16[:])
        E.g(e.tensor_copy(out=strv[:], in_=st2[:]), "v")
        E.g(e.tensor_tensor(out=stmr[:], in0=stm[:], in1=strv[:], op=ALU.mult), "v")
    ctx.sec("v", f2_fin)

    ap_end = {}
    t1_end = {}
    t2_end = {}
    v1_end = {}
    v2_end = {}
    for n in range(NT):
        def f2_ap(e, E, n=n):
            if n == 0:
                E.w("v"); E.w("p")
            else:
                E.w("p", t2_end[n - 1])
            E.inc(e.tensor_scalar(out=sq_scr[:], in0=ACC[:, n, :],
                                  scalar1=strv[:, n:n + 1], scalar2=stmr[:, n:n + 1],
                                  op0=ALU.mult, op1=ALU.subtract), "v")
            E.inc(e.tensor_copy(out=xbt[:], in_=sq_scr[:]), "v")
            E.g(e.tensor_copy(out=xqt[:], in_=sq_scr[:]), "v")
            E.inc(e.tensor_copy(out=xct[:], in_=xqt[:]), "v")
        ctx.sec("v", f2_ap)
        ap_end[n] = ctx.cnt["v"]
        def f2_t1(e, E, n=n):
            E.w("v", ap_end[n])
            if n >= 1:
                E.w("v", v1_end[n - 1])
            for j in range(NJ):
                ins = e.transpose(out=pT[:, j * 128:(j + 1) * 128],
                                  in_=xbt[:, j * 128:(j + 1) * 128],
                                  identity=ident_b[:])
            E.inc(ins, "p")
        ctx.sec("p", f2_t1)
        t1_end[n] = ctx.cnt["p"]
        def f2_t2(e, E, n=n):
            E.w("v", ap_end[n])
            if n >= 1:
                E.w("v", v2_end[n - 1])
            for j in range(NJ):
                ins = e.transpose(out=pT2[:, j * 128:(j + 1) * 128],
                                  in_=xct[:, j * 128:(j + 1) * 128],
                                  identity=ident_b[:])
            E.inc(ins, "p")
        ctx.sec("p", f2_t2)
        t2_end[n] = ctx.cnt["p"]
        def f2_v1(e, E, n=n):
            E.w("p", t1_end[n])
            E.inc(e.tensor_copy(out=q_op[:, :, n * 128:(n + 1) * 128],
                                in_=pT[:].rearrange("p (j t) -> p j t", j=NJ)), "v")
        ctx.sec("v", f2_v1)
        v1_end[n] = ctx.cnt["v"]
        def f2_v2(e, E, n=n):
            E.w("p", t2_end[n])
            E.inc(e.tensor_copy(out=xq_fm[:, :, n * 128:(n + 1) * 128],
                                in_=pT2[:].rearrange("p (j t) -> p j t", j=NJ)), "v")
        ctx.sec("v", f2_v2)
        v2_end[n] = ctx.cnt["v"]

    def fs_spx(e, E):
        E.w("v")
        E.inc(e.dma_start(out=sp_x1[:], in_=ACC[:]), "dS", 16)
        if (dq := dump_req("qop", [128, NJ, T], BF16)) is not None:
            E.inc(e.dma_start(out=dq, in_=q_op[:]), "dS", 16)
        if (dx8 := dump_req("xq", [128, NJ, T], FP8)) is not None:
            E.inc(e.dma_start(out=dx8, in_=xq_fm[:]), "dS", 16)
    ctx.sec("s", fs_spx)

    if build_upto < 7:
        return finish(ACC[:])

    g_xct.__exit__(None, None, None)
    g_xqt.__exit__(None, None, None)
    g_xbt.__exit__(None, None, None)
    g_ACC.__exit__(None, None, None)
    for g in (g_si, g_asm, g_atm, g_ssc, g_SvT, g_Sv):
        g.__exit__(None, None, None)

    TSG, NSG = 512, 4
    gb1, g_gb1 = mk("sb_gb1", [128, NJ, TSG], F32, side="right")
    gb2, g_gb2 = mk("sb_gb2", [128, NJ, TSG], F32, side="right")
    gb3, g_gb3 = mk("sb_gb3", [128, NJ, TSG], F32, side="right")
    gb4, g_gb4 = mk("sb_gb4", [128, NJ, TSG], F32, side="right")
    dcf, g_dcf = mk("sb_dcf", [128, NJ, TSG], F32, side="right")
    w3p, g_w3p = mk("sb_w3p", [128, 8, D], BF16, side="right")
    GFL = lambda b: b[:].rearrange("p j t -> p (j t)")
    dsc, car2, car3, d512 = ccl, ccprev, hcar, nccst

    def fc_d(e, E):
        E.w("dS")
        E.inc(e.activation(out=dp_s[:, 1:2], in_=dp_s[:, 0:1], func=AF.Sigmoid), "c")
    ctx.sec("c", fc_d)
    def fv_d(e, E):
        E.w("c")
        E.g(e.tensor_scalar(out=dp_s[:, 2:3], in0=dp_s[:, 1:2], scalar1=1e-8,
                              scalar2=None, op0=ALU.max), "v")
    ctx.sec("v", fv_d)
    def fc_d2(e, E):
        E.w("v")
        E.inc(e.activation(out=dp_s[:, 2:3], in_=dp_s[:, 2:3], func=AF.Ln), "c")
    ctx.sec("c", fc_d2)
    def fp_d(e, E):
        E.w("c")
        for j in range(NJ):
            ins = e.matmul(out=pD[:, j:j + 1], lhsT=GTt[:, j * 128:(j + 1) * 128],
                           rhs=dp_s[:, 2:3], start=True, stop=True)
        E.inc(ins, "p")
    ctx.sec("p", fp_d)
    def fv_d2(e, E):
        E.w("p")
        E.g(e.tensor_copy(out=logd[:], in_=pD[:, 0:8]), "v")
    ctx.sec("v", fv_d2)
    def fc_d3(e, E):
        E.w("v")
        E.inc(e.activation(out=stage16[:, 0:8], in_=logd[:], func=AF.Exp), "c")
    ctx.sec("c", fc_d3)
    def fv_d3(e, E):
        E.w("c")
        for j in range(NJ):
            ins = e.tensor_scalar(out=dcf[:, j, :], in0=ones_ch[:, 0:TSG],
                                  scalar1=stage16[:, j:j + 1], scalar2=None, op0=ALU.mult)
        E.g(ins, "v")
        for j in range(NJ):
            ins = e.tensor_tensor_scan(out=gb1[:, j, :], data0=dcf[:, j, :],
                                       data1=ones_ch[:, 0:TSG], initial=consts[:, 13:14],
                                       op0=ALU.mult, op1=ALU.mult)
        E.g(ins, "v")
        E.g(e.tensor_copy(out=d512[:], in_=gb1[:, :, TSG - 1]), "v")
        E.g(e.memset(dsc[:], 1.0), "v")
        E.g(e.memset(kvcar[:], 0.0), "v")
        E.g(e.memset(kscar[:], 0.0), "v")
    ctx.sec("v", fv_d3)
    vpw = ctx.cnt["v"]
    def fs_pw(e, E, vpw=vpw):
        E.w("v", vpw)
        E.inc(e.dma_start(out=sp_cloc[:, :, 0:TSG], in_=gb1[:]), "dS", 16)
    ctx.sec("s", fs_pw)

    psn_g = ctx.cnt["p"]
    def fg_q(e, E, psn_g=psn_g):
        E.w("p", psn_g)
        E.inc(e.dma_start(out=w1p[:], in_=wqT_in), "dG", 16)
        E.inc(e.dma_start(out=w2p[:], in_=wkT_in), "dG", 16)
        E.inc(e.dma_start(out=w3p[:], in_=wvT1_in), "dG", 16)
    ctx.sec("g", fg_q)
    dg_q = ctx.cnt["dG"]

    gmm_end = {}
    gspl_end = {}
    for s in range(NSG):
        c0 = s * TSG
        evq = {}
        for gate, wp in ((0, w1p), (1, w2p), (2, w3p)):
            for m in range(NJ):
                ps = (pA, pB, pC, pD, pE, pE)[gate * 2 + (m % 2)]
                pslice = ps[:, 0:TSG] if gate < 2 else (
                    pE[:, 0:TSG] if m % 2 == 0 else pE[:, TSG:2 * TSG])
                vwait = evq.get((gate, m - 2), gmm_end.get((s - 1, 'v'), 0))
                def fp_g1(e, E, s=s, m=m, pslice=pslice, wp=wp, c0=c0, vwait=vwait):
                    E.w("dG", dg_q)
                    E.w("v", vwait)
                    for k in range(NJ):
                        ins = e.matmul(out=pslice, lhsT=wp[:, k, m * 128:(m + 1) * 128],
                                       rhs=q_op[:, k, c0:c0 + TSG], start=(k == 0), stop=(k == NJ - 1))
                    E.inc(ins, "p")
                ctx.sec("p", fp_g1)
                pm = ctx.cnt["p"]
                def fv_gev(e, E, s=s, m=m, pslice=pslice, pm=pm, gate=gate):
                    E.w("p", pm)
                    if gate == 0 and m == 0:
                        if s >= 1:
                            E.w("dG", gspl_end[s - 1])
                        else:
                            E.w("dS")
                    if gate == 2:
                        E.inc(e.tensor_copy(out=gb3[:, m, :], in_=pslice), "v")
                    else:
                        dst = gb1 if gate == 0 else gb2
                        E.inc(e.tensor_scalar(out=dst[:, m, :], in0=pslice, scalar1=0.0,
                                              scalar2=None, op0=ALU.max), "v")
                        E.inc(e.tensor_scalar(out=gb4[:, m, :], in0=pslice, scalar1=0.0,
                                              scalar2=None, op0=ALU.min), "v")
                ctx.sec("v", fv_gev)
                evq[(gate, m)] = ctx.cnt["v"]
            if gate < 2:
                vev = ctx.cnt["v"]
                def fc_phi(e, E, vev=vev):
                    E.w("v", vev)
                    E.inc(e.activation(out=GFL(gb4), in_=GFL(gb4), func=AF.Exp), "c")
                ctx.sec("c", fc_phi)
                cph = ctx.cnt["c"]
                def fv_phi(e, E, gate=gate, cph=cph):
                    E.w("c", cph)
                    dst = gb1 if gate == 0 else gb2
                    E.inc(e.tensor_tensor(out=GFL(dst), in0=GFL(dst), in1=GFL(gb4), op=ALU.add), "v")
                ctx.sec("v", fv_phi)
        gmm_end[(s, 'v')] = ctx.cnt["v"]

        def fv_gs(e, E, s=s, c0=c0):
            E.w("v")
            for j in range(NJ):
                E.inc(e.tensor_copy(out=q_op[:, j, c0:c0 + TSG], in_=gb1[:, j, :]), "v")
            E.w("v")
            E.inc(e.tensor_tensor(out=GFL(gb4), in0=GFL(gb2), in1=GFL(gb3), op=ALU.mult), "v")
            E.w("v")
            for j in range(NJ):
                ins = e.tensor_tensor_scan(out=gb1[:, j, :], data0=dcf[:, j, :],
                                           data1=gb4[:, j, :], initial=kvcar[:, j:j + 1],
                                           op0=ALU.mult, op1=ALU.add)
            E.g(ins, "v")
            E.g(e.tensor_copy(out=kvcar[:], in_=gb1[:, :, TSG - 1]), "v")
            for j in range(NJ):
                ins = e.tensor_tensor_scan(out=gb4[:, j, :], data0=dcf[:, j, :],
                                           data1=gb2[:, j, :], initial=kscar[:, j:j + 1],
                                           op0=ALU.mult, op1=ALU.add)
            E.g(ins, "v")
            E.g(e.tensor_copy(out=kscar[:], in_=gb4[:, :, TSG - 1]), "v")
        ctx.sec("v", fv_gs)
        vsp = ctx.cnt["v"]
        def fg_sp2(e, E, s=s, c0=c0, vsp=vsp):
            E.w("v", vsp)
            E.inc(e.dma_start(out=sp_v[:, :, c0:c0 + TSG], in_=gb3[:]), "dG", 16)
            E.inc(e.dma_start(out=sp_kv[:, :, c0:c0 + TSG], in_=gb1[:]), "dG", 16)
            E.inc(e.dma_start(out=sp_ks[:, :, c0:c0 + TSG], in_=gb4[:]), "dG", 16)
        ctx.sec("g", fg_sp2)
        gspl_end[s] = ctx.cnt["dG"]

    def fv_s3(e, E):
        E.w("v")
        E.g(e.tensor_scalar(out=stage16[:, 0:8], in0=kvcar[:], scalar1=flagS[:, 0:1],
                              scalar2=None, op0=ALU.mult), "v")
        E.g(e.tensor_scalar(out=stage16[:, 8:16], in0=kscar[:], scalar1=flagS[:, 0:1],
                              scalar2=None, op0=ALU.mult), "v")
    ctx.sec("v", fv_s3)
    collective(stage16[:], recv16[:], ccl1i, ccl1o)
    def fv_r3(e, E):
        E.w("dG")
        E.g(e.tensor_scalar(out=kvcar[:], in0=recv16[:, 0:8], scalar1=flagR[:, 0:1],
                              scalar2=None, op0=ALU.mult), "v")
        E.g(e.tensor_scalar(out=kscar[:], in0=recv16[:, 8:16], scalar1=flagR[:, 0:1],
                              scalar2=None, op0=ALU.mult), "v")
    ctx.sec("v", fv_r3)

    psn_g2 = ctx.cnt["p"]
    def fg_g(e, E, psn_g2=psn_g2):
        E.w("p", psn_g2)
        E.inc(e.dma_start(out=w1p[:], in_=wgT_in), "dG", 16)
        E.inc(e.dma_start(out=w2p[:], in_=woutT_in), "dG", 16)
    ctx.sec("g", fg_g)
    dg_g = ctx.cnt["dG"]

    gld_end = {}
    gpo_end = {}
    for s in range(NSG):
        c0 = s * TSG
        def fg_l3(e, E, s=s, c0=c0):
            if s == 0:
                E.w("v")
                E.w("dG")
            else:
                E.w("v", gpo_end.get((s - 1, 'kv'), 0))
            E.inc(e.dma_start(out=gb2[:], in_=sp_kv[:, :, c0:c0 + TSG]), "dG", 16)
            E.inc(e.dma_start(out=gb3[:], in_=sp_ks[:, :, c0:c0 + TSG]), "dG", 16)
        ctx.sec("g", fg_l3)
        gld_end[s] = ctx.cnt["dG"]

        def fs_pwl(e, E, s=s):
            if s == 0:
                E.w("dG")
                E.w("v")
            else:
                E.w("v", gpo_end.get((s - 1, 'pw'), 0))
            E.inc(e.dma_start(out=gb1[:], in_=sp_cloc[:, :, 0:TSG]), "dS", 16)
        ctx.sec("s", fs_pwl)
        pwl = ctx.cnt["dS"]

        def fv_gc(e, E, s=s, pwl=pwl):
            E.w("dG", gld_end[s])
            E.w("dS", pwl)
            E.w("c")
            E.g(e.tensor_tensor(out=car2[:], in0=dsc[:], in1=kvcar[:], op=ALU.mult), "v")
            E.g(e.tensor_tensor(out=car3[:], in0=dsc[:], in1=kscar[:], op=ALU.mult), "v")
            for j in range(NJ):
                e.scalar_tensor_tensor(out=gb2[:, j, :], in0=gb1[:, j, :],
                                       scalar=car2[:, j:j + 1], in1=gb2[:, j, :],
                                       op0=ALU.mult, op1=ALU.add)
                ins = e.scalar_tensor_tensor(out=gb3[:, j, :], in0=gb1[:, j, :],
                                             scalar=car3[:, j:j + 1], in1=gb3[:, j, :],
                                             op0=ALU.mult, op1=ALU.add)
            E.inc(ins, "v")
            E.g(e.tensor_tensor(out=dsc[:], in0=dsc[:], in1=d512[:], op=ALU.mult), "v")
            for j in range(NJ):
                ins = e.tensor_tensor(out=gb4[:, j, :], in0=q_op[:, j, s * TSG:(s + 1) * TSG],
                                      in1=gb3[:, j, :], op=ALU.mult)
            E.inc(ins, "v")
        ctx.sec("v", fv_gc)
        vgc = ctx.cnt["v"]

        def fp_den(e, E, vgc=vgc):
            E.w("v", vgc)
            for j in range(NJ):
                ins = e.matmul(out=pC[0:16, 0:TSG], lhsT=Gt[:, j * 16:(j + 1) * 16],
                               rhs=gb4[:, j, :], start=(j == 0), stop=(j == NJ - 1))
            E.inc(ins, "p")
        ctx.sec("p", fp_den)
        pden = ctx.cnt["p"]

        def fv_den(e, E, pden=pden):
            E.w("p", pden)
            E.g(e.tensor_scalar(out=sq_scr[0:16, 0:TSG], in0=pC[0:16, 0:TSG],
                                  scalar1=1e-6, scalar2=None, op0=ALU.max), "v")
            E.g(e.reciprocal(out=sq_scr2[0:16, 0:TSG], in_=sq_scr[0:16, 0:TSG]), "v")
        ctx.sec("v", fv_den)
        vden = ctx.cnt["v"]

        for j in range(NJ):
            def fp_rb(e, E, j=j, vden=vden):
                E.w("v", vden)
                ps = pA if j % 2 == 0 else pB
                E.inc(e.matmul(out=ps[:, 0:TSG], lhsT=GTt[:, j * 128:(j + 1) * 128],
                               rhs=sq_scr2[0:16, 0:TSG], start=True, stop=True), "p")
            ctx.sec("p", fp_rb)
            prb = ctx.cnt["p"]
            def fv_rb(e, E, s=s, j=j, prb=prb):
                E.w("p", prb)
                ps = pA if j % 2 == 0 else pB
                e.tensor_tensor(out=gb4[:, j, :], in0=q_op[:, j, s * TSG:(s + 1) * TSG],
                                in1=gb2[:, j, :], op=ALU.mult)
                E.inc(e.tensor_tensor(out=gb4[:, j, :], in0=gb4[:, j, :],
                                      in1=ps[:, 0:TSG], op=ALU.mult), "v")
            ctx.sec("v", fv_rb)

        def fv_ocast(e, E, s=s, c0=c0):
            E.w("v")
            for j in range(NJ):
                E.inc(e.tensor_copy(out=q_op[:, j, c0:c0 + TSG], in_=gb4[:, j, :]), "v")
        ctx.sec("v", fv_ocast)
        voc = ctx.cnt["v"]

        evg = {}
        for m in range(NJ):
            ps = pA if m % 2 == 0 else pB
            vwait = evg.get(m - 2, voc)
            def fp_g2(e, E, s=s, m=m, ps=ps, c0=c0, voc=voc, vwait=vwait):
                E.w("v", voc)
                E.w("v", vwait)
                E.w("dG", dg_g)
                for k in range(NJ):
                    ins = e.matmul(out=ps[:, 0:TSG], lhsT=w1p[:, k, m * 128:(m + 1) * 128],
                                   rhs=q_op[:, k, c0:c0 + TSG], start=(k == 0), stop=(k == NJ - 1))
                E.inc(ins, "p")
            ctx.sec("p", fp_g2)
            pg2 = ctx.cnt["p"]
            def fv_gg(e, E, m=m, ps=ps, pg2=pg2):
                E.w("p", pg2)
                E.inc(e.tensor_scalar(out=gb1[:, m, :], in0=ps[:, 0:TSG],
                                      scalar1=bg_pp[:, m:m + 1], scalar2=None, op0=ALU.add), "v")
            ctx.sec("v", fv_gg)
            evg[m] = ctx.cnt["v"]
        vgg = ctx.cnt["v"]
        def fc_gs(e, E, vgg=vgg):
            E.w("v", vgg)
            E.inc(e.activation(out=GFL(gb1), in_=GFL(gb1), func=AF.Sigmoid), "c")
        ctx.sec("c", fc_gs)
        cgs = ctx.cnt["c"]

        def fg_lv(e, E, s=s, c0=c0, vgc=vgc):
            E.w("v", vgc)
            E.inc(e.dma_start(out=gb3[:], in_=sp_v[:, :, c0:c0 + TSG]), "dG", 16)
        ctx.sec("g", fg_lv)
        glv = ctx.cnt["dG"]

        def fv_mx(e, E, s=s, c0=c0, cgs=cgs, glv=glv):
            E.w("c", cgs)
            E.w("dG", glv)
            E.inc(e.tensor_tensor(out=GFL(gb4), in0=GFL(gb4), in1=GFL(gb3), op=ALU.subtract), "v")
            E.inc(e.tensor_tensor(out=GFL(gb4), in0=GFL(gb4), in1=GFL(gb1), op=ALU.mult), "v")
            E.inc(e.tensor_tensor(out=GFL(gb4), in0=GFL(gb4), in1=GFL(gb3), op=ALU.add), "v")
            E.w("v")
            for j in range(NJ):
                E.inc(e.tensor_copy(out=q_op[:, j, c0:c0 + TSG], in_=gb4[:, j, :]), "v")
        ctx.sec("v", fv_mx)
        gpo_end[(s, 'kv')] = ctx.cnt["v"]
        gpo_end[(s, 'pw')] = ctx.cnt["v"]

        for sub in range(TSG // 128):
            nt = (c0 + sub * 128) // 128
            def fp_ao(e, E, s=s, sub=sub, c0=c0):
                E.w("v"); E.w("dG", dg_g); E.w("dS")
                for half in range(2):
                    for k in range(NJ):
                        ins = e.matmul(out=pE[:, half * 512:(half + 1) * 512],
                                       lhsT=q_op[:, k, c0 + sub * 128:c0 + (sub + 1) * 128],
                                       rhs=w2p[:, k, half * 512:(half + 1) * 512],
                                       start=(k == 0), stop=(k == NJ - 1))
                E.inc(ins, "p")
            ctx.sec("p", fp_ao)
            pao = ctx.cnt["p"]
            def fv_ao(e, E, pao=pao):
                E.w("p", pao)
                E.inc(e.tensor_copy(out=sq_scr[:], in_=pE[:, 0:1024]), "v")
            ctx.sec("v", fv_ao)
            cao = ctx.cnt["v"]
            def fs_ao(e, E, nt=nt, cao=cao):
                E.w("v", cao)
                E.inc(e.dma_start(out=sp_at[:, nt, :], in_=sq_scr[:]), "dS", 16)
                if (da1 := dump_req("at1", [128, NT, D])) is not None:
                    E.inc(e.dma_start(out=da1[:, nt, :], in_=sq_scr[:]), "dS", 16)
            ctx.sec("s", fs_ao)
            def fc_ao2(e, E):
                E.w("dS")
            ctx.sec("c", fc_ao2)

    for g in (g_w3p, g_dcf, g_gb4, g_gb3, g_gb2, g_gb1):
        g.__exit__(None, None, None)

    if build_upto < 8:
        return finish(None)

    g_qop.__exit__(None, None, None)
    g_w2p.__exit__(None, None, None)
    g_w1p.__exit__(None, None, None)
    W1q, g_W1q = mk("W1q", [128, 8, 4096], FP8)
    W2q, g_W2q = mk("W2q", [128, 32, D], FP8)
    ACC2, g_A2 = mk("ACC2", [128, NT, D])
    b2q_rep, g_b2q = mk("b2q_rep", [128, D])
    b2q_row, g_b2r = mk("b2q_row", [1, D])
    b2q8, g_b28 = mk("b2q8", [1, D], FP8)
    h8F, g_h8 = mk("h8F", [128, 32, TSF], FP8, side="right")
    hstage, g_hst = mk("hstage", [128, TSF], BF16, side="right")
    hstage2, g_hst2 = mk("hstage2", [128, TSF], BF16, side="right")
    mask_st, g_msk = mk("mask_st", [128, 4, D], BF16, side="right")
    sq_bf, g_sqb = mk("sb_sqbf", [128, D], BF16, side="right")

    def fg_f(e, E):
        psn = ctx.cnt["p"]
        E.w("p", psn)
        E.inc(e.dma_start(out=W1q[:], in_=W1fT_in), "dG", 16)
        E.inc(e.dma_start(out=W2q[:], in_=W2fT_in), "dG", 16)
    ctx.sec("g", fg_f)

    def fs_f(e, E):
        E.inc(e.dma_start(out=b2q_row[:], in_=b2f_in), "dS", 16)
    ctx.sec("s", fs_f)

    def fv_bq(e, E):
        E.w("dS")
        E.g(e.tensor_copy(out=b1q8[:], in_=b1f_pp[:]), "v")
        E.g(e.tensor_copy(out=b1fq_pp[:], in_=b1q8[:]), "v")
        E.g(e.tensor_copy(out=b2q8[:], in_=b2q_row[:]), "v")
        E.g(e.tensor_copy(out=b2q_row[:], in_=b2q8[:]), "v")
    ctx.sec("v", fv_bq)
    repl_row(b2q_row, D, b2q_rep)

    dgh = ctx.cnt["dG"]
    vq0 = ctx.cnt["v"]
    ph_end = {}
    rel = {}
    cast = {}
    for cf in range(NCF):
        c0 = cf * TSF
        for m in range(32):
            ps = pA if m % 2 == 0 else pB
            hst = hstage if m % 2 == 0 else hstage2
            def fp_h1(e, E, m=m, ps=ps, c0=c0, cf=cf):
                E.w("dG", dgh)
                E.w("v", vq0)
                E.w("v", rel.get((cf, m - 2), rel.get((cf - 1, m + 30), 0)))
                for k in range(NJ):
                    ins = e.matmul(out=ps[:, 0:TSF], lhsT=W1q[:, k, m * 128:(m + 1) * 128],
                                   rhs=xq_fm[:, k, c0:c0 + TSF], start=(k == 0), stop=(k == NJ - 1))
                E.inc(ins, "p")
            ctx.sec("p", fp_h1)
            pm = ctx.cnt["p"]
            def fv_h1b(e, E, m=m, ps=ps, pm=pm, hst=hst, cf=cf):
                E.w("p", pm)
                E.inc(e.tensor_scalar(out=hst[:], in0=ps[:, 0:TSF],
                                      scalar1=b1fq_pp[:, m:m + 1], scalar2=0.0,
                                      op0=ALU.add, op1=ALU.max), "v")
            ctx.sec("v", fv_h1b)
            rel[(cf, m)] = ctx.cnt["v"]
            def fv_h8(e, E, m=m, cf=cf, hst=hst):
                if m == 0 and cf >= 1:
                    E.w("p", ph_end[cf - 1])
                E.inc(e.tensor_copy(out=h8F[:, m, :], in_=hst[:]), "v")
            ctx.sec("v", fv_h8)
            cast[(cf, m)] = ctx.cnt["v"]
        def fs_m(e, E, cf=cf):
            E.w("v")
            E.inc(e.dma_start(out=mask_st[:],
                              in_=mask_own.rearrange("(n p) d -> p n d", p=128)[:, cf * 4:(cf + 1) * 4, :]), "dS", 16)
        ctx.sec("s", fs_m)
        vh_end = ctx.cnt["v"]
        for sub in range(TSF // 128):
            nt = (c0 + sub * 128) // 128
            def fp_y(e, E, sub=sub, c0=c0, vh_end=vh_end):
                E.w("v")
                for half in range(2):
                    for k in range(32):
                        ins = e.matmul(out=pE[:, half * 512:(half + 1) * 512],
                                       lhsT=h8F[:, k, sub * 128:(sub + 1) * 128],
                                       rhs=W2q[:, k, half * 512:(half + 1) * 512],
                                       start=(k == 0), stop=(k == 31))
                E.inc(ins, "p")
            ctx.sec("p", fp_y)
            py = ctx.cnt["p"]
            def fv_y(e, E, sub=sub, nt=nt, py=py):
                E.w("p", py); E.w("dS")
                E.g(e.tensor_tensor(out=sq_bf[:], in0=pE[:, 0:1024],
                                    in1=b2q_rep[:], op=ALU.add), "v")
                E.inc(e.tensor_tensor(out=ACC2[:, nt, :], in0=sq_bf[:],
                                      in1=mask_st[:, sub, :], op=ALU.mult), "v")
            ctx.sec("v", fv_y)
        ph_end[cf] = ctx.cnt["p"]

    for g in (g_sqb, g_msk, g_hst2, g_hst, g_h8):
        g.__exit__(None, None, None)

    fb1, g_fb1 = mk("sb_fb1", [128, D], F32, side="right")
    fb2, g_fb2 = mk("sb_fb2", [128, D], F32, side="right")
    fadd_end = {}
    fld_end = {}
    for nt in range(NT):
        pa, pb_ = (sq_scr, sq_scr2) if nt % 2 == 0 else (fb1, fb2)
        def fs_f2(e, E, nt=nt, pa=pa, pb_=pb_):
            if nt < 2:
                E.w("v")
            else:
                E.w("v", fadd_end[nt - 2])
            E.inc(e.dma_start(out=pa[:], in_=sp_at[:, nt, :]), "dS", 16)
            E.inc(e.dma_start(out=pb_[:], in_=sp_x1[:, nt, :]), "dS", 16)
        ctx.sec("s", fs_f2)
        fld_end[nt] = ctx.cnt["dS"]
        def fv_f(e, E, nt=nt, pa=pa, pb_=pb_):
            E.w("dS", fld_end[nt])
            E.g(e.tensor_tensor(out=ACC2[:, nt, :], in0=ACC2[:, nt, :], in1=pa[:], op=ALU.add), "v")
            E.inc(e.tensor_tensor(out=ACC2[:, nt, :], in0=ACC2[:, nt, :], in1=pb_[:], op=ALU.add), "v")
        ctx.sec("v", fv_f)
        fadd_end[nt] = ctx.cnt["v"]
        def fc_y(e, E, nt=nt):
            E.w("v", fadd_end[nt])
            E.inc(e.dma_start(out=y_out.rearrange("(n p) d -> p n d", p=128)[:, nt, :],
                              in_=ACC2[:, nt, :]), "dS", 16)
        ctx.sec("c", fc_y)

    return finish(None)


def _host_inputs(inputs):
    import jax
    x = np.asarray(inputs["L0_attn_x"], np.float32)
    with jax.default_device(jax.local_devices(backend="cpu")[0]):
        keep = np.asarray(jax.random.bernoulli(jax.random.key(42, impl="threefry2x32"),
                                               0.8, x.shape))
    maskf = (keep.astype(np.float32) * np.float32(1.25)).astype(BF)

    common = {}
    common["consts_in"] = np.tile(np.asarray(CONSTS, np.float32)[None, :], (128, 1))
    common["ident_b_i"] = np.eye(128, dtype=BF)
    common["ones_f_i"] = np.ones((1, 128), np.float32)
    common["ones_colb_i"] = np.ones((128, 1), BF)
    common["iota_t_i"] = np.tile(np.arange(1, TS1 + 1, dtype=np.float32)[None, :], (128, 1))
    G = np.zeros((128, 8, 16), np.float32)
    GT = np.zeros((16, 8, 128), np.float32)
    for j in range(8):
        for p in range(128):
            hh = (j * 128 + p) // 64
            G[p, j, hh] = 1.0
            GT[hh, j, p] = 1.0
    common["G_in"] = G.reshape(128, 128)
    common["GT_in"] = np.ascontiguousarray(GT.reshape(16, 1024))

    for l in range(3):
        common[f"wrT{l}"] = _wT(inputs["L0_attn_wr_w"][l])
        common[f"wiT{l}"] = _wT(inputs["L0_attn_wi_w"][l])
        common[f"woT{l}"] = _wT(inputs["L0_attn_wo_w"][l])
        cwl = np.asarray(inputs["L0_attn_conv_w"][l], np.float32)
        common[f"cw{l}"] = np.ascontiguousarray(
            cwl.reshape(8, 128, 4).transpose(1, 0, 2).reshape(128, 32))
        common[f"cb{l}"] = _pp(inputs["L0_attn_conv_b"][l])
        common[f"wrb{l}"] = _pp(inputs["L0_attn_wr_b"][l])
        common[f"wib{l}"] = _pp(inputs["L0_attn_wi_b"][l])
        common[f"wrbr{l}"] = np.asarray(inputs["L0_attn_wr_b"][l], np.float32)[None, :]
        common[f"wibr{l}"] = np.asarray(inputs["L0_attn_wi_b"][l], np.float32)[None, :]
        common[f"la{l}"] = _pp(inputs["L0_attn_log_a"][l])
    for l in range(2):
        common[f"nwpp{l}"] = _pp(inputs["L0_attn_norm_w"][l])
    common["nw2row"] = np.asarray(inputs["L0_attn_norm_w"][2], np.float32)[None, :]
    common["nowrow"] = np.asarray(inputs["L0_attn_norm_out_w"], np.float32)[None, :]

    w1 = np.asarray(inputs["L0_ffn_w1"], np.float32)
    common["w1aT"] = _wT(w1[:, :D])
    common["w1bT"] = _wT(w1[:, D:])
    common["b1r"] = _pp(inputs["L0_ffn_b1"])
    common["w2T"] = _wT(inputs["L0_ffn_w2"])
    common["b2r"] = np.asarray(inputs["L0_ffn_b2"], np.float32)[None, :]
    common["slot_init"] = np.asarray(inputs["L0_ffn_slot_init"], np.float32)
    common["slot_scale"] = np.asarray(inputs["L0_ffn_slot_scale"], np.float32)
    common["wihT"] = _wT(inputs["L0_ffn_gru_wih"])
    common["whhT"] = _wT(inputs["L0_ffn_gru_whh"])
    common["bih"] = np.asarray(inputs["L0_ffn_gru_bih"], np.float32)[None, :]
    common["bhh"] = np.asarray(inputs["L0_ffn_gru_bhh"], np.float32)[None, :]
    common["wvTf"] = _wT(inputs["L0_ffn_wv"])
    common["bv"] = np.asarray(inputs["L0_ffn_bv"], np.float32)[None, :]
    common["wfoT"] = _wT(inputs["L0_ffn_wo"])
    common["bo"] = np.asarray(inputs["L0_ffn_bo"], np.float32)[None, :]
    common["tau"] = np.asarray(inputs["L0_ffn_tau"], np.float32).reshape(1, 1)

    wqkv = np.asarray(inputs["L1_attn_Wqkv"], np.float32)
    common["wqT"] = _wT(wqkv[0:D])
    common["wkT"] = _wT(wqkv[D:2 * D])
    common["wvT1"] = _wT(wqkv[2 * D:3 * D])
    common["wgT"] = _wT(inputs["L1_attn_Wgate"])
    common["bg"] = _pp(inputs["L1_attn_bgate"])
    common["woutT"] = _wT(inputs["L1_attn_Wout"])
    common["dp"] = np.asarray(inputs["L1_attn_decay_param"], np.float32).reshape(16, 1)

    common["W1fT"] = _wT(inputs["L1_ffn_W1"])
    common["b1f"] = _pp(inputs["L1_ffn_b1"])
    common["W2fT"] = _wT(inputs["L1_ffn_W2"])
    common["b2f"] = np.asarray(inputs["L1_ffn_b2"], np.float32)[None, :]

    in_maps = []
    for c in range(8):
        b, h = c // 2, c % 2
        m = dict(common)
        t0 = h * T
        m["x_own"] = np.ascontiguousarray(x[b, t0:t0 + T])
        m["mask_own"] = np.ascontiguousarray(maskf[b, t0:t0 + T])
        m["xprev3"] = (np.zeros((3, D), np.float32) if h == 0
                       else np.ascontiguousarray(x[b, t0 - 3:t0]))
        m["flag_send"] = np.full((128, 1), 1.0 if h == 0 else 0.0, np.float32)
        m["flag_recv"] = np.full((128, 1), 1.0 if h == 1 else 0.0, np.float32)
        in_maps.append(m)
    return in_maps


def kernel(**inputs):
    nc, dbg = build_nc()
    in_maps = _host_inputs(inputs)
    res = run_bass_kernel_spmd(nc, in_maps, core_ids=list(range(8)))
    out = np.empty((4, 4096, D), np.float32)
    for c in range(8):
        b, h = c // 2, c % 2
        out[b, h * T:(h + 1) * T] = res.results[c]["y"]
    return out


# version 11 (final)
# speedup vs baseline: 1.0290x; 1.0290x over previous
import numpy as np
import ml_dtypes
import concourse.bass as bass
import concourse.mybir as mybir
from concourse.bass_utils import run_bass_kernel_spmd

F32 = mybir.dt.float32
BF16 = mybir.dt.bfloat16
FP8 = mybir.dt.float8e5
AF = mybir.ActivationFunctionType
ALU = mybir.AluOpType
AX = mybir.AxisListType.X
BF = ml_dtypes.bfloat16

T = 2048
NT = 16
NJ = 8
D = 1024
TSA = 128
NCA = T // TSA
TS1 = 128
NC1 = T // TS1
TSF = 512
NCF = T // TSF
TSD = 512
NSD = T // TSD

BUILD_UPTO = 9
DEBUG_KEYS = ()

RG = [[0, 1], [2, 3], [4, 5], [6, 7]]

CONSTS = [2.0, -1e30, 1e-6, -80.0, 1.5, -0.5, 1e-8, 1.0 / 1024.0, 1.0 / 64.0,
          1e-5, 8.0, 0.1, -1.0, 1.0, 0.0]


def _pp(vec):
    v = np.ascontiguousarray(np.asarray(vec, np.float32)).reshape(-1)
    k = v.shape[0] // 128
    return np.ascontiguousarray(v.reshape(k, 128).T)


def _wT(w):
    wt = np.ascontiguousarray(np.asarray(w, np.float32).T)
    kin, mout = wt.shape
    return np.ascontiguousarray(wt.reshape(kin // 128, 128, mout).transpose(1, 0, 2))


class Ctx:
    ENG = {"v": "vector", "c": "scalar", "p": "tensor", "g": "gpsimd", "s": "sync"}

    def __init__(self, nc, block, sems):
        self.nc, self.block, self.sem = nc, block, sems
        self.cnt = {k: 0 for k in sems}
        self.wcache = {e: {} for e in self.ENG}

    def sec(self, eng, fn):
        ctx = self

        def body(e):
            class E:
                @staticmethod
                def w(prod, val=None):
                    v = ctx.cnt[prod] if val is None else val
                    if v <= 0 or ctx.wcache[eng].get(prod, 0) >= v:
                        return
                    e.wait_ge(ctx.sem[prod], v)
                    ctx.wcache[eng][prod] = v

                @staticmethod
                def inc(instr, key, n=1):
                    instr.then_inc(ctx.sem[key], n)
                    ctx.cnt[key] += n
                    return ctx.cnt[key]

                @staticmethod
                def g(instr, key):
                    instr.then_inc(ctx.sem[key], 1)
                    ctx.cnt[key] += 1
                    e.wait_ge(ctx.sem[key], ctx.cnt[key])
                    ctx.wcache[eng][key] = ctx.cnt[key]
            fn(e, E)
        getattr(self.block, self.ENG[eng])(body)


def build_nc(build_upto=BUILD_UPTO, debug_keys=DEBUG_KEYS):
    nc = bass.Bass("TRN2", target_bir_lowering=False, debug=False)

    def din(name, shape, dtype=F32):
        return nc.dram_tensor(name, list(shape), dtype, kind="ExternalInput").ap()

    def scratch(name, shape, dtype=F32):
        return nc.dram_tensor(name, list(shape), dtype, kind="Internal").ap()

    x_own = din("x_own", [T, D])
    xprev3 = din("xprev3", [3, D])
    mask_own = din("mask_own", [T, D], BF16)
    flag_send = din("flag_send", [128, 1])
    flag_recv = din("flag_recv", [128, 1])
    consts_in = din("consts_in", [128, len(CONSTS)])
    ident_b_in = din("ident_b_i", [128, 128], BF16)
    ones_f_in = din("ones_f_i", [1, 128])
    ones_colb_in = din("ones_colb_i", [128, 1], BF16)
    iota_t_in = din("iota_t_i", [128, TS1])
    G_in = din("G_in", [128, 8 * 16])
    GT_in = din("GT_in", [16, 8 * 128])

    wrT = [din(f"wrT{l}", [128, 8, D]) for l in range(3)]
    wiT = [din(f"wiT{l}", [128, 8, D]) for l in range(3)]
    woT = [din(f"woT{l}", [128, 8, D]) for l in range(3)]
    cw_in = [din(f"cw{l}", [128, 32]) for l in range(3)]
    cb_in = [din(f"cb{l}", [128, 8]) for l in range(3)]
    wrb_in = [din(f"wrb{l}", [128, 8]) for l in range(3)]
    wib_in = [din(f"wib{l}", [128, 8]) for l in range(3)]
    wrbr_in = [din(f"wrbr{l}", [1, D]) for l in range(3)]
    wibr_in = [din(f"wibr{l}", [1, D]) for l in range(3)]
    la_in = [din(f"la{l}", [128, 8]) for l in range(3)]
    nwpp_in = [din(f"nwpp{l}", [128, 8]) for l in range(2)]
    nw2row_in = din("nw2row", [1, D])
    nowrow_in = din("nowrow", [1, D])

    w1aT_in = din("w1aT", [128, 8, 512])
    w1bT_in = din("w1bT", [128, 8, 512])
    b1r_in = din("b1r", [128, 4])
    w2T_in = din("w2T", [128, 4, 64])
    b2r_in = din("b2r", [1, 64])
    slot_init_in = din("slot_init", [64, D])
    slot_scale_in = din("slot_scale", [64, 1])
    wihT_in = din("wihT", [128, 8, 3072])
    whhT_in = din("whhT", [128, 8, 3072])
    bih_in = din("bih", [1, 3072])
    bhh_in = din("bhh", [1, 3072])
    wvTf_in = din("wvTf", [128, 8, D])
    bv_in = din("bv", [1, D])
    wfoT_in = din("wfoT", [128, 8, D])
    bo_in = din("bo", [1, D])
    tau_in = din("tau", [1, 1])

    wqT_in = din("wqT", [128, 8, D])
    wkT_in = din("wkT", [128, 8, D])
    wvT1_in = din("wvT1", [128, 8, D])
    wgT_in = din("wgT", [128, 8, D])
    bg_in = din("bg", [128, 8])
    woutT_in = din("woutT", [128, 8, D])
    dp_in = din("dp", [16, 1])

    W1fT_in = din("W1fT", [128, 8, 4096])
    b1f_in = din("b1f", [128, 32])
    W2fT_in = din("W2fT", [128, 32, D])
    b2f_in = din("b2f", [1, D])

    y_out = nc.dram_tensor("y", [T, D], F32, kind="ExternalOutput").ap()
    dbg = {}

    def dump_req(name, shape, dtype=F32):
        if name in debug_keys:
            if name not in dbg:
                dbg[name] = nc.dram_tensor("dbg_" + name, list(shape), dtype,
                                           kind="ExternalOutput").ap()
            return dbg[name]
        return None

    sp_cloc = scratch("sp_cloc", [128, NJ, T])
    sp_b = scratch("sp_b", [128, NJ, T], BF16)
    sp_P = scratch("sp_P", [128, NJ, T])
    sp_x1 = scratch("sp_x1", [128, NT, D])
    sp_at = scratch("sp_at", [128, NT, D])
    sp_v = scratch("sp_v", [128, NJ, T], BF16)
    sp_kv = scratch("sp_kv", [128, NJ, T], BF16)
    sp_ks = scratch("sp_ks", [128, NJ, T], BF16)
    cc_slot_i = scratch("cc_slot_i", [64, 1025])
    cc_slot_o = scratch("cc_slot_o", [64, 1025])
    cc1i = [scratch(f"cc1i{l}", [128, 8]) for l in range(3)]
    cc1o = [scratch(f"cc1o{l}", [128, 8]) for l in range(3)]
    cc2i = [scratch(f"cc2i{l}", [128, 8]) for l in range(3)]
    cc2o = [scratch(f"cc2o{l}", [128, 8]) for l in range(3)]
    cchi = [scratch(f"cchi{l}", [128, 24]) for l in range(2)]
    ccho = [scratch(f"ccho{l}", [128, 24]) for l in range(2)]
    ccl1i = scratch("ccl1i", [128, 16])
    ccl1o = scratch("ccl1o", [128, 16])

    def mk(name, shape, dtype=F32, side="left"):
        g = nc.sbuf_tensor(name, list(shape), dtype, side=side)
        return g.__enter__(), g

    def sbp(name, shape, dtype=F32):
        t, _ = mk(name, shape, dtype)
        return t

    consts = sbp("sb_consts", [128, len(CONSTS)])
    for i, v in enumerate(CONSTS):
        nc.const_aps.register(F32, float(v), consts[:, i:i + 1])
    ident_b = sbp("sb_ident", [128, 128], BF16)
    ones_f = sbp("sb_onesf", [1, 128])
    ones_colb = sbp("sb_onescolb", [128, 1], BF16)
    Gt = sbp("sb_G", [128, 8 * 16])
    GTt = sbp("sb_GT", [16, 8 * 128])
    flagS = sbp("sb_flagS", [128, 1])
    flagR = sbp("sb_flagR", [128, 1])
    cw = [sbp(f"sb_cw{l}", [128, 32]) for l in range(3)]
    cb = [sbp(f"sb_cb{l}", [128, 8]) for l in range(3)]
    wrb = [sbp(f"sb_wrb{l}", [128, 8]) for l in range(3)]
    wib = [sbp(f"sb_wib{l}", [128, 8]) for l in range(3)]
    la = [sbp(f"sb_la{l}", [128, 8]) for l in range(3)]
    scv = [sbp(f"sb_scv{l}", [128, 8]) for l in range(3)]
    nwpp = [sbp(f"sb_nwpp{l}", [128, 8]) for l in range(2)]
    nw2row = sbp("sb_nw2row", [1, D])
    nowrow = sbp("sb_nowrow", [1, D])
    bg_pp = sbp("sb_bg", [128, 8])
    b1f_pp = sbp("sb_b1f", [128, 32])
    b1fq_pp = sbp("sb_b1fq", [128, 32])
    b1q8 = sbp("sb_b1q8", [128, 32], FP8)
    dp_s = sbp("sb_dp", [16, 4])
    logd = sbp("sb_logd", [128, 8])

    st1 = sbp("sb_st1", [128, NT])
    st2 = sbp("sb_st2", [128, NT])
    stm = sbp("sb_stm", [128, NT])
    strv = sbp("sb_strv", [128, NT])
    stmr = sbp("sb_stmr", [128, NT])
    scr16 = sbp("sb_scr16", [128, NT])
    sq_scr = sbp("sb_sqscr", [128, D])
    sq_scr2 = sbp("sb_sqscr2", [128, D])
    p3st = sbp("sb_p3st", [3, 4])

    ccar = sbp("sb_ccar", [128, 8])
    nccst = sbp("sb_nccst", [128, 8])
    ccprev = sbp("sb_ccprev", [128, 8])
    hcar = sbp("sb_hcar", [128, 8])
    ccl = sbp("sb_ccl", [128, 8])
    hll = sbp("sb_hll", [128, 8])
    kvcar = sbp("sb_kvcar", [128, 8])
    kscar = sbp("sb_kscar", [128, 8])
    stage32 = sbp("sb_stage32", [128, 32])
    recv32 = sbp("sb_recv32", [128, 32])
    stage16 = sbp("sb_stage16", [128, 16])
    recv16 = sbp("sb_recv16", [128, 16])

    ones_ch = sbp("sb_ones_ch", [128, TSD])
    ones_r5 = sbp("sb_ones_r5", [1, TSD], BF16)

    def psum(name, shape, dtype=F32):
        g = nc.psum_tensor(name, list(shape), dtype)
        return g.__enter__()
    pA = psum("pA", [128, 512])
    pB = psum("pB", [128, 512])
    pC = psum("pC", [128, 512])
    pD = psum("pD", [128, 512])
    pE = psum("pE", [128, 1024])
    pT = psum("pT", [128, 1024], BF16)
    pT2 = psum("pT2", [128, 1024], BF16)

    sems = {}
    for k in ["v", "c", "p", "g", "dS", "dG"]:
        g = nc.semaphore("sem_" + k)
        sems[k] = g.__enter__()

    blockg = nc.Block()
    block = blockg.__enter__()
    ctx = Ctx(nc, block, sems)

    def newton_rsqrt(e, E, veps_ap, rstd_ap, scr_ap):
        E.g(e.reciprocal(out=rstd_ap, in_=scr_ap), "v")
        E.g(e.tensor_tensor(out=scr_ap, in0=rstd_ap, in1=rstd_ap, op=ALU.mult), "v")
        E.g(e.tensor_tensor(out=scr_ap, in0=scr_ap, in1=veps_ap, op=ALU.mult), "v")
        E.g(e.tensor_scalar(out=scr_ap, in0=scr_ap, scalar1=-0.5, scalar2=1.5,
                              op0=ALU.mult, op1=ALU.add), "v")
        E.g(e.tensor_tensor(out=rstd_ap, in0=rstd_ap, in1=scr_ap, op=ALU.mult), "v")

    def collective(stage_ap, recv_ap, dram_i, dram_o):
        def fg(e, E):
            E.w("v")
            E.inc(e.dma_start(out=dram_i, in_=stage_ap), "dG", 16)
            E.w("dG")
            E.inc(e.collective_compute("AllReduce", ALU.add, replica_groups=RG,
                                       ins=[dram_i], outs=[dram_o]), "g")
            E.w("g")
            E.inc(e.dma_start(out=recv_ap, in_=dram_o), "dG", 16)
        ctx.sec("g", fg)

    def repl_row(row_ap, ncols, dst):
        off = 0
        while off < ncols:
            nn = min(512, ncols - off)
            def fp1(e, E, off=off, nn=nn):
                E.w("dS"); E.w("v"); E.w("c")
                E.inc(e.matmul(out=pD[:, 0:nn], lhsT=ones_f[:], rhs=row_ap[:, off:off + nn],
                               start=True, stop=True), "p")
            ctx.sec("p", fp1)
            def fv1(e, E, off=off, nn=nn):
                E.w("p")
                E.inc(e.tensor_copy(out=dst[:, off:off + nn], in_=pD[:, 0:nn]), "v")
            ctx.sec("v", fv1)
            off += nn

    w1p, g_w1p = mk("sb_w1p", [128, 8, D], BF16)
    w2p, g_w2p = mk("sb_w2p", [128, 8, D], BF16)
    Sv, g_Sv = mk("Sv", [64, D])
    SvT_bf, g_SvT = mk("SvT_bf", [128, 8, 64], BF16)
    sscale, g_ssc = mk("sscale", [64, 1])
    alpha_tm, g_atm = mk("alpha_tm", [128, NT, 64], BF16)
    alpha_sm, g_asm = mk("alpha_sm", [64, T], BF16)
    si_full, g_si = mk("si_full", [64, 1025])
    hX, g_hX = mk("hX", [128, NJ, 3 + T], BF16)

    xn_tm_bf, g_xnbf = mk("xn_tm_bf", [128, NT, D], BF16, side="right")
    xp3, g_xp3 = mk("sb_xp3", [3, D], side="right")
    xp3n, g_xp3n = mk("sb_xp3n", [3, D], BF16, side="right")
    xst0, g_xst0 = mk("xst0", [128, D], side="right")
    xst1, g_xst1 = mk("xst1", [128, D], side="right")

    def s0(e, E):
        pairs = [
            (consts[:], consts_in), (ident_b[:], ident_b_in), (ones_f[:], ones_f_in),
            (ones_colb[:], ones_colb_in),
            (Gt[:], G_in), (GTt[:], GT_in), (flagS[:], flag_send), (flagR[:], flag_recv),
            (xp3[:], xprev3),
            (nw2row[:], nw2row_in), (nowrow[:], nowrow_in),
            (bg_pp[:], bg_in), (b1f_pp[:], b1f_in), (dp_s[:, 0:1], dp_in),
        ]
        for l in range(3):
            pairs += [(cw[l][:], cw_in[l]), (cb[l][:], cb_in[l]), (wrb[l][:], wrb_in[l]),
                      (wib[l][:], wib_in[l]), (la[l][:], la_in[l])]
        for l in range(2):
            pairs += [(nwpp[l][:], nwpp_in[l])]
        for dst, src in pairs:
            E.inc(e.dma_start(out=dst, in_=src), "dS", 16)
    ctx.sec("s", s0)

    def s0v(e, E):
        E.inc(e.memset(ones_ch[:], 1.0), "v")
        E.inc(e.memset(ones_r5[:], 1.0), "v")
    ctx.sec("v", s0v)

    x_t = x_own.rearrange("(n p) d -> p n d", p=128)
    for n in range(NT):
        xt = xst0 if n % 2 == 0 else xst1
        def f_lx(e, E, n=n, xt=xt):
            if n >= 2:
                E.w("v"); E.w("c")
            E.inc(e.dma_start(out=xt[:], in_=x_t[:, n, :]), "dS", 16)
        ctx.sec("s" if n % 2 == 0 else "c", f_lx)
        def f_lr(e, E, n=n, xt=xt):
            E.w("dS")
            E.inc(e.reduce_sum(out=st1[:, n:n + 1], in_=xt[:], axis=AX), "v")
        ctx.sec("v", f_lr)
        def f_ls(e, E, n=n, xt=xt):
            E.w("dS")
            E.inc(e.activation(out=sq_scr[:], in_=xt[:], func=AF.Square,
                               accum_out=st2[:, n:n + 1]), "c")
        ctx.sec("c", f_ls)

    def f_ln_mv(e, E):
        E.w("c")
        E.g(e.tensor_scalar(out=stm[:], in0=st1[:], scalar1=1.0 / 1024.0,
                              scalar2=None, op0=ALU.mult), "v")
        E.g(e.tensor_scalar(out=strv[:], in0=st2[:], scalar1=1.0 / 1024.0,
                              scalar2=None, op0=ALU.mult), "v")
        E.g(e.tensor_tensor(out=scr16[:], in0=stm[:], in1=stm[:], op=ALU.mult), "v")
        E.g(e.tensor_tensor(out=strv[:], in0=strv[:], in1=scr16[:], op=ALU.subtract), "v")
        E.g(e.tensor_scalar(out=strv[:], in0=strv[:], scalar1=1e-5,
                              scalar2=None, op0=ALU.add), "v")
    ctx.sec("v", f_ln_mv)

    def f_ln_sqrt(e, E):
        E.w("v")
        E.inc(e.activation(out=scr16[:], in_=strv[:], func=AF.Sqrt), "c")
    ctx.sec("c", f_ln_sqrt)

    def f_ln_fin(e, E):
        E.w("c")
        newton_rsqrt(e, E, strv[:], st2[:], scr16[:])
        E.g(e.tensor_copy(out=strv[:], in_=st2[:]), "v")
        E.g(e.tensor_tensor(out=stmr[:], in0=stm[:], in1=strv[:], op=ALU.mult), "v")
        E.inc(e.reduce_sum(out=p3st[:, 0:1], in_=xp3[:], axis=AX), "v")
    ctx.sec("v", f_ln_fin)
    for n in range(NT):
        xt = xst0 if n % 2 == 0 else xst1
        def f_ax(e, E, n=n, xt=xt):
            E.w("v")
            E.inc(e.dma_start(out=xt[:], in_=x_t[:, n, :]), "dS", 16)
        ctx.sec("s", f_ax)
        def f_av(e, E, n=n, xt=xt):
            E.w("dS")
            E.inc(e.tensor_scalar(out=xn_tm_bf[:, n, :], in0=xt[:],
                                  scalar1=strv[:, n:n + 1], scalar2=stmr[:, n:n + 1],
                                  op0=ALU.mult, op1=ALU.subtract), "v")
        ctx.sec("v", f_av)

    def f_p3c(e, E):
        E.w("dS")
        E.inc(e.activation(out=sq_scr[0:3, :], in_=xp3[:], func=AF.Square,
                           accum_out=p3st[:, 1:2]), "c")
    ctx.sec("c", f_p3c)

    def f_p3v(e, E):
        E.w("c")
        E.g(e.tensor_scalar(out=p3st[:, 0:2], in0=p3st[:, 0:2], scalar1=1.0 / 1024.0,
                              scalar2=None, op0=ALU.mult), "v")
        E.g(e.tensor_scalar(out=p3st[:, 2:3], in0=p3st[:, 0:1], scalar1=p3st[:, 0:1],
                              scalar2=None, op0=ALU.mult), "v")
        E.g(e.tensor_tensor(out=p3st[:, 1:2], in0=p3st[:, 1:2], in1=p3st[:, 2:3], op=ALU.subtract), "v")
        E.g(e.tensor_scalar(out=p3st[:, 1:2], in0=p3st[:, 1:2], scalar1=1e-5,
                              scalar2=None, op0=ALU.add), "v")
    ctx.sec("v", f_p3v)

    def f_p3c2(e, E):
        E.w("v")
        E.inc(e.activation(out=p3st[:, 2:3], in_=p3st[:, 1:2], func=AF.Sqrt), "c")
    ctx.sec("c", f_p3c2)

    def f_p3v2(e, E):
        E.w("c")
        E.g(e.reciprocal(out=p3st[:, 3:4], in_=p3st[:, 2:3]), "v")
        E.g(e.tensor_scalar(out=p3st[:, 0:1], in0=p3st[:, 0:1], scalar1=p3st[:, 3:4],
                              scalar2=None, op0=ALU.mult), "v")
        E.g(e.tensor_scalar(out=xp3n[:], in0=xp3[:], scalar1=p3st[:, 3:4],
                              scalar2=p3st[:, 0:1], op0=ALU.mult, op1=ALU.subtract), "v")
    ctx.sec("v", f_p3v2)

    if (d := dump_req("xn_bf", [128, NT, D], BF16)) is not None:
        def fd(e, E, d=d):
            E.w("v")
            E.inc(e.dma_start(out=d, in_=xn_tm_bf[:]), "dS", 16)
        ctx.sec("s", fd)

    def finish(extra=None):
        def ffin(e, E):
            for k in ["v", "c", "p", "g", "dS", "dG"]:
                E.w(k)
            if extra is not None:
                E.inc(e.dma_start(out=y_out.rearrange("(n p) d -> p n d", p=128),
                                  in_=extra), "dS", 16)
                E.w("dS")
        ctx.sec("s", ffin)
        blockg.__exit__(None, None, None)
        return nc, dbg

    if build_upto < 2:
        return finish(None)

    g_xst1.__exit__(None, None, None)
    g_xst0.__exit__(None, None, None)


    bt_end = {}
    bv_end = {}
    for n in range(NT):
        pTx = pT if n % 2 == 0 else pT2
        def fp(e, E, n=n, pTx=pTx):
            if n < 2:
                E.w("v")
            else:
                E.w("v", bv_end[n - 2])
            for j in range(NJ):
                ins = e.transpose(out=pTx[:, j * 128:(j + 1) * 128],
                                  in_=xn_tm_bf[:, n, j * 128:(j + 1) * 128],
                                  identity=ident_b[:])
            E.inc(ins, "p")
        ctx.sec("p", fp)
        bt_end[n] = ctx.cnt["p"]

        def fv(e, E, n=n, pTx=pTx):
            E.w("p", bt_end[n])
            E.inc(e.tensor_copy(out=hX[:, :, 3 + n * 128: 3 + (n + 1) * 128],
                                in_=pTx[:].rearrange("p (j t) -> p j t", j=NJ)), "v")
        ctx.sec("v", fv)
        bv_end[n] = ctx.cnt["v"]

    def fp2(e, E):
        E.w("v")
        for j in range(NJ):
            ins = e.matmul(out=pA[:, j * 8:j * 8 + 3], lhsT=xp3n[:, j * 128:(j + 1) * 128],
                           rhs=ident_b[0:3, 0:3], start=True, stop=True)
        E.inc(ins, "p")
    ctx.sec("p", fp2)

    def fv2(e, E):
        E.w("p")
        E.inc(e.tensor_copy(out=hX[:, :, 0:3],
                            in_=pA[:, 0:NJ * 8].rearrange("p (j t) -> p j t", j=NJ)[:, :, 0:3]), "v")
    ctx.sec("v", fv2)
    g_xp3n.__exit__(None, None, None)
    g_xp3.__exit__(None, None, None)

    if (d2 := dump_req("xn_fm", [128, NJ, 3 + T], BF16)) is not None:
        def fd2(e, E, d2=d2):
            E.w("v")
            E.inc(e.dma_start(out=d2, in_=hX[:]), "dS", 16)
        ctx.sec("s", fd2)

    routeg = []
    def mkr(name, shape, dtype=F32):
        t, g = mk(name, shape, dtype, side="right")
        routeg.append(g)
        return t
    meanT_bf = mkr("meanT_bf", [128, 8], BF16)
    combo_b = mkr("combo_b", [128, 4])
    b1pp = mkr("b1pp", [128, 4])
    h_route = mkr("h_route", [128, 4, T], BF16)
    logits_tm = mkr("logits_tm", [128, NT, 64])
    vals8 = mkr("vals8", [128, NT, 8])
    mrep_t = mkr("mrep_t", [128, 64])
    e_scr = mkr("e_scr", [128, 64])
    negm = mkr("negm", [128, NT])
    Z8 = mkr("Z8", [128, NT])
    invZ = mkr("invZ", [128, NT])
    invtau = mkr("invtau", [128, 1])
    tau_s = mkr("tau_s", [1, 4])
    b2_rep = mkr("b2_rep", [128, 64])
    b2_row = mkr("b2_row", [1, 64])
    si_stage = mkr("si_stage", [64, 1025])
    sv_bf_scr = mkr("sv_bf_scr", [64, D], BF16)

    if build_upto >= 3:
        def fs_c(e, E):
            E.inc(e.dma_start(out=Sv[:], in_=slot_init_in), "dS", 16)
            E.inc(e.dma_start(out=sscale[:], in_=slot_scale_in), "dS", 16)
            E.inc(e.dma_start(out=b1pp[:], in_=b1r_in), "dS", 16)
            E.inc(e.dma_start(out=b2_row[:], in_=b2r_in), "dS", 16)
            E.inc(e.dma_start(out=tau_s[:, 0:1], in_=tau_in), "dS", 16)
        ctx.sec("s", fs_c)

        def fg0(e, E):
            E.inc(e.dma_start(out=w1p[:, :, 0:512], in_=w1aT_in), "dG", 16)
            E.inc(e.dma_start(out=w1p[:, :, 512:1024], in_=w1bT_in), "dG", 16)
            E.inc(e.dma_start(out=w2p[:, 0:4, 0:64], in_=w2T_in), "dG", 16)
        ctx.sec("g", fg0)

        def fv_c1(e, E):
            E.w("dS")
            E.g(e.tensor_scalar(out=Sv[:], in0=Sv[:], scalar1=sscale[:, 0:1],
                                  scalar2=None, op0=ALU.mult), "v")
            E.g(e.tensor_copy(out=sv_bf_scr[:], in_=Sv[:]), "v")
        ctx.sec("v", fv_c1)

        def fp_c1(e, E):
            E.w("v")
            for j in range(NJ):
                ins = e.transpose(out=pT[:, j * 64:(j + 1) * 64],
                                  in_=sv_bf_scr[:, j * 128:(j + 1) * 128],
                                  identity=ident_b[0:64, 0:64])
            E.inc(ins, "p")
        ctx.sec("p", fp_c1)

        def fv_c2(e, E):
            E.w("p")
            E.g(e.tensor_copy(out=SvT_bf[:], in_=pT[:, 0:512].rearrange("p (j s) -> p j s", j=NJ)), "v")
            E.g(e.tensor_copy(out=sq_scr[:, 0:512], in_=SvT_bf[:].rearrange("p j s -> p (j s)")), "v")
            for j in range(NJ):
                E.inc(e.reduce_sum(out=stage32[:, j:j + 1], in_=sq_scr[:, j * 64:(j + 1) * 64], axis=AX), "v")
            E.w("v")
            E.g(e.tensor_scalar(out=stage32[:, 0:8], in0=stage32[:, 0:8], scalar1=1.0 / 64.0,
                                  scalar2=None, op0=ALU.mult), "v")
            E.g(e.tensor_copy(out=meanT_bf[:], in_=stage32[:, 0:8]), "v")
        ctx.sec("v", fv_c2)

        def fp_c2(e, E):
            E.w("v"); E.w("dG")
            for m in range(4):
                for k in range(8):
                    ins = e.matmul(out=pA[:, m:m + 1],
                                   lhsT=w1p[:, k, 512 + m * 128:512 + (m + 1) * 128],
                                   rhs=meanT_bf[:, k:k + 1], start=(k == 0), stop=(k == 7))
            E.inc(ins, "p")
        ctx.sec("p", fp_c2)

        def fv_c3(e, E):
            E.w("p")
            E.g(e.tensor_tensor(out=combo_b[:], in0=b1pp[:], in1=pA[:, 0:4], op=ALU.add), "v")
        ctx.sec("v", fv_c3)

        def fc_c0(e, E):
            E.w("dS")
            E.inc(e.activation(out=tau_s[:, 1:2], in_=tau_s[:, 0:1], func=AF.Abs), "c")
        ctx.sec("c", fc_c0)

        def fv_c4(e, E):
            E.w("c")
            E.g(e.tensor_scalar(out=tau_s[:, 1:2], in0=tau_s[:, 1:2], scalar1=0.1,
                                  scalar2=None, op0=ALU.add), "v")
            E.g(e.reciprocal(out=tau_s[:, 2:3], in_=tau_s[:, 1:2]), "v")
        ctx.sec("v", fv_c4)

        def fp_c3(e, E):
            E.w("v"); E.w("c")
            E.inc(e.matmul(out=pB[:, 0:1], lhsT=ones_f[:], rhs=tau_s[:, 2:3],
                           start=True, stop=True), "p")
            E.inc(e.matmul(out=pB[:, 64:128], lhsT=ones_f[:], rhs=b2_row[:],
                           start=True, stop=True), "p")
        ctx.sec("p", fp_c3)

        def fv_c5(e, E):
            E.w("p")
            E.inc(e.tensor_copy(out=invtau[:], in_=pB[:, 0:1]), "v")
            E.inc(e.tensor_copy(out=b2_rep[:], in_=pB[:, 64:128]), "v")
        ctx.sec("v", fv_c5)

        for m in range(4):
            for c4 in range(4):
                def fp_h(e, E, m=m, c4=c4):
                    E.w("v"); E.w("dG"); E.w("c")
                    ps = pA if (m * 4 + c4) % 2 == 0 else pB
                    for k in range(8):
                        ins = e.matmul(out=ps[:], lhsT=w1p[:, k, m * 128:(m + 1) * 128],
                                       rhs=hX[:, k, 3 + c4 * 512:3 + (c4 + 1) * 512],
                                       start=(k == 0), stop=(k == 7))
                    E.inc(ins, "p")
                ctx.sec("p", fp_h)

                def fc_h(e, E, m=m, c4=c4):
                    ps = pA if (m * 4 + c4) % 2 == 0 else pB
                    E.w("p")
                    E.inc(e.activation(out=h_route[:, m, c4 * 512:(c4 + 1) * 512], in_=ps[:],
                                       func=AF.Gelu, bias=combo_b[:, m:m + 1]), "c")
                ctx.sec("c", fc_h)

        for n in range(NT):
            def fp_lg(e, E, n=n):
                E.w("c")
                if n > 0:
                    E.w("v")
                for k in range(4):
                    ins = e.matmul(out=pC[:, 0:64], lhsT=h_route[:, k, n * 128:(n + 1) * 128],
                                   rhs=w2p[:, k, 0:64], start=(k == 0), stop=(k == 3))
                E.inc(ins, "p")
            ctx.sec("p", fp_lg)

            def fv_lg(e, E, n=n):
                E.w("p")
                E.g(e.tensor_tensor(out=logits_tm[:, n, :], in0=pC[:, 0:64],
                                    in1=b2_rep[:], op=ALU.add), "v")
                E.inc(e.tensor_scalar(out=logits_tm[:, n, :], in0=logits_tm[:, n, :],
                                      scalar1=invtau[:, 0:1], scalar2=None, op0=ALU.mult), "v")
            ctx.sec("v", fv_lg)

        if (dsv := dump_req("svbf", [64, D], BF16)) is not None:
            def fdsv(e, E, dsv=dsv):
                E.w("v")
                E.inc(e.dma_start(out=dsv, in_=sv_bf_scr[:]), "dS", 16)
            ctx.sec("s", fdsv)
        if (dst := dump_req("svt", [128, 8, 64], BF16)) is not None:
            def fdst(e, E, dst=dst):
                E.w("v")
                E.inc(e.dma_start(out=dst, in_=SvT_bf[:]), "dS", 16)
            ctx.sec("s", fdst)
        if (dm := dump_req("meanT", [128, 8], BF16)) is not None:
            def fdm(e, E, dm=dm):
                E.w("v")
                E.inc(e.dma_start(out=dm, in_=meanT_bf[:]), "dS", 16)
            ctx.sec("s", fdm)
        if (dcb := dump_req("combo", [128, 4])) is not None:
            def fdcb(e, E, dcb=dcb):
                E.w("v")
                E.inc(e.dma_start(out=dcb, in_=combo_b[:]), "dS", 16)
            ctx.sec("s", fdcb)
        if (dhr := dump_req("hroute", [128, 4, T], BF16)) is not None:
            def fdhr(e, E, dhr=dhr):
                E.w("c")
                E.inc(e.dma_start(out=dhr, in_=h_route[:]), "dS", 16)
            ctx.sec("s", fdhr)
        if (d3 := dump_req("logits", [128, NT, 64])) is not None:
            def fd3(e, E, d3=d3):
                E.w("v")
                E.inc(e.dma_start(out=d3, in_=logits_tm[:]), "dS", 16)
            ctx.sec("s", fd3)

        for n in range(NT):
            def fv_t8(e, E, n=n):
                E.g(e.max(out=vals8[:, n, :], in_=logits_tm[:, n, :]), "v")
                E.g(e.match_replace(out=mrep_t[:], in_to_replace=vals8[:, n, :],
                                      in_values=logits_tm[:, n, :], imm_value=-1e30), "v")
                E.g(e.tensor_scalar(out=negm[:, n:n + 1], in0=vals8[:, n, 0:1],
                                      scalar1=-1.0, scalar2=None, op0=ALU.mult), "v")
            ctx.sec("v", fv_t8)

            def fc_t8(e, E, n=n):
                E.w("v")
                E.inc(e.activation(out=e_scr[:], in_=logits_tm[:, n, :], func=AF.Exp,
                                   bias=negm[:, n:n + 1]), "c")
                E.inc(e.activation(out=sq_scr[:, 0:8], in_=vals8[:, n, :], func=AF.Exp,
                                   bias=negm[:, n:n + 1], accum_out=Z8[:, n:n + 1]), "c")
            ctx.sec("c", fc_t8)

            def fv_t9(e, E, n=n):
                E.w("c")
                E.g(e.reciprocal(out=invZ[:, n:n + 1], in_=Z8[:, n:n + 1]), "v")
                E.g(e.tensor_scalar(out=mrep_t[:], in0=mrep_t[:], scalar1=-1e30,
                                      scalar2=None, op0=ALU.is_equal), "v")
                E.g(e.tensor_scalar(out=e_scr[:], in0=e_scr[:], scalar1=invZ[:, n:n + 1],
                                      scalar2=None, op0=ALU.mult), "v")
                E.g(e.tensor_tensor(out=alpha_tm[:, n, :], in0=e_scr[:], in1=mrep_t[:],
                                      op=ALU.mult), "v")
            ctx.sec("v", fv_t9)

        for n in range(NT):
            def fp_at(e, E, n=n):
                E.w("v")
                E.inc(e.transpose(out=pT[0:64, 0:128], in_=alpha_tm[:, n, :],
                                  identity=ident_b[:]), "p")
            ctx.sec("p", fp_at)

            def fv_at(e, E, n=n):
                E.w("p")
                E.inc(e.tensor_copy(out=alpha_sm[:, n * 128:(n + 1) * 128], in_=pT[0:64, 0:128]), "v")
            ctx.sec("v", fv_at)

        def fp_si(e, E):
            E.w("v")
            for n in range(NT):
                for half in range(2):
                    ins = e.matmul(out=pE[0:64, half * 512:(half + 1) * 512],
                                   lhsT=alpha_tm[:, n, :],
                                   rhs=xn_tm_bf[:, n, half * 512:(half + 1) * 512],
                                   start=(n == 0), stop=(n == NT - 1))
            E.inc(ins, "p")
        ctx.sec("p", fp_si)

        def fv_si(e, E):
            E.w("p")
            E.inc(e.tensor_copy(out=si_stage[:, 0:1024], in_=pE[0:64, :]), "v")
            E.inc(e.reduce_sum(out=si_stage[:, 1024:1025], in_=alpha_sm[:], axis=AX), "v")
        ctx.sec("v", fv_si)

        if (dat := dump_req("alpha", [128, NT, 64], BF16)) is not None:
            def fdat(e, E, dat=dat):
                E.w("v")
                E.inc(e.dma_start(out=dat, in_=alpha_tm[:]), "dS", 16)
            ctx.sec("s", fdat)
        if (dss := dump_req("sistage", [64, 1025])) is not None:
            def fdss(e, E, dss=dss):
                E.w("v")
                E.inc(e.dma_start(out=dss, in_=si_stage[:]), "dS", 16)
            ctx.sec("s", fdss)
        collective(si_stage[:], si_full[:], cc_slot_i, cc_slot_o)
        if (d4 := dump_req("si", [64, 1025])) is not None:
            def fd4(e, E, d4=d4):
                E.w("dG")
                E.inc(e.dma_start(out=d4, in_=si_full[:]), "dS", 16)
            ctx.sec("s", fd4)

    for g in reversed(routeg):
        g.__exit__(None, None, None)
    g_xnbf.__exit__(None, None, None)

    wrbrow, g_wrbr = mk("sb_wrbrow", [1, D], BF16, side="right")
    wibrow, g_wibr = mk("sb_wibrow", [1, D], BF16, side="right")
    xcv, g_xcv = mk("sb_xcv", [128, TSD], F32, side="right")
    xbfA, g_xbfA = mk("sb_xbfA", [128, NJ, TSD], BF16, side="right")
    xbfB, g_xbfB = mk("sb_xbfB", [128, NJ, TSD], BF16, side="right")
    rb, g_rb = mk("sb_rb", [128, NJ, TSD], F32, side="right")
    ib, g_ib = mk("sb_ib", [128, NJ, TSD], F32, side="right")
    clb, g_clb = mk("sb_clb", [128, NJ, TSD], F32, side="right")
    hbf_full, g_hbf = mk("hbf_full", [128, NJ, T], BF16, side="right")
    FLT = lambda b: b[:].rearrange("p j t -> p (j t)")

    def rglru_layer(l, h):
        def fc_p(e, E):
            E.w("dS")
            E.g(e.activation(out=scv[l][:], in_=la[l][:], func=AF.Sigmoid), "c")
            E.g(e.activation(out=scv[l][:], in_=scv[l][:], func=AF.Ln), "c")
        ctx.sec("c", fc_p)

        def fv_p(e, E):
            E.w("c")
            E.g(e.tensor_scalar(out=scv[l][:], in0=scv[l][:], scalar1=8.0,
                                  scalar2=None, op0=ALU.mult), "v")
            E.g(e.memset(ccl[:], 0.0), "v")
        ctx.sec("v", fv_p)

        psnap = ctx.cnt["p"]
        def fg_w(e, E, psnap=psnap):
            E.w("p", psnap)
            E.inc(e.dma_start(out=w1p[:], in_=wrT[l]), "dG", 16)
            E.inc(e.dma_start(out=w2p[:], in_=wiT[l]), "dG", 16)
            E.inc(e.dma_start(out=wrbrow[:], in_=wrbr_in[l]), "dG", 16)
            E.inc(e.dma_start(out=wibrow[:], in_=wibr_in[l]), "dG", 16)
        ctx.sec("g", fg_w)
        dg_w = ctx.cnt["dG"]

        cast_end = {}
        mm_end = {}
        ev_end = {}
        spl_end = {}

        def conv_stripe(s):
            c0 = s * TSD
            xbf = xbfA if s % 2 == 0 else xbfB
            def fv_cv(e, E, s=s, c0=c0, xbf=xbf):
                if s >= 2:
                    E.w("p", mm_end[s - 2])
                for j in range(NJ):
                    e.tensor_scalar(out=xbf[:, j, :], in0=h[:, j, c0:c0 + TSD],
                                    scalar1=cw[l][:, j * 4:j * 4 + 1], scalar2=None, op0=ALU.mult)
                    for kk in range(1, 3):
                        e.scalar_tensor_tensor(out=xbf[:, j, :], in0=h[:, j, c0 + kk:c0 + kk + TSD],
                                               scalar=cw[l][:, j * 4 + kk:j * 4 + kk + 1],
                                               in1=xbf[:, j, :], op0=ALU.mult, op1=ALU.add)
                    e.scalar_tensor_tensor(out=xcv[:], in0=h[:, j, c0 + 3:c0 + 3 + TSD],
                                           scalar=cw[l][:, j * 4 + 3:j * 4 + 4],
                                           in1=xbf[:, j, :], op0=ALU.mult, op1=ALU.add)
                    ins = e.tensor_scalar(out=xbf[:, j, :], in0=xcv[:],
                                          scalar1=cb[l][:, j:j + 1], scalar2=None, op0=ALU.add)
                E.inc(ins, "v")
            ctx.sec("v", fv_cv)
            cast_end[s] = ctx.cnt["v"]

        def mms_stripe(s):
            xbf = xbfA if s % 2 == 0 else xbfB
            evc = {}
            for gate, (wp, dst) in enumerate(((w1p, rb), (w2p, ib))):
                for m in range(NJ):
                    ps = (pA, pB, pC, pD)[gate * 2 + (m % 2)]
                    cwait = evc.get((gate, m - 2), ev_end.get(s - 1, 0))
                    brow = (wrbrow, wibrow)[gate]
                    def fp_m(e, E, s=s, m=m, ps=ps, wp=wp, xbf=xbf, cwait=cwait,
                             brow=brow, gate=gate):
                        E.w("v", cast_end[s])
                        E.w("dG", dg_w)
                        E.w("v", cwait)
                        if gate == 1 and m == 0:
                            E.w("dG", spl_end.get(s - 1, 0))
                        e.matmul(out=ps[:, 0:TSD], lhsT=brow[0:1, m * 128:(m + 1) * 128],
                                 rhs=ones_r5[:], start=True, stop=False)
                        for k in range(NJ):
                            ins = e.matmul(out=ps[:, 0:TSD], lhsT=wp[:, k, m * 128:(m + 1) * 128],
                                           rhs=xbf[:, k, :], start=False, stop=(k == NJ - 1))
                        E.inc(ins, "p")
                    ctx.sec("p", fp_m)
                    pm = ctx.cnt["p"]
                    def fv_ev(e, E, s=s, m=m, ps=ps, dst=dst, pm=pm):
                        E.w("p", pm)
                        E.inc(e.tensor_copy(out=dst[:, m, :], in_=ps[:, 0:TSD]), "v")
                    ctx.sec("v", fv_ev)
                    evc[(gate, m)] = ctx.cnt["v"]
            mm_end[s] = ctx.cnt["p"]
            ev_end[s] = ctx.cnt["v"]
            def fc_sig(e, E, s=s):
                E.w("v", ev_end[s])
                E.inc(e.activation(out=FLT(rb), in_=FLT(rb), func=AF.Sigmoid), "c")
                E.inc(e.activation(out=FLT(ib), in_=FLT(ib), func=AF.Sigmoid), "c")
            ctx.sec("c", fc_sig)

        def vchainA_stripe(s):
            xbf = xbfA if s % 2 == 0 else xbfB
            def fv_a(e, E, s=s, xbf=xbf):
                E.w("c")
                if s >= 1:
                    E.w("dG", spl_end[s - 1])
                E.inc(e.tensor_tensor(out=FLT(ib), in0=FLT(ib), in1=FLT(xbf), op=ALU.mult), "v")
                for j in range(NJ):
                    e.tensor_scalar(out=rb[:, j, :], in0=rb[:, j, :],
                                    scalar1=scv[l][:, j:j + 1], scalar2=None, op0=ALU.mult)
                E.w("v")
                for j in range(NJ):
                    ins = e.tensor_tensor_scan(out=clb[:, j, :], data0=ones_ch[:, 0:TSD],
                                               data1=rb[:, j, :], initial=ccl[:, j:j + 1],
                                               op0=ALU.mult, op1=ALU.add)
                E.g(ins, "v")
                E.g(e.tensor_copy(out=ccl[:], in_=clb[:, :, TSD - 1]), "v")
            ctx.sec("v", fv_a)

        def exp_stripe(s):
            vA = ctx.cnt["v"]
            def fc_x(e, E, vA=vA):
                E.w("v", vA)
                E.inc(e.activation(out=FLT(rb), in_=FLT(rb), func=AF.Exp, scale=2.0), "c")
            ctx.sec("c", fc_x)

        def vchainB_stripe(s):
            def fv_b1(e, E):
                E.w("c")
                E.inc(e.tensor_scalar(out=FLT(rb), in0=FLT(rb), scalar1=-1.0,
                                      scalar2=1.0, op0=ALU.mult, op1=ALU.add), "v")
                E.inc(e.tensor_scalar(out=FLT(rb), in0=FLT(rb), scalar1=1e-6,
                                      scalar2=None, op0=ALU.max), "v")
            ctx.sec("v", fv_b1)

        def sqrt_stripe(s, vsnap):
            def fc_s(e, E, vsnap=vsnap):
                E.w("v", vsnap)
                E.inc(e.activation(out=FLT(rb), in_=FLT(rb), func=AF.Sqrt), "c")
            ctx.sec("c", fc_s)

        def bspill_stripe(s):
            c0 = s * TSD
            csnap = ctx.cnt["c"]
            def fv_bb(e, E, csnap=csnap):
                E.w("c", csnap)
                E.inc(e.tensor_tensor(out=FLT(ib), in0=FLT(ib), in1=FLT(rb), op=ALU.mult), "v")
            ctx.sec("v", fv_bb)
            vsn = ctx.cnt["v"]
            def fg_sp(e, E, c0=c0, vsn=vsn):
                E.w("v", vsn)
                E.inc(e.dma_start(out=sp_cloc[:, :, c0:c0 + TSD], in_=clb[:]), "dG", 16)
                E.inc(e.dma_start(out=sp_b[:, :, c0:c0 + TSD], in_=ib[:]), "dG", 16)
            ctx.sec("g", fg_sp)
            spl_end[s] = ctx.cnt["dG"]

        conv_stripe(0)
        conv_stripe(1)
        for s in range(NSD):
            mms_stripe(s)
            vchainA_stripe(s)
            exp_stripe(s)
            if s + 2 < NSD:
                conv_stripe(s + 2)
            vchainB_stripe(s)
            vB = ctx.cnt["v"]
            sqrt_stripe(s, vB)
            bspill_stripe(s)

        def fv_s1(e, E):
            E.w("v")
            E.g(e.tensor_scalar(out=stage32[:, 0:8], in0=ccl[:], scalar1=-80.0,
                                  scalar2=None, op0=ALU.max), "v")
            E.g(e.tensor_scalar(out=stage32[:, 0:8], in0=stage32[:, 0:8],
                                  scalar1=flagS[:, 0:1], scalar2=None, op0=ALU.mult), "v")
        ctx.sec("v", fv_s1)
        collective(stage32[:, 0:8], recv32[:, 0:8], cc1i[l], cc1o[l])

        def fv_r1(e, E):
            E.w("dG")
            E.g(e.tensor_scalar(out=ccar[:], in0=recv32[:, 0:8], scalar1=flagR[:, 0:1],
                                  scalar2=None, op0=ALU.mult), "v")
            E.g(e.tensor_scalar(out=nccst[:], in0=ccar[:], scalar1=-80.0,
                                  scalar2=None, op0=ALU.max), "v")
            E.g(e.tensor_copy(out=ccprev[:], in_=nccst[:]), "v")
            E.g(e.tensor_scalar(out=nccst[:], in0=nccst[:], scalar1=-1.0,
                                  scalar2=None, op0=ALU.mult), "v")
            E.g(e.memset(hll[:], 0.0), "v")
        ctx.sec("v", fv_r1)

        ld2_end = {}
        hs_end = {}
        sp2_end = {}
        for s in range(NSD):
            c0 = s * TSD
            xbf2 = xbfA if s % 2 == 0 else xbfB
            def fs_l2(e, E, s=s, c0=c0, xbf2=xbf2):
                if s == 0:
                    E.w("v")
                    E.w("dS")
                    E.w("dG")
                else:
                    E.w("dS", sp2_end[s - 1])
                    if s >= 2:
                        E.w("v", hs_end[s - 2])
                E.inc(e.dma_start(out=clb[:], in_=sp_cloc[:, :, c0:c0 + TSD]), "dS", 16)
                E.inc(e.dma_start(out=xbf2[:], in_=sp_b[:, :, c0:c0 + TSD]), "dS", 16)
            ctx.sec("s", fs_l2)
            ld2_end[s] = ctx.cnt["dS"]

            def fv_cc(e, E, s=s):
                E.w("dS", ld2_end[s])
                for j in range(NJ):
                    e.tensor_scalar(out=clb[:, j, :], in0=clb[:, j, :],
                                    scalar1=ccar[:, j:j + 1], scalar2=-80.0,
                                    op0=ALU.add, op1=ALU.max)
                for j in range(NJ):
                    e.tensor_tensor(out=rb[:, j, 1:TSD], in0=clb[:, j, 1:TSD],
                                    in1=clb[:, j, 0:TSD - 1], op=ALU.subtract)
                    e.tensor_tensor(out=rb[:, j, 0:1], in0=clb[:, j, 0:1],
                                    in1=ccprev[:, j:j + 1], op=ALU.subtract)
                E.w("v")
                E.g(e.tensor_copy(out=ccprev[:], in_=clb[:, :, TSD - 1]), "v")
            ctx.sec("v", fv_cc)
            vcc = ctx.cnt["v"]

            def fc_x2(e, E, vcc=vcc):
                E.w("v", vcc)
                E.inc(e.activation(out=FLT(rb), in_=FLT(rb), func=AF.Exp), "c")
                E.inc(e.activation(out=FLT(clb), in_=FLT(clb), func=AF.Exp), "c")
            ctx.sec("c", fc_x2)
            cx2 = ctx.cnt["c"]

            xbf2 = xbfA if s % 2 == 0 else xbfB
            def fv_hs(e, E, s=s, c0=c0, cx2=cx2, xbf2=xbf2):
                E.w("c", cx2)
                for j in range(NJ):
                    ins = e.tensor_tensor_scan(out=ib[:, j, :], data0=rb[:, j, :],
                                               data1=xbf2[:, j, :], initial=hll[:, j:j + 1],
                                               op0=ALU.mult, op1=ALU.add)
                E.g(ins, "v")
                E.g(e.tensor_copy(out=hll[:], in_=ib[:, :, TSD - 1]), "v")
                for j in range(NJ):
                    E.inc(e.tensor_copy(out=hbf_full[:, j, c0:c0 + TSD], in_=ib[:, j, :]), "v")
            ctx.sec("v", fv_hs)
            hs_end[s] = ctx.cnt["v"]

            def fs_sP(e, E, s=s, c0=c0, cx2=cx2):
                E.w("c", cx2)
                E.inc(e.dma_start(out=sp_P[:, :, c0:c0 + TSD], in_=clb[:]), "dS", 16)
            ctx.sec("s", fs_sP)
            sp2_end[s] = ctx.cnt["dS"]

        def fv_s2(e, E):
            E.w("v")
            E.g(e.tensor_scalar(out=stage32[:, 0:8], in0=hll[:], scalar1=flagS[:, 0:1],
                                  scalar2=None, op0=ALU.mult), "v")
        ctx.sec("v", fv_s2)
        collective(stage32[:, 0:8], recv32[:, 0:8], cc2i[l], cc2o[l])

        def fv_r2(e, E):
            E.w("dG")
            E.g(e.tensor_scalar(out=hcar[:], in0=recv32[:, 0:8], scalar1=flagR[:, 0:1],
                                  scalar2=None, op0=ALU.mult), "v")
        ctx.sec("v", fv_r2)
        def fc_r2(e, E):
            E.w("v")
            E.inc(e.activation(out=recv32[:, 0:8], in_=nccst[:], func=AF.Exp), "c")
        ctx.sec("c", fc_r2)
        def fv_r2b(e, E):
            E.w("c")
            E.g(e.tensor_tensor(out=hcar[:], in0=hcar[:], in1=recv32[:, 0:8], op=ALU.mult), "v")
        ctx.sec("v", fv_r2b)

        co_end = {}

        def emit_corr(s):
            c0 = s * TSD
            def fs_lP(e, E, s=s, c0=c0):
                if s >= 1:
                    E.w("v", co_end[s - 1])
                else:
                    E.w("dS")
                E.inc(e.dma_start(out=clb[:], in_=sp_P[:, :, c0:c0 + TSD]), "dS", 16)
            ctx.sec("s", fs_lP)
            lp = ctx.cnt["dS"]

            def fv_co(e, E, s=s, c0=c0, lp=lp):
                E.w("dS", lp)
                for j in range(NJ):
                    ins = e.scalar_tensor_tensor(out=hbf_full[:, j, c0:c0 + TSD],
                                                 in0=clb[:, j, :], scalar=hcar[:, j:j + 1],
                                                 in1=hbf_full[:, j, c0:c0 + TSD],
                                                 op0=ALU.mult, op1=ALU.add)
                E.inc(ins, "v")
            ctx.sec("v", fv_co)
            co_end[s] = ctx.cnt["v"]

        psnap2 = ctx.cnt["p"]
        def fg_wo(e, E, psnap2=psnap2):
            E.w("p", psnap2)
            E.inc(e.dma_start(out=w1p[:], in_=woT[l]), "dG", 16)
        ctx.sec("g", fg_wo)
        dg_wo = ctx.cnt["dG"]

        if l < 2:
            for s in range(NSD):
                c0 = s * TSD
                emit_corr(s)
                for m in range(NJ):
                    ps = pA if m % 2 == 0 else pB
                    def fp_w(e, E, s=s, m=m, ps=ps, c0=c0):
                        E.w("v")
                        E.w("dG", dg_wo)
                        for k in range(NJ):
                            ins = e.matmul(out=ps[:, 0:TSD], lhsT=w1p[:, k, m * 128:(m + 1) * 128],
                                           rhs=hbf_full[:, k, c0:c0 + TSD],
                                           start=(k == 0), stop=(k == NJ - 1))
                        E.inc(ins, "p")
                    ctx.sec("p", fp_w)
                    pm = ctx.cnt["p"]
                    def fv_wev(e, E, m=m, ps=ps, pm=pm):
                        E.w("p", pm)
                        E.inc(e.tensor_copy(out=rb[:, m, :], in_=ps[:, 0:TSD]), "v")
                    ctx.sec("v", fv_wev)

                def fv_sq(e, E):
                    E.inc(e.tensor_tensor(out=FLT(xbfB), in0=FLT(rb), in1=FLT(rb),
                                          op=ALU.mult), "v")
                ctx.sec("v", fv_sq)

                def fp_ss(e, E):
                    E.w("v")
                    for k in range(NJ):
                        ins = e.matmul(out=pE[0:1, 0:TSD], lhsT=ones_colb[:],
                                       rhs=xbfB[:, k, :], start=(k == 0), stop=(k == NJ - 1))
                    E.inc(ins, "p")
                ctx.sec("p", fp_ss)

                def fv_rs(e, E):
                    E.w("p")
                    E.g(e.tensor_scalar(out=sq_scr[0:1, 0:TSD], in0=pE[0:1, 0:TSD],
                                          scalar1=1.0 / 1024.0, scalar2=1e-6,
                                          op0=ALU.mult, op1=ALU.add), "v")
                ctx.sec("v", fv_rs)

                def fc_rs(e, E):
                    E.w("v")
                    E.inc(e.activation(out=sq_scr[0:1, TSD:2 * TSD], in_=sq_scr[0:1, 0:TSD],
                                       func=AF.Sqrt), "c")
                ctx.sec("c", fc_rs)

                def fv_rn(e, E):
                    E.w("c")
                    newton_rsqrt(e, E, sq_scr[0:1, 0:TSD], sq_scr2[0:1, 0:TSD],
                                 sq_scr[0:1, TSD:2 * TSD])
                ctx.sec("v", fv_rn)

                def fp_bc(e, E):
                    E.w("v")
                    E.inc(e.matmul(out=pD[:, 0:TSD], lhsT=ones_f[:], rhs=sq_scr2[0:1, 0:TSD],
                                   start=True, stop=True), "p")
                ctx.sec("p", fp_bc)

                def fv_ap(e, E, c0=c0):
                    E.w("p")
                    for j in range(NJ):
                        ins = e.scalar_tensor_tensor(out=h[:, j, 3 + c0:3 + c0 + TSD],
                                                     in0=rb[:, j, :],
                                                     scalar=nwpp[l][:, j:j + 1],
                                                     in1=pD[:, 0:TSD],
                                                     op0=ALU.mult, op1=ALU.mult)
                    E.inc(ins, "v")
                ctx.sec("v", fv_ap)

            def fv_hl(e, E):
                E.w("v")
                for j in range(NJ):
                    ins = e.tensor_copy(out=stage32[:, 8 + j * 3:8 + (j + 1) * 3],
                                        in_=h[:, j, 3 + T - 3:3 + T])
                E.g(ins, "v")
                E.g(e.tensor_scalar(out=stage32[:, 8:32], in0=stage32[:, 8:32],
                                      scalar1=flagS[:, 0:1], scalar2=None, op0=ALU.mult), "v")
            ctx.sec("v", fv_hl)
        else:
            def fp_rep(e, E):
                E.w("dS"); E.w("v"); E.w("c")
                E.inc(e.matmul(out=pA[:], lhsT=ones_f[:], rhs=nw2row[:, 0:512],
                               start=True, stop=True), "p")
                E.inc(e.matmul(out=pB[:], lhsT=ones_f[:], rhs=nw2row[:, 512:1024],
                               start=True, stop=True), "p")
                E.inc(e.matmul(out=pC[:], lhsT=ones_f[:], rhs=nowrow[:, 0:512],
                               start=True, stop=True), "p")
                E.inc(e.matmul(out=pD[:], lhsT=ones_f[:], rhs=nowrow[:, 512:1024],
                               start=True, stop=True), "p")
            ctx.sec("p", fp_rep)
            prep = ctx.cnt["p"]
            def fv_rep(e, E, prep=prep):
                E.w("p", prep)
                E.inc(e.tensor_copy(out=rb[:, 0, :], in_=pA[:]), "v")
                E.inc(e.tensor_copy(out=rb[:, 1, :], in_=pB[:]), "v")
                E.inc(e.tensor_copy(out=ib[:, 0, :], in_=pC[:]), "v")
                E.inc(e.tensor_copy(out=ib[:, 1, :], in_=pD[:]), "v")
            ctx.sec("v", fv_rep)
            vrep = ctx.cnt["v"]
            e3_end = {}
            at_end = {}
            sq_t = {0: ((rb[:, 2, :], rb[:, 3, :]), (rb[:, 4, :], rb[:, 5, :])),
                    1: ((ib[:, 2, :], ib[:, 3, :]), (ib[:, 4, :], ib[:, 5, :])),
                    2: ((rb[:, 6, :], rb[:, 7, :]), (sq_scr[:, 0:512], sq_scr[:, 512:1024])),
                    3: ((ib[:, 6, :], ib[:, 7, :]), (sq_scr2[:, 0:512], sq_scr2[:, 512:1024]))}
            for nt in range(NT):
                if nt % 4 == 0:
                    emit_corr(nt // 4)
                pzs = [(pE[:, 0:512], pE[:, 512:1024]), (pA[:], pB[:]),
                       (pC[:], pD[:])][nt % 3]
                (sa0, sa1), (sb0, sb1) = sq_t[nt % 4]
                def fp_w2(e, E, nt=nt, pzs=pzs):
                    E.w("v", max(vrep, e3_end.get(nt - 3, 0)))
                    E.w("dG", dg_wo)
                    if nt % 4 == 0:
                        E.w("v")
                    for half in range(2):
                        for k in range(NJ):
                            ins = e.matmul(out=pzs[half],
                                           lhsT=hbf_full[:, k, nt * 128:(nt + 1) * 128],
                                           rhs=w1p[:, k, half * 512:(half + 1) * 512],
                                           start=(k == 0), stop=(k == NJ - 1))
                    E.inc(ins, "p")
                ctx.sec("p", fp_w2)
                pw2 = ctx.cnt["p"]

                def fc_sq1(e, E, nt=nt, pzs=pzs, pw2=pw2, sa0=sa0, sa1=sa1):
                    E.w("p", pw2)
                    E.inc(e.activation(out=sa0, in_=pzs[0], func=AF.Square,
                                       accum_out=st1[:, nt:nt + 1]), "c")
                    E.inc(e.activation(out=sa1, in_=pzs[1], func=AF.Square,
                                       accum_out=st2[:, nt:nt + 1]), "c")
                ctx.sec("c", fc_sq1)
                csq = ctx.cnt["c"]

                def fv_e1(e, E, nt=nt, csq=csq):
                    E.w("c", csq)
                    E.g(e.tensor_tensor(out=st1[:, nt:nt + 1], in0=st1[:, nt:nt + 1],
                                        in1=st2[:, nt:nt + 1], op=ALU.add), "v")
                    E.g(e.tensor_scalar(out=st1[:, nt:nt + 1], in0=st1[:, nt:nt + 1],
                                          scalar1=1.0 / 1024.0, scalar2=1e-6,
                                          op0=ALU.mult, op1=ALU.add), "v")
                ctx.sec("v", fv_e1)
                ve1 = ctx.cnt["v"]

                def fc_e1(e, E, nt=nt, ve1=ve1):
                    E.w("v", ve1)
                    E.inc(e.activation(out=st2[:, nt:nt + 1], in_=st1[:, nt:nt + 1],
                                       func=AF.Sqrt), "c")
                ctx.sec("c", fc_e1)
                ce1 = ctx.cnt["c"]

                def fv_e2(e, E, nt=nt, pzs=pzs, ce1=ce1, sa0=sa0, sa1=sa1):
                    E.w("c", ce1)
                    newton_rsqrt(e, E, st1[:, nt:nt + 1], stm[:, nt:nt + 1], st2[:, nt:nt + 1])
                    E.inc(e.tensor_scalar(out=sa0, in0=pzs[0], scalar1=stm[:, nt:nt + 1],
                                          scalar2=None, op0=ALU.mult), "v")
                    E.g(e.tensor_scalar(out=sa1, in0=pzs[1], scalar1=stm[:, nt:nt + 1],
                                          scalar2=None, op0=ALU.mult), "v")
                    E.inc(e.tensor_tensor(out=sa0, in0=sa0, in1=rb[:, 0, :], op=ALU.mult), "v")
                    E.g(e.tensor_tensor(out=sa1, in0=sa1, in1=rb[:, 1, :], op=ALU.mult), "v")
                ctx.sec("v", fv_e2)
                ve2 = ctx.cnt["v"]

                def fc_sq2(e, E, nt=nt, ve2=ve2, sa0=sa0, sa1=sa1, sb0=sb0, sb1=sb1):
                    E.w("v", ve2)
                    if nt >= 4:
                        E.w("dS", at_end[nt - 4])
                    E.inc(e.activation(out=sb0, in_=sa0, func=AF.Square,
                                       accum_out=st2[:, nt:nt + 1]), "c")
                    E.inc(e.activation(out=sb1, in_=sa1, func=AF.Square,
                                       accum_out=strv[:, nt:nt + 1]), "c")
                ctx.sec("c", fc_sq2)
                csq2 = ctx.cnt["c"]

                def fv_e2b(e, E, nt=nt, csq2=csq2):
                    E.w("c", csq2)
                    E.g(e.tensor_tensor(out=st2[:, nt:nt + 1], in0=st2[:, nt:nt + 1],
                                        in1=strv[:, nt:nt + 1], op=ALU.add), "v")
                    E.g(e.tensor_scalar(out=st2[:, nt:nt + 1], in0=st2[:, nt:nt + 1],
                                          scalar1=1.0 / 1024.0, scalar2=1e-6,
                                          op0=ALU.mult, op1=ALU.add), "v")
                ctx.sec("v", fv_e2b)
                ve2b = ctx.cnt["v"]

                def fc_e2(e, E, nt=nt, ve2b=ve2b):
                    E.w("v", ve2b)
                    E.inc(e.activation(out=strv[:, nt:nt + 1], in_=st2[:, nt:nt + 1],
                                       func=AF.Sqrt), "c")
                ctx.sec("c", fc_e2)
                ce2 = ctx.cnt["c"]

                def fv_e3(e, E, nt=nt, ce2=ce2, sa0=sa0, sa1=sa1, sb0=sb0, sb1=sb1):
                    E.w("c", ce2)
                    newton_rsqrt(e, E, st2[:, nt:nt + 1], stmr[:, nt:nt + 1], strv[:, nt:nt + 1])
                    E.inc(e.tensor_scalar(out=sa0, in0=sa0, scalar1=stmr[:, nt:nt + 1],
                                          scalar2=None, op0=ALU.mult), "v")
                    E.g(e.tensor_scalar(out=sa1, in0=sa1, scalar1=stmr[:, nt:nt + 1],
                                          scalar2=None, op0=ALU.mult), "v")
                    E.inc(e.tensor_tensor(out=sb0, in0=sa0, in1=ib[:, 0, :], op=ALU.mult), "v")
                    E.inc(e.tensor_tensor(out=sb1, in0=sa1, in1=ib[:, 1, :], op=ALU.mult), "v")
                ctx.sec("v", fv_e3)
                e3_end[nt] = ctx.cnt["v"]

                def fs_at(e, E, nt=nt, sb0=sb0, sb1=sb1):
                    E.w("v", e3_end[nt])
                    E.inc(e.dma_start(out=sp_at[:, nt, 0:512], in_=sb0), "dS", 16)
                    E.inc(e.dma_start(out=sp_at[:, nt, 512:1024], in_=sb1), "dS", 16)
                    if (dat2 := dump_req("at", [128, NT, D])) is not None:
                        E.inc(e.dma_start(out=dat2[:, nt, 0:512], in_=sb0), "dS", 16)
                        E.inc(e.dma_start(out=dat2[:, nt, 512:1024], in_=sb1), "dS", 16)
                ctx.sec("s", fs_at)
                at_end[nt] = ctx.cnt["dS"]

    if build_upto >= 4:
        rglru_layer(0, hX)
        collective(stage32[:, 8:32], recv32[:, 8:32], cchi[0], ccho[0])
        def fv_h1(e, E):
            E.w("dG")
            E.g(e.tensor_scalar(out=recv32[:, 8:32], in0=recv32[:, 8:32],
                                  scalar1=flagR[:, 0:1], scalar2=None, op0=ALU.mult), "v")
            E.g(e.tensor_copy(out=hX[:, :, 0:3],
                                in_=recv32[:, 8:32].rearrange("p (j t) -> p j t", j=NJ)), "v")
        ctx.sec("v", fv_h1)
        if (dh0 := dump_req("h0", [128, NJ, 3 + T], BF16)) is not None:
            def fdh0(e, E, dh0=dh0):
                E.w("v")
                E.inc(e.dma_start(out=dh0, in_=hX[:]), "dS", 16)
            ctx.sec("s", fdh0)
        rglru_layer(1, hX)
        collective(stage32[:, 8:32], recv32[:, 8:32], cchi[1], ccho[1])
        def fv_h2(e, E):
            E.w("dG")
            E.g(e.tensor_scalar(out=recv32[:, 8:32], in0=recv32[:, 8:32],
                                  scalar1=flagR[:, 0:1], scalar2=None, op0=ALU.mult), "v")
            E.g(e.tensor_copy(out=hX[:, :, 0:3],
                                in_=recv32[:, 8:32].rearrange("p (j t) -> p j t", j=NJ)), "v")
        ctx.sec("v", fv_h2)
        if (dh1 := dump_req("h1", [128, NJ, 3 + T], BF16)) is not None:
            def fdh1(e, E, dh1=dh1):
                E.w("v")
                E.inc(e.dma_start(out=dh1, in_=hX[:]), "dS", 16)
            ctx.sec("s", fdh1)
        rglru_layer(2, hX)

    if build_upto < 5:
        return finish(None)

    for g in (g_hbf, g_clb, g_ib, g_rb, g_xbfB, g_xbfA, g_xcv, g_wibr, g_wrbr):
        g.__exit__(None, None, None)
    g_hX.__exit__(None, None, None)
    ACC, g_ACC = mk("ACC", [128, NT, D])
    gg = []
    def mkg(name, shape, dtype=F32):
        t, g = mk(name, shape, dtype, side="right")
        gg.append(g)
        return t
    gx_s = mkg("gx_s", [64, 3072])
    gh_s = mkg("gh_s", [64, 3072])
    giT_bf = mkg("giT_bf", [128, 8, 64], BF16)
    hnew = mkg("hnew", [64, D])
    hnT_bf = mkg("hnT_bf", [128, 8, 64], BF16)
    svals = mkg("svals", [64, D])
    svals_bf = mkg("svals_bf", [64, D], BF16)
    scr64 = mkg("scr64", [64, D])
    scr64b = mkg("scr64b", [64, D])
    brow = mkg("brow", [1, 3072])
    bvrow_s = brow[0:1, 0:D]
    borow_s = brow[0:1, D:2 * D]
    out_bf = mkg("out_bf", [128, NJ, TSF], BF16)

    def fs_e(e, E):
        E.inc(e.dma_start(out=brow[:], in_=bih_in), "dS", 16)
    ctx.sec("s", fs_e)

    def fv_gi(e, E):
        E.w("dG"); E.w("dS")
        E.g(e.tensor_scalar(out=si_full[:, 1024:1025], in0=si_full[:, 1024:1025],
                              scalar1=1e-8, scalar2=None, op0=ALU.add), "v")
        E.g(e.reciprocal(out=scr64[:, 0:1], in_=si_full[:, 1024:1025]), "v")
        E.g(e.tensor_scalar(out=scr64b[:], in0=si_full[:, 0:1024],
                              scalar1=scr64[:, 0:1], scalar2=None, op0=ALU.mult), "v")
        E.g(e.tensor_copy(out=svals_bf[:], in_=scr64b[:]), "v")
    ctx.sec("v", fv_gi)

    def fp_gi(e, E):
        E.w("v")
        for j in range(NJ):
            ins = e.transpose(out=pT[:, j * 64:(j + 1) * 64],
                              in_=svals_bf[:, j * 128:(j + 1) * 128],
                              identity=ident_b[0:64, 0:64])
        E.inc(ins, "p")
    ctx.sec("p", fp_gi)

    def fv_gi2(e, E):
        E.w("p")
        E.inc(e.tensor_copy(out=giT_bf[:], in_=pT[:, 0:512].rearrange("p (j s) -> p j s", j=NJ)), "v")
    ctx.sec("v", fv_gi2)

    guse = {}
    gld = {}
    pieces = []
    for gri, (lhsT_src, dst, win) in enumerate(((giT_bf, gx_s, wihT_in),
                                                (SvT_bf, gh_s, whhT_in))):
        for part in range(3):
            pieces.append((gri, part, lhsT_src, dst, win))
    for idx, (gri, part, lhsT_src, dst, win) in enumerate(pieces):
        wbuf = w1p if idx % 2 == 0 else w2p
        if gri == 1 and part == 0:
            def fs_bhh(e, E):
                psn = ctx.cnt["p"]
                E.w("p", psn)
                E.inc(e.dma_start(out=brow[:], in_=bhh_in), "dS", 16)
            ctx.sec("s", fs_bhh)
        def fg_l(e, E, idx=idx, part=part, win=win, wbuf=wbuf):
            E.w("p", guse.get(idx - 2, ctx.cnt["p"] if idx < 2 else 0))
            E.inc(e.dma_start(out=wbuf[:], in_=win[:, :, part * 1024:(part + 1) * 1024]), "dG", 16)
        ctx.sec("g", fg_l)
        gld[idx] = ctx.cnt["dG"]
        for half in range(2):
            def fp_g(e, E, idx=idx, part=part, half=half, lhsT_src=lhsT_src,
                     wbuf=wbuf, gri=gri):
                E.w("dG", gld[idx])
                E.w("v"); E.w("c")
                off = part * 1024 + half * 512
                for k in range(8):
                    e.matmul(out=pD[0:64, 0:512],
                             lhsT=lhsT_src[:, k, :],
                             rhs=wbuf[:, k, half * 512:(half + 1) * 512],
                             start=(k == 0), stop=False)
                ins = e.matmul(out=pD[0:64, 0:512], lhsT=ones_f[:, 0:64],
                               rhs=brow[:, off:off + 512], start=False, stop=True)
                E.inc(ins, "p")
            ctx.sec("p", fp_g)
            pg = ctx.cnt["p"]
            def fv_g2(e, E, part=part, half=half, dst=dst, pg=pg):
                E.w("p", pg)
                off = part * 1024 + half * 512
                E.inc(e.tensor_copy(out=dst[:, off:off + 512], in_=pD[0:64, 0:512]), "v")
            ctx.sec("v", fv_g2)
        guse[idx] = ctx.cnt["p"]

    def fv_gt1(e, E):
        E.w("v")
        E.inc(e.tensor_tensor(out=scr64[:], in0=gx_s[:, 0:1024], in1=gh_s[:, 0:1024], op=ALU.add), "v")
    ctx.sec("v", fv_gt1)
    def fc_g1(e, E):
        E.w("v")
        E.inc(e.activation(out=scr64[:], in_=scr64[:], func=AF.Sigmoid), "c")
    ctx.sec("c", fc_g1)
    def fv_gt2(e, E):
        E.w("c")
        E.inc(e.tensor_tensor(out=scr64b[:], in0=gh_s[:, 2048:3072], in1=scr64[:], op=ALU.mult), "v")
        E.inc(e.tensor_tensor(out=scr64b[:], in0=scr64b[:], in1=gx_s[:, 2048:3072], op=ALU.add), "v")
    ctx.sec("v", fv_gt2)
    def fc_g2(e, E):
        E.w("v")
        E.inc(e.activation(out=scr64b[:], in_=scr64b[:], func=AF.Tanh), "c")
    ctx.sec("c", fc_g2)
    def fv_gt3(e, E):
        E.w("c")
        E.inc(e.tensor_tensor(out=scr64[:], in0=gx_s[:, 1024:2048], in1=gh_s[:, 1024:2048], op=ALU.add), "v")
    ctx.sec("v", fv_gt3)
    def fc_g3(e, E):
        E.w("v")
        E.inc(e.activation(out=scr64[:], in_=scr64[:], func=AF.Sigmoid), "c")
    ctx.sec("c", fc_g3)
    def fv_gt4(e, E):
        E.w("c")
        E.inc(e.tensor_tensor(out=hnew[:], in0=Sv[:], in1=scr64b[:], op=ALU.subtract), "v")
        E.inc(e.tensor_tensor(out=hnew[:], in0=hnew[:], in1=scr64[:], op=ALU.mult), "v")
        E.inc(e.tensor_tensor(out=hnew[:], in0=hnew[:], in1=scr64b[:], op=ALU.add), "v")
        E.inc(e.tensor_copy(out=svals_bf[:], in_=hnew[:]), "v")
    ctx.sec("v", fv_gt4)

    def fp_hn(e, E):
        E.w("v")
        for j in range(NJ):
            ins = e.transpose(out=pT[:, j * 64:(j + 1) * 64],
                              in_=svals_bf[:, j * 128:(j + 1) * 128],
                              identity=ident_b[0:64, 0:64])
        E.inc(ins, "p")
    ctx.sec("p", fp_hn)
    def fv_hn(e, E):
        E.w("p")
        E.inc(e.tensor_copy(out=hnT_bf[:], in_=pT[:, 0:512].rearrange("p (j s) -> p j s", j=NJ)), "v")
    ctx.sec("v", fv_hn)

    def fg_wv(e, E):
        psn = ctx.cnt["p"]
        E.w("p", psn)
        E.inc(e.dma_start(out=w1p[:], in_=wvTf_in), "dG", 16)
        E.inc(e.dma_start(out=w2p[:], in_=wfoT_in), "dG", 16)
        E.inc(e.dma_start(out=bvrow_s, in_=bv_in), "dG", 16)
        E.inc(e.dma_start(out=borow_s, in_=bo_in), "dG", 16)
    ctx.sec("g", fg_wv)

    for half in range(2):
        def fp_sv(e, E, half=half):
            E.w("dG"); E.w("v"); E.w("c")
            for k in range(8):
                e.matmul(out=pD[0:64, 0:512], lhsT=hnT_bf[:, k, :],
                         rhs=w1p[:, k, half * 512:(half + 1) * 512],
                         start=(k == 0), stop=False)
            ins = e.matmul(out=pD[0:64, 0:512], lhsT=ones_f[:, 0:64],
                           rhs=bvrow_s[:, half * 512:(half + 1) * 512],
                           start=False, stop=True)
            E.inc(ins, "p")
        ctx.sec("p", fp_sv)
        def fv_sv2(e, E, half=half):
            E.w("p")
            E.inc(e.tensor_copy(out=svals[:, half * 512:(half + 1) * 512], in_=pD[0:64, 0:512]), "v")
        ctx.sec("v", fv_sv2)
    def fv_svb(e, E):
        E.w("v")
        E.inc(e.tensor_copy(out=svals_bf[:], in_=svals[:]), "v")
    ctx.sec("v", fv_svb)

    for c4 in range(4):
        for m in range(NJ):
            ps = pA if m % 2 == 0 else pB
            def fp_oe(e, E, c4=c4, m=m, ps=ps):
                E.w("v"); E.w("c")
                E.inc(e.matmul(out=ps[:], lhsT=svals_bf[:, m * 128:(m + 1) * 128],
                               rhs=alpha_sm[:, c4 * 512:(c4 + 1) * 512],
                               start=True, stop=True), "p")
            ctx.sec("p", fp_oe)
            poe = ctx.cnt["p"]
            def fv_oe(e, E, c4=c4, m=m, ps=ps, poe=poe):
                E.w("p", poe)
                E.inc(e.tensor_copy(out=out_bf[:, m, :], in_=ps[:]), "v")
            ctx.sec("v", fv_oe)
        for sub in range(4):
            nt = c4 * 4 + sub
            def fp_fo(e, E, sub=sub, nt=nt):
                E.w("v"); E.w("dG")
                for half in range(2):
                    for k in range(NJ):
                        e.matmul(out=pE[:, half * 512:(half + 1) * 512],
                                 lhsT=out_bf[:, k, sub * 128:(sub + 1) * 128],
                                 rhs=w2p[:, k, half * 512:(half + 1) * 512],
                                 start=(k == 0), stop=False)
                    ins = e.matmul(out=pE[:, half * 512:(half + 1) * 512],
                                   lhsT=ones_f[:], rhs=borow_s[:, half * 512:(half + 1) * 512],
                                   start=False, stop=True)
                E.inc(ins, "p")
            ctx.sec("p", fp_fo)

            def fs_x(e, E, nt=nt):
                E.w("v")
                E.inc(e.dma_start(out=sq_scr[:], in_=sp_at[:, nt, :]), "dS", 16)
                E.inc(e.dma_start(out=sq_scr2[:], in_=x_own.rearrange("(n p) d -> p n d", p=128)[:, nt, :]), "dS", 16)
            ctx.sec("s", fs_x)

            def fv_x1(e, E, nt=nt):
                E.w("p"); E.w("dS")
                E.inc(e.tensor_tensor(out=ACC[:, nt, :], in0=pE[:, 0:1024], in1=sq_scr[:], op=ALU.add), "v")
                E.inc(e.tensor_tensor(out=ACC[:, nt, :], in0=ACC[:, nt, :], in1=sq_scr2[:], op=ALU.add), "v")
            ctx.sec("v", fv_x1)

    for g in reversed(gg):
        g.__exit__(None, None, None)

    if (d6 := dump_req("x1", [128, NT, D])) is not None:
        def fd6(e, E, d6=d6):
            E.w("v")
            E.inc(e.dma_start(out=d6, in_=ACC[:]), "dS", 16)
        ctx.sec("s", fd6)

    if build_upto < 6:
        return finish(ACC[:])

    xq_fm, g_xqf = mk("xq_fm", [128, NJ, T], FP8, side="right")
    q_op, g_qop = mk("q_op", [128, NJ, T], BF16, side="right")
    xbt, g_xbt = mk("xbt", [128, D], BF16, side="right")
    xqt, g_xqt = mk("xqt", [128, D], FP8, side="right")
    xct, g_xct = mk("xct", [128, D], BF16, side="right")

    def f2_red(e, E):
        E.w("v")
        for n in range(NT):
            E.inc(e.reduce_sum(out=st1[:, n:n + 1], in_=ACC[:, n, :], axis=AX), "v")
    ctx.sec("v", f2_red)
    def f2_sq(e, E):
        E.w("v")
        for n in range(NT):
            E.inc(e.activation(out=sq_scr[:], in_=ACC[:, n, :], func=AF.Square,
                               accum_out=st2[:, n:n + 1]), "c")
    ctx.sec("c", f2_sq)
    def f2_mv(e, E):
        E.w("c")
        E.g(e.tensor_scalar(out=stm[:], in0=st1[:], scalar1=1.0 / 1024.0,
                              scalar2=None, op0=ALU.mult), "v")
        E.g(e.tensor_scalar(out=strv[:], in0=st2[:], scalar1=1.0 / 1024.0,
                              scalar2=None, op0=ALU.mult), "v")
        E.g(e.tensor_tensor(out=scr16[:], in0=stm[:], in1=stm[:], op=ALU.mult), "v")
        E.g(e.tensor_tensor(out=strv[:], in0=strv[:], in1=scr16[:], op=ALU.subtract), "v")
        E.g(e.tensor_scalar(out=strv[:], in0=strv[:], scalar1=1e-5,
                              scalar2=None, op0=ALU.add), "v")
    ctx.sec("v", f2_mv)
    def f2_sqrt(e, E):
        E.w("v")
        E.inc(e.activation(out=scr16[:], in_=strv[:], func=AF.Sqrt), "c")
    ctx.sec("c", f2_sqrt)
    def f2_fin(e, E):
        E.w("c")
        newton_rsqrt(e, E, strv[:], st2[:], scr16[:])
        E.g(e.tensor_copy(out=strv[:], in_=st2[:]), "v")
        E.g(e.tensor_tensor(out=stmr[:], in0=stm[:], in1=strv[:], op=ALU.mult), "v")
    ctx.sec("v", f2_fin)

    ap_end = {}
    t1_end = {}
    t2_end = {}
    v1_end = {}
    v2_end = {}
    for n in range(NT):
        def f2_ap(e, E, n=n):
            if n == 0:
                E.w("v"); E.w("p")
            else:
                E.w("p", t2_end[n - 1])
            E.inc(e.tensor_scalar(out=sq_scr[:], in0=ACC[:, n, :],
                                  scalar1=strv[:, n:n + 1], scalar2=stmr[:, n:n + 1],
                                  op0=ALU.mult, op1=ALU.subtract), "v")
            E.inc(e.tensor_copy(out=xbt[:], in_=sq_scr[:]), "v")
            E.g(e.tensor_copy(out=xqt[:], in_=sq_scr[:]), "v")
            E.inc(e.tensor_copy(out=xct[:], in_=xqt[:]), "v")
        ctx.sec("v", f2_ap)
        ap_end[n] = ctx.cnt["v"]
        def f2_t1(e, E, n=n):
            E.w("v", ap_end[n])
            if n >= 1:
                E.w("v", v1_end[n - 1])
            for j in range(NJ):
                ins = e.transpose(out=pT[:, j * 128:(j + 1) * 128],
                                  in_=xbt[:, j * 128:(j + 1) * 128],
                                  identity=ident_b[:])
            E.inc(ins, "p")
        ctx.sec("p", f2_t1)
        t1_end[n] = ctx.cnt["p"]
        def f2_t2(e, E, n=n):
            E.w("v", ap_end[n])
            if n >= 1:
                E.w("v", v2_end[n - 1])
            for j in range(NJ):
                ins = e.transpose(out=pT2[:, j * 128:(j + 1) * 128],
                                  in_=xct[:, j * 128:(j + 1) * 128],
                                  identity=ident_b[:])
            E.inc(ins, "p")
        ctx.sec("p", f2_t2)
        t2_end[n] = ctx.cnt["p"]
        def f2_v1(e, E, n=n):
            E.w("p", t1_end[n])
            E.inc(e.tensor_copy(out=q_op[:, :, n * 128:(n + 1) * 128],
                                in_=pT[:].rearrange("p (j t) -> p j t", j=NJ)), "v")
        ctx.sec("v", f2_v1)
        v1_end[n] = ctx.cnt["v"]
        def f2_v2(e, E, n=n):
            E.w("p", t2_end[n])
            E.inc(e.tensor_copy(out=xq_fm[:, :, n * 128:(n + 1) * 128],
                                in_=pT2[:].rearrange("p (j t) -> p j t", j=NJ)), "v")
        ctx.sec("v", f2_v2)
        v2_end[n] = ctx.cnt["v"]

    def fs_spx(e, E):
        E.w("v")
        E.inc(e.dma_start(out=sp_x1[:], in_=ACC[:]), "dS", 16)
        if (dq := dump_req("qop", [128, NJ, T], BF16)) is not None:
            E.inc(e.dma_start(out=dq, in_=q_op[:]), "dS", 16)
        if (dx8 := dump_req("xq", [128, NJ, T], FP8)) is not None:
            E.inc(e.dma_start(out=dx8, in_=xq_fm[:]), "dS", 16)
    ctx.sec("s", fs_spx)

    if build_upto < 7:
        return finish(ACC[:])

    g_xct.__exit__(None, None, None)
    g_xqt.__exit__(None, None, None)
    g_xbt.__exit__(None, None, None)
    g_ACC.__exit__(None, None, None)
    for g in (g_si, g_asm, g_atm, g_ssc, g_SvT, g_Sv):
        g.__exit__(None, None, None)

    TSG, NSG = 512, 4
    gb1, g_gb1 = mk("sb_gb1", [128, NJ, TSG], F32, side="right")
    gb2, g_gb2 = mk("sb_gb2", [128, NJ, TSG], F32, side="right")
    gb3, g_gb3 = mk("sb_gb3", [128, NJ, TSG], F32, side="right")
    gb4, g_gb4 = mk("sb_gb4", [128, NJ, TSG], F32, side="right")
    dcf, g_dcf = mk("sb_dcf", [128, NJ, TSG], F32, side="right")
    w3p, g_w3p = mk("sb_w3p", [128, 8, D], BF16, side="right")
    GFL = lambda b: b[:].rearrange("p j t -> p (j t)")
    dsc, car2, car3, d512 = ccl, ccprev, hcar, nccst

    def fc_d(e, E):
        E.w("dS")
        E.inc(e.activation(out=dp_s[:, 1:2], in_=dp_s[:, 0:1], func=AF.Sigmoid), "c")
    ctx.sec("c", fc_d)
    def fv_d(e, E):
        E.w("c")
        E.g(e.tensor_scalar(out=dp_s[:, 2:3], in0=dp_s[:, 1:2], scalar1=1e-8,
                              scalar2=None, op0=ALU.max), "v")
    ctx.sec("v", fv_d)
    def fc_d2(e, E):
        E.w("v")
        E.inc(e.activation(out=dp_s[:, 2:3], in_=dp_s[:, 2:3], func=AF.Ln), "c")
    ctx.sec("c", fc_d2)
    def fp_d(e, E):
        E.w("c")
        for j in range(NJ):
            ins = e.matmul(out=pD[:, j:j + 1], lhsT=GTt[:, j * 128:(j + 1) * 128],
                           rhs=dp_s[:, 2:3], start=True, stop=True)
        E.inc(ins, "p")
    ctx.sec("p", fp_d)
    def fv_d2(e, E):
        E.w("p")
        E.g(e.tensor_copy(out=logd[:], in_=pD[:, 0:8]), "v")
    ctx.sec("v", fv_d2)
    def fc_d3(e, E):
        E.w("v")
        E.inc(e.activation(out=stage16[:, 0:8], in_=logd[:], func=AF.Exp), "c")
    ctx.sec("c", fc_d3)
    def fv_d3(e, E):
        E.w("c")
        for j in range(NJ):
            ins = e.tensor_scalar(out=dcf[:, j, :], in0=ones_ch[:, 0:TSG],
                                  scalar1=stage16[:, j:j + 1], scalar2=None, op0=ALU.mult)
        E.g(ins, "v")
        for j in range(NJ):
            ins = e.tensor_tensor_scan(out=gb1[:, j, :], data0=dcf[:, j, :],
                                       data1=ones_ch[:, 0:TSG], initial=consts[:, 13:14],
                                       op0=ALU.mult, op1=ALU.mult)
        E.g(ins, "v")
        E.g(e.tensor_copy(out=d512[:], in_=gb1[:, :, TSG - 1]), "v")
        E.g(e.memset(dsc[:], 1.0), "v")
        E.g(e.memset(kvcar[:], 0.0), "v")
        E.g(e.memset(kscar[:], 0.0), "v")
    ctx.sec("v", fv_d3)
    vpw = ctx.cnt["v"]
    def fs_pw(e, E, vpw=vpw):
        E.w("v", vpw)
        E.inc(e.dma_start(out=sp_cloc[:, :, 0:TSG], in_=gb1[:]), "dS", 16)
    ctx.sec("s", fs_pw)

    psn_g = ctx.cnt["p"]
    def fg_q(e, E, psn_g=psn_g):
        E.w("p", psn_g)
        E.inc(e.dma_start(out=w1p[:], in_=wqT_in), "dG", 16)
        E.inc(e.dma_start(out=w2p[:], in_=wkT_in), "dG", 16)
        E.inc(e.dma_start(out=w3p[:], in_=wvT1_in), "dG", 16)
    ctx.sec("g", fg_q)
    dg_q = ctx.cnt["dG"]

    gmm_end = {}
    gspl_end = {}
    for s in range(NSG):
        c0 = s * TSG
        evq = {}
        for gate, wp in ((0, w1p), (1, w2p), (2, w3p)):
            for m in range(NJ):
                ps = (pA, pB, pC, pD, pE, pE)[gate * 2 + (m % 2)]
                pslice = ps[:, 0:TSG] if gate < 2 else (
                    pE[:, 0:TSG] if m % 2 == 0 else pE[:, TSG:2 * TSG])
                vwait = evq.get((gate, m - 2), gmm_end.get((s - 1, 'v'), 0))
                def fp_g1(e, E, s=s, m=m, pslice=pslice, wp=wp, c0=c0, vwait=vwait):
                    E.w("dG", dg_q)
                    E.w("v", vwait)
                    for k in range(NJ):
                        ins = e.matmul(out=pslice, lhsT=wp[:, k, m * 128:(m + 1) * 128],
                                       rhs=q_op[:, k, c0:c0 + TSG], start=(k == 0), stop=(k == NJ - 1))
                    E.inc(ins, "p")
                ctx.sec("p", fp_g1)
                pm = ctx.cnt["p"]
                def fv_gev(e, E, s=s, m=m, pslice=pslice, pm=pm, gate=gate):
                    E.w("p", pm)
                    if gate == 0 and m == 0:
                        if s >= 1:
                            E.w("dG", gspl_end[s - 1])
                        else:
                            E.w("dS")
                    if gate == 2:
                        E.inc(e.tensor_copy(out=gb3[:, m, :], in_=pslice), "v")
                    else:
                        dst = gb1 if gate == 0 else gb2
                        E.inc(e.tensor_scalar(out=dst[:, m, :], in0=pslice, scalar1=0.0,
                                              scalar2=None, op0=ALU.max), "v")
                        E.inc(e.tensor_scalar(out=gb4[:, m, :], in0=pslice, scalar1=0.0,
                                              scalar2=None, op0=ALU.min), "v")
                ctx.sec("v", fv_gev)
                evq[(gate, m)] = ctx.cnt["v"]
            if gate < 2:
                vev = ctx.cnt["v"]
                def fc_phi(e, E, vev=vev):
                    E.w("v", vev)
                    E.inc(e.activation(out=GFL(gb4), in_=GFL(gb4), func=AF.Exp), "c")
                ctx.sec("c", fc_phi)
                cph = ctx.cnt["c"]
                def fv_phi(e, E, gate=gate, cph=cph):
                    E.w("c", cph)
                    dst = gb1 if gate == 0 else gb2
                    E.inc(e.tensor_tensor(out=GFL(dst), in0=GFL(dst), in1=GFL(gb4), op=ALU.add), "v")
                ctx.sec("v", fv_phi)
        gmm_end[(s, 'v')] = ctx.cnt["v"]

        def fv_gs(e, E, s=s, c0=c0):
            E.w("v")
            for j in range(NJ):
                E.inc(e.tensor_copy(out=q_op[:, j, c0:c0 + TSG], in_=gb1[:, j, :]), "v")
            E.w("v")
            E.inc(e.tensor_tensor(out=GFL(gb4), in0=GFL(gb2), in1=GFL(gb3), op=ALU.mult), "v")
            E.w("v")
            for j in range(NJ):
                ins = e.tensor_tensor_scan(out=gb1[:, j, :], data0=dcf[:, j, :],
                                           data1=gb4[:, j, :], initial=kvcar[:, j:j + 1],
                                           op0=ALU.mult, op1=ALU.add)
            E.g(ins, "v")
            E.g(e.tensor_copy(out=kvcar[:], in_=gb1[:, :, TSG - 1]), "v")
            for j in range(NJ):
                ins = e.tensor_tensor_scan(out=gb4[:, j, :], data0=dcf[:, j, :],
                                           data1=gb2[:, j, :], initial=kscar[:, j:j + 1],
                                           op0=ALU.mult, op1=ALU.add)
            E.g(ins, "v")
            E.g(e.tensor_copy(out=kscar[:], in_=gb4[:, :, TSG - 1]), "v")
        ctx.sec("v", fv_gs)
        vsp = ctx.cnt["v"]
        def fg_sp2(e, E, s=s, c0=c0, vsp=vsp):
            E.w("v", vsp)
            E.inc(e.dma_start(out=sp_v[:, :, c0:c0 + TSG], in_=gb3[:]), "dG", 16)
            E.inc(e.dma_start(out=sp_kv[:, :, c0:c0 + TSG], in_=gb1[:]), "dG", 16)
            E.inc(e.dma_start(out=sp_ks[:, :, c0:c0 + TSG], in_=gb4[:]), "dG", 16)
        ctx.sec("g", fg_sp2)
        gspl_end[s] = ctx.cnt["dG"]

    def fv_s3(e, E):
        E.w("v")
        E.g(e.tensor_scalar(out=stage16[:, 0:8], in0=kvcar[:], scalar1=flagS[:, 0:1],
                              scalar2=None, op0=ALU.mult), "v")
        E.g(e.tensor_scalar(out=stage16[:, 8:16], in0=kscar[:], scalar1=flagS[:, 0:1],
                              scalar2=None, op0=ALU.mult), "v")
    ctx.sec("v", fv_s3)
    collective(stage16[:], recv16[:], ccl1i, ccl1o)
    def fv_r3(e, E):
        E.w("dG")
        E.g(e.tensor_scalar(out=kvcar[:], in0=recv16[:, 0:8], scalar1=flagR[:, 0:1],
                              scalar2=None, op0=ALU.mult), "v")
        E.g(e.tensor_scalar(out=kscar[:], in0=recv16[:, 8:16], scalar1=flagR[:, 0:1],
                              scalar2=None, op0=ALU.mult), "v")
    ctx.sec("v", fv_r3)

    psn_g2 = ctx.cnt["p"]
    def fg_g(e, E, psn_g2=psn_g2):
        E.w("p", psn_g2)
        E.inc(e.dma_start(out=w1p[:], in_=wgT_in), "dG", 16)
        E.inc(e.dma_start(out=w2p[:], in_=woutT_in), "dG", 16)
    ctx.sec("g", fg_g)
    dg_g = ctx.cnt["dG"]

    gld_end = {}
    gpo_end = {}
    for s in range(NSG):
        c0 = s * TSG
        def fg_l3(e, E, s=s, c0=c0):
            if s == 0:
                E.w("v")
                E.w("dG")
            else:
                E.w("v", gpo_end.get((s - 1, 'kv'), 0))
            E.inc(e.dma_start(out=gb2[:], in_=sp_kv[:, :, c0:c0 + TSG]), "dG", 16)
            E.inc(e.dma_start(out=gb3[:], in_=sp_ks[:, :, c0:c0 + TSG]), "dG", 16)
        ctx.sec("g", fg_l3)
        gld_end[s] = ctx.cnt["dG"]

        def fs_pwl(e, E, s=s):
            if s == 0:
                E.w("dG")
                E.w("v")
            else:
                E.w("v", gpo_end.get((s - 1, 'pw'), 0))
            E.inc(e.dma_start(out=gb1[:], in_=sp_cloc[:, :, 0:TSG]), "dS", 16)
        ctx.sec("s", fs_pwl)
        pwl = ctx.cnt["dS"]

        def fv_gc(e, E, s=s, pwl=pwl):
            E.w("dG", gld_end[s])
            E.w("dS", pwl)
            E.w("c")
            E.g(e.tensor_tensor(out=car2[:], in0=dsc[:], in1=kvcar[:], op=ALU.mult), "v")
            E.g(e.tensor_tensor(out=car3[:], in0=dsc[:], in1=kscar[:], op=ALU.mult), "v")
            for j in range(NJ):
                e.scalar_tensor_tensor(out=gb2[:, j, :], in0=gb1[:, j, :],
                                       scalar=car2[:, j:j + 1], in1=gb2[:, j, :],
                                       op0=ALU.mult, op1=ALU.add)
                ins = e.scalar_tensor_tensor(out=gb3[:, j, :], in0=gb1[:, j, :],
                                             scalar=car3[:, j:j + 1], in1=gb3[:, j, :],
                                             op0=ALU.mult, op1=ALU.add)
            E.inc(ins, "v")
            E.g(e.tensor_tensor(out=dsc[:], in0=dsc[:], in1=d512[:], op=ALU.mult), "v")
            for j in range(NJ):
                ins = e.tensor_tensor(out=gb4[:, j, :], in0=q_op[:, j, s * TSG:(s + 1) * TSG],
                                      in1=gb3[:, j, :], op=ALU.mult)
            E.inc(ins, "v")
        ctx.sec("v", fv_gc)
        vgc = ctx.cnt["v"]

        def fp_den(e, E, vgc=vgc):
            E.w("v", vgc)
            for j in range(NJ):
                ins = e.matmul(out=pC[0:16, 0:TSG], lhsT=Gt[:, j * 16:(j + 1) * 16],
                               rhs=gb4[:, j, :], start=(j == 0), stop=(j == NJ - 1))
            E.inc(ins, "p")
        ctx.sec("p", fp_den)
        pden = ctx.cnt["p"]

        def fv_den(e, E, pden=pden):
            E.w("p", pden)
            E.g(e.tensor_scalar(out=sq_scr[0:16, 0:TSG], in0=pC[0:16, 0:TSG],
                                  scalar1=1e-6, scalar2=None, op0=ALU.max), "v")
            E.g(e.reciprocal(out=sq_scr2[0:16, 0:TSG], in_=sq_scr[0:16, 0:TSG]), "v")
        ctx.sec("v", fv_den)
        vden = ctx.cnt["v"]

        for j in range(NJ):
            def fp_rb(e, E, j=j, vden=vden):
                E.w("v", vden)
                ps = pA if j % 2 == 0 else pB
                E.inc(e.matmul(out=ps[:, 0:TSG], lhsT=GTt[:, j * 128:(j + 1) * 128],
                               rhs=sq_scr2[0:16, 0:TSG], start=True, stop=True), "p")
            ctx.sec("p", fp_rb)
            prb = ctx.cnt["p"]
            def fv_rb(e, E, s=s, j=j, prb=prb):
                E.w("p", prb)
                ps = pA if j % 2 == 0 else pB
                e.tensor_tensor(out=gb4[:, j, :], in0=q_op[:, j, s * TSG:(s + 1) * TSG],
                                in1=gb2[:, j, :], op=ALU.mult)
                E.inc(e.tensor_tensor(out=gb4[:, j, :], in0=gb4[:, j, :],
                                      in1=ps[:, 0:TSG], op=ALU.mult), "v")
            ctx.sec("v", fv_rb)

        def fv_ocast(e, E, s=s, c0=c0):
            E.w("v")
            for j in range(NJ):
                E.inc(e.tensor_copy(out=q_op[:, j, c0:c0 + TSG], in_=gb4[:, j, :]), "v")
        ctx.sec("v", fv_ocast)
        voc = ctx.cnt["v"]

        evg = {}
        for m in range(NJ):
            ps = pA if m % 2 == 0 else pB
            vwait = evg.get(m - 2, voc)
            def fp_g2(e, E, s=s, m=m, ps=ps, c0=c0, voc=voc, vwait=vwait):
                E.w("v", voc)
                E.w("v", vwait)
                E.w("dG", dg_g)
                for k in range(NJ):
                    ins = e.matmul(out=ps[:, 0:TSG], lhsT=w1p[:, k, m * 128:(m + 1) * 128],
                                   rhs=q_op[:, k, c0:c0 + TSG], start=(k == 0), stop=(k == NJ - 1))
                E.inc(ins, "p")
            ctx.sec("p", fp_g2)
            pg2 = ctx.cnt["p"]
            def fv_gg(e, E, m=m, ps=ps, pg2=pg2):
                E.w("p", pg2)
                E.inc(e.tensor_scalar(out=gb1[:, m, :], in0=ps[:, 0:TSG],
                                      scalar1=bg_pp[:, m:m + 1], scalar2=None, op0=ALU.add), "v")
            ctx.sec("v", fv_gg)
            evg[m] = ctx.cnt["v"]
        vgg = ctx.cnt["v"]
        def fc_gs(e, E, vgg=vgg):
            E.w("v", vgg)
            E.inc(e.activation(out=GFL(gb1), in_=GFL(gb1), func=AF.Sigmoid), "c")
        ctx.sec("c", fc_gs)
        cgs = ctx.cnt["c"]

        def fg_lv(e, E, s=s, c0=c0, vgc=vgc):
            E.w("v", vgc)
            E.inc(e.dma_start(out=gb3[:], in_=sp_v[:, :, c0:c0 + TSG]), "dG", 16)
        ctx.sec("g", fg_lv)
        glv = ctx.cnt["dG"]

        def fv_mx(e, E, s=s, c0=c0, cgs=cgs, glv=glv):
            E.w("c", cgs)
            E.w("dG", glv)
            E.inc(e.tensor_tensor(out=GFL(gb4), in0=GFL(gb4), in1=GFL(gb3), op=ALU.subtract), "v")
            E.inc(e.tensor_tensor(out=GFL(gb4), in0=GFL(gb4), in1=GFL(gb1), op=ALU.mult), "v")
            E.inc(e.tensor_tensor(out=GFL(gb4), in0=GFL(gb4), in1=GFL(gb3), op=ALU.add), "v")
            E.w("v")
            for j in range(NJ):
                E.inc(e.tensor_copy(out=q_op[:, j, c0:c0 + TSG], in_=gb4[:, j, :]), "v")
        ctx.sec("v", fv_mx)
        gpo_end[(s, 'kv')] = ctx.cnt["v"]
        gpo_end[(s, 'pw')] = ctx.cnt["v"]

        for sub in range(TSG // 128):
            nt = (c0 + sub * 128) // 128
            def fp_ao(e, E, s=s, sub=sub, c0=c0):
                E.w("v"); E.w("dG", dg_g); E.w("dS")
                for half in range(2):
                    for k in range(NJ):
                        ins = e.matmul(out=pE[:, half * 512:(half + 1) * 512],
                                       lhsT=q_op[:, k, c0 + sub * 128:c0 + (sub + 1) * 128],
                                       rhs=w2p[:, k, half * 512:(half + 1) * 512],
                                       start=(k == 0), stop=(k == NJ - 1))
                E.inc(ins, "p")
            ctx.sec("p", fp_ao)
            pao = ctx.cnt["p"]
            def fv_ao(e, E, pao=pao):
                E.w("p", pao)
                E.inc(e.tensor_copy(out=sq_scr[:], in_=pE[:, 0:1024]), "v")
            ctx.sec("v", fv_ao)
            cao = ctx.cnt["v"]
            def fs_ao(e, E, nt=nt, cao=cao):
                E.w("v", cao)
                E.inc(e.dma_start(out=sp_at[:, nt, :], in_=sq_scr[:]), "dS", 16)
                if (da1 := dump_req("at1", [128, NT, D])) is not None:
                    E.inc(e.dma_start(out=da1[:, nt, :], in_=sq_scr[:]), "dS", 16)
            ctx.sec("s", fs_ao)
            def fc_ao2(e, E):
                E.w("dS")
            ctx.sec("c", fc_ao2)

    for g in (g_w3p, g_dcf, g_gb4, g_gb3, g_gb2, g_gb1):
        g.__exit__(None, None, None)

    if build_upto < 8:
        return finish(None)

    g_qop.__exit__(None, None, None)
    g_w2p.__exit__(None, None, None)
    g_w1p.__exit__(None, None, None)
    W1q, g_W1q = mk("W1q", [128, 8, 4096], FP8)
    W2q, g_W2q = mk("W2q", [128, 32, D], FP8)
    ACC2, g_A2 = mk("ACC2", [128, NT, D])
    b2q_rep, g_b2q = mk("b2q_rep", [128, D])
    b2q_row, g_b2r = mk("b2q_row", [1, D])
    b2q8, g_b28 = mk("b2q8", [1, D], FP8)
    h8F, g_h8 = mk("h8F", [128, 32, TSF], FP8, side="right")
    hstage, g_hst = mk("hstage", [128, TSF], BF16, side="right")
    hstage2, g_hst2 = mk("hstage2", [128, TSF], BF16, side="right")
    mask_st, g_msk = mk("mask_st", [128, 4, D], BF16, side="right")
    sq_bf, g_sqb = mk("sb_sqbf", [128, D], BF16, side="right")

    def fg_f(e, E):
        psn = ctx.cnt["p"]
        E.w("p", psn)
        E.inc(e.dma_start(out=W1q[:], in_=W1fT_in), "dG", 16)
        E.inc(e.dma_start(out=W2q[:], in_=W2fT_in), "dG", 16)
    ctx.sec("g", fg_f)

    def fs_f(e, E):
        E.inc(e.dma_start(out=b2q_row[:], in_=b2f_in), "dS", 16)
    ctx.sec("s", fs_f)

    def fv_bq(e, E):
        E.w("dS")
        E.g(e.tensor_copy(out=b1q8[:], in_=b1f_pp[:]), "v")
        E.g(e.tensor_copy(out=b1fq_pp[:], in_=b1q8[:]), "v")
        E.g(e.tensor_copy(out=b2q8[:], in_=b2q_row[:]), "v")
        E.g(e.tensor_copy(out=b2q_row[:], in_=b2q8[:]), "v")
    ctx.sec("v", fv_bq)
    repl_row(b2q_row, D, b2q_rep)

    dgh = ctx.cnt["dG"]
    vq0 = ctx.cnt["v"]
    ph_end = {}
    rel = {}
    cast = {}
    for cf in range(NCF):
        c0 = cf * TSF
        for m in range(32):
            ps = pA if m % 2 == 0 else pB
            hst = hstage if m % 2 == 0 else hstage2
            def fp_h1(e, E, m=m, ps=ps, c0=c0, cf=cf):
                E.w("dG", dgh)
                E.w("v", vq0)
                E.w("v", rel.get((cf, m - 2), rel.get((cf - 1, m + 30), 0)))
                for k in range(NJ):
                    ins = e.matmul(out=ps[:, 0:TSF], lhsT=W1q[:, k, m * 128:(m + 1) * 128],
                                   rhs=xq_fm[:, k, c0:c0 + TSF], start=(k == 0), stop=(k == NJ - 1))
                E.inc(ins, "p")
            ctx.sec("p", fp_h1)
            pm = ctx.cnt["p"]
            def fv_h1b(e, E, m=m, ps=ps, pm=pm, hst=hst, cf=cf):
                E.w("p", pm)
                E.inc(e.tensor_scalar(out=hst[:], in0=ps[:, 0:TSF],
                                      scalar1=b1fq_pp[:, m:m + 1], scalar2=0.0,
                                      op0=ALU.add, op1=ALU.max), "v")
            ctx.sec("v", fv_h1b)
            rel[(cf, m)] = ctx.cnt["v"]
            def fv_h8(e, E, m=m, cf=cf, hst=hst):
                if m == 0 and cf >= 1:
                    E.w("p", ph_end[cf - 1])
                E.inc(e.tensor_copy(out=h8F[:, m, :], in_=hst[:]), "v")
            ctx.sec("v", fv_h8)
            cast[(cf, m)] = ctx.cnt["v"]
        def fs_m(e, E, cf=cf):
            E.w("v")
            E.inc(e.dma_start(out=mask_st[:],
                              in_=mask_own.rearrange("(n p) d -> p n d", p=128)[:, cf * 4:(cf + 1) * 4, :]), "dS", 16)
        ctx.sec("s", fs_m)
        vh_end = ctx.cnt["v"]
        for sub in range(TSF // 128):
            nt = (c0 + sub * 128) // 128
            def fp_y(e, E, sub=sub, c0=c0, vh_end=vh_end):
                E.w("v")
                for half in range(2):
                    for k in range(32):
                        ins = e.matmul(out=pE[:, half * 512:(half + 1) * 512],
                                       lhsT=h8F[:, k, sub * 128:(sub + 1) * 128],
                                       rhs=W2q[:, k, half * 512:(half + 1) * 512],
                                       start=(k == 0), stop=(k == 31))
                E.inc(ins, "p")
            ctx.sec("p", fp_y)
            py = ctx.cnt["p"]
            def fv_y(e, E, sub=sub, nt=nt, py=py):
                E.w("p", py); E.w("dS")
                E.g(e.tensor_tensor(out=sq_bf[:], in0=pE[:, 0:1024],
                                    in1=b2q_rep[:], op=ALU.add), "v")
                E.inc(e.tensor_tensor(out=ACC2[:, nt, :], in0=sq_bf[:],
                                      in1=mask_st[:, sub, :], op=ALU.mult), "v")
            ctx.sec("v", fv_y)
        ph_end[cf] = ctx.cnt["p"]

    for g in (g_sqb, g_msk, g_hst2, g_hst, g_h8):
        g.__exit__(None, None, None)

    fb1, g_fb1 = mk("sb_fb1", [128, D], F32, side="right")
    fb2, g_fb2 = mk("sb_fb2", [128, D], F32, side="right")
    fadd_end = {}
    fld_end = {}
    for nt in range(NT):
        pa, pb_ = (sq_scr, sq_scr2) if nt % 2 == 0 else (fb1, fb2)
        def fs_f2(e, E, nt=nt, pa=pa, pb_=pb_):
            if nt < 2:
                E.w("v")
            else:
                E.w("v", fadd_end[nt - 2])
            E.inc(e.dma_start(out=pa[:], in_=sp_at[:, nt, :]), "dS", 16)
            E.inc(e.dma_start(out=pb_[:], in_=sp_x1[:, nt, :]), "dS", 16)
        ctx.sec("s", fs_f2)
        fld_end[nt] = ctx.cnt["dS"]
        def fv_f(e, E, nt=nt, pa=pa, pb_=pb_):
            E.w("dS", fld_end[nt])
            E.g(e.tensor_tensor(out=ACC2[:, nt, :], in0=ACC2[:, nt, :], in1=pa[:], op=ALU.add), "v")
            E.inc(e.tensor_tensor(out=ACC2[:, nt, :], in0=ACC2[:, nt, :], in1=pb_[:], op=ALU.add), "v")
        ctx.sec("v", fv_f)
        fadd_end[nt] = ctx.cnt["v"]
        def fc_y(e, E, nt=nt):
            E.w("v", fadd_end[nt])
            E.inc(e.dma_start(out=y_out.rearrange("(n p) d -> p n d", p=128)[:, nt, :],
                              in_=ACC2[:, nt, :]), "dS", 16)
        ctx.sec("c", fc_y)

    return finish(None)


def _host_inputs(inputs):
    import jax
    x = np.asarray(inputs["L0_attn_x"], np.float32)
    with jax.default_device(jax.local_devices(backend="cpu")[0]):
        keep = np.asarray(jax.random.bernoulli(jax.random.key(42, impl="threefry2x32"),
                                               0.8, x.shape))
    maskf = (keep.astype(np.float32) * np.float32(1.25)).astype(BF)

    common = {}
    common["consts_in"] = np.tile(np.asarray(CONSTS, np.float32)[None, :], (128, 1))
    common["ident_b_i"] = np.eye(128, dtype=BF)
    common["ones_f_i"] = np.ones((1, 128), np.float32)
    common["ones_colb_i"] = np.ones((128, 1), BF)
    common["iota_t_i"] = np.tile(np.arange(1, TS1 + 1, dtype=np.float32)[None, :], (128, 1))
    G = np.zeros((128, 8, 16), np.float32)
    GT = np.zeros((16, 8, 128), np.float32)
    for j in range(8):
        for p in range(128):
            hh = (j * 128 + p) // 64
            G[p, j, hh] = 1.0
            GT[hh, j, p] = 1.0
    common["G_in"] = G.reshape(128, 128)
    common["GT_in"] = np.ascontiguousarray(GT.reshape(16, 1024))

    for l in range(3):
        common[f"wrT{l}"] = _wT(inputs["L0_attn_wr_w"][l])
        common[f"wiT{l}"] = _wT(inputs["L0_attn_wi_w"][l])
        common[f"woT{l}"] = _wT(inputs["L0_attn_wo_w"][l])
        cwl = np.asarray(inputs["L0_attn_conv_w"][l], np.float32)
        common[f"cw{l}"] = np.ascontiguousarray(
            cwl.reshape(8, 128, 4).transpose(1, 0, 2).reshape(128, 32))
        common[f"cb{l}"] = _pp(inputs["L0_attn_conv_b"][l])
        common[f"wrb{l}"] = _pp(inputs["L0_attn_wr_b"][l])
        common[f"wib{l}"] = _pp(inputs["L0_attn_wi_b"][l])
        common[f"wrbr{l}"] = np.asarray(inputs["L0_attn_wr_b"][l], np.float32)[None, :]
        common[f"wibr{l}"] = np.asarray(inputs["L0_attn_wi_b"][l], np.float32)[None, :]
        common[f"la{l}"] = _pp(inputs["L0_attn_log_a"][l])
    for l in range(2):
        common[f"nwpp{l}"] = _pp(inputs["L0_attn_norm_w"][l])
    common["nw2row"] = np.asarray(inputs["L0_attn_norm_w"][2], np.float32)[None, :]
    common["nowrow"] = np.asarray(inputs["L0_attn_norm_out_w"], np.float32)[None, :]

    w1 = np.asarray(inputs["L0_ffn_w1"], np.float32)
    common["w1aT"] = _wT(w1[:, :D])
    common["w1bT"] = _wT(w1[:, D:])
    common["b1r"] = _pp(inputs["L0_ffn_b1"])
    common["w2T"] = _wT(inputs["L0_ffn_w2"])
    common["b2r"] = np.asarray(inputs["L0_ffn_b2"], np.float32)[None, :]
    common["slot_init"] = np.asarray(inputs["L0_ffn_slot_init"], np.float32)
    common["slot_scale"] = np.asarray(inputs["L0_ffn_slot_scale"], np.float32)
    common["wihT"] = _wT(inputs["L0_ffn_gru_wih"])
    common["whhT"] = _wT(inputs["L0_ffn_gru_whh"])
    common["bih"] = np.asarray(inputs["L0_ffn_gru_bih"], np.float32)[None, :]
    common["bhh"] = np.asarray(inputs["L0_ffn_gru_bhh"], np.float32)[None, :]
    common["wvTf"] = _wT(inputs["L0_ffn_wv"])
    common["bv"] = np.asarray(inputs["L0_ffn_bv"], np.float32)[None, :]
    common["wfoT"] = _wT(inputs["L0_ffn_wo"])
    common["bo"] = np.asarray(inputs["L0_ffn_bo"], np.float32)[None, :]
    common["tau"] = np.asarray(inputs["L0_ffn_tau"], np.float32).reshape(1, 1)

    wqkv = np.asarray(inputs["L1_attn_Wqkv"], np.float32)
    common["wqT"] = _wT(wqkv[0:D])
    common["wkT"] = _wT(wqkv[D:2 * D])
    common["wvT1"] = _wT(wqkv[2 * D:3 * D])
    common["wgT"] = _wT(inputs["L1_attn_Wgate"])
    common["bg"] = _pp(inputs["L1_attn_bgate"])
    common["woutT"] = _wT(inputs["L1_attn_Wout"])
    common["dp"] = np.asarray(inputs["L1_attn_decay_param"], np.float32).reshape(16, 1)

    common["W1fT"] = _wT(inputs["L1_ffn_W1"])
    common["b1f"] = _pp(inputs["L1_ffn_b1"])
    common["W2fT"] = _wT(inputs["L1_ffn_W2"])
    common["b2f"] = np.asarray(inputs["L1_ffn_b2"], np.float32)[None, :]

    in_maps = []
    for c in range(8):
        b, h = c // 2, c % 2
        m = dict(common)
        t0 = h * T
        m["x_own"] = np.ascontiguousarray(x[b, t0:t0 + T])
        m["mask_own"] = np.ascontiguousarray(maskf[b, t0:t0 + T])
        m["xprev3"] = (np.zeros((3, D), np.float32) if h == 0
                       else np.ascontiguousarray(x[b, t0 - 3:t0]))
        m["flag_send"] = np.full((128, 1), 1.0 if h == 0 else 0.0, np.float32)
        m["flag_recv"] = np.full((128, 1), 1.0 if h == 1 else 0.0, np.float32)
        in_maps.append(m)
    return in_maps


def kernel(**inputs):
    nc, dbg = build_nc()
    in_maps = _host_inputs(inputs)
    res = run_bass_kernel_spmd(nc, in_maps, core_ids=list(range(8)))
    out = np.empty((4, 4096, D), np.float32)
    for c in range(8):
        b, h = c // 2, c % 2
        out[b, h * T:(h + 1) * T] = res.results[c]["y"]
    return out
